# version 01
# baseline (speedup 1.0000x reference)
import math
import contextlib
import numpy as np
import ml_dtypes
import concourse.bass as bass
import concourse.mybir as mybir
from concourse.bass_utils import run_bass_kernel_spmd

BF16N = ml_dtypes.bfloat16
F32 = mybir.dt.float32
F32R = mybir.dt.float32r
BF16 = mybir.dt.bfloat16
AF = mybir.ActivationFunctionType
ALU = mybir.AluOpType
AX = mybir.AxisListType

B, S, D = 4, 2048, 2048
NH, HD = 16, 128
TOK = 1024
NT = TOK // 128
NDC = D // 128
NEG = -1e9
QS0 = 1.0 / math.sqrt(HD)
QS1 = 1.0 / math.sqrt(HD + 64)
FF, NE = 512, 8
ITER = 5632
NIC = ITER // 128


class P:
    def __init__(self, nc, sems, which, engines):
        self.nc = nc
        self.sems = sems
        self.which = which
        self.engines = engines
        self.cnt = {n: 0 for n in sems}

    def _do(self, eng, fn, inc):
        if self.which == eng:
            inst = fn(self.engines[eng])
            if inc is not None:
                inst.then_inc(self.sems[inc[0]], inc[1])
        if inc is not None:
            self.cnt[inc[0]] += inc[1]

    def wait(self, eng, sem, val=None):
        if val is None:
            val = self.cnt[sem]
        val = max(0, val)
        if self.which == eng:
            self.engines[eng].wait_ge(self.sems[sem], val)

    def dma(self, q, out, in_, inc):
        self._do(q, lambda e: e.dma_start(out=out, in_=in_), inc)

    def dmat(self, q, out, in_, inc):
        self._do(q, lambda e: e.dma_start_transpose(out, in_), inc)

    def mm(self, out, lhsT, rhs, start, stop, inc=None):
        self._do("tensor", lambda e: e.matmul(out, lhsT, rhs, start=start, stop=stop), inc)

    def trans(self, out, in_, ident, inc=None):
        self._do("tensor", lambda e: e.transpose(out, in_, ident), inc)

    def act(self, out, in_, func, bias=0.0, scale=1.0, accum_out=None, inc=None):
        self._do("scalar", lambda e: e.activation(out, in_, func, bias=bias, scale=scale,
                                                  accum_out=accum_out), inc)

    def vop(self, name, *args, inc=None, eng="vector", **kw):
        self._do(eng, lambda e: getattr(e, name)(*args, **kw), inc)


def mkprog(build_fn, sem_names):
    nc = bass.Bass("TRN2", target_bir_lowering=False, debug=False)
    state = {}
    build_fn(None, nc, state)
    with contextlib.ExitStack() as st:
        sems = {n: st.enter_context(nc.semaphore(n)) for n in sem_names}
        for name, h in state["tensors"]:
            state[name] = st.enter_context(h)
        block = st.enter_context(nc.Block())
        engines = {}

        def reg(name):
            def deco(engobj):
                engines[name] = engobj
                build_fn(P(nc, sems, name, engines), nc, state)
            return deco

        block.sync(reg("sync"))
        block.tensor(reg("tensor"))
        block.vector(reg("vector"))
        block.scalar(reg("scalar"))
        block.gpsimd(reg("gpsimd"))
    return nc


def declare(nc, s, specs):
    s["tensors"] = []
    for kind, name, shape, dt in specs:
        if kind == "dram_in":
            s[name] = nc.dram_tensor(name, shape, dt, kind="ExternalInput").ap()
        elif kind == "dram_out":
            s[name] = nc.dram_tensor(name, shape, dt, kind="ExternalOutput").ap()
        elif kind == "sbuf":
            s["tensors"].append((name, nc.sbuf_tensor(name, shape, dt)))
        elif kind == "psum":
            s["tensors"].append((name, nc.psum_tensor(name, shape, dt)))


def psum_specs(with_bf=True):
    sp = [("psum", "ps%d" % i, [128, 512], F32) for i in range(4)]
    if with_bf:
        sp += [("psum", "pb0", [128, 1024], BF16), ("psum", "pb1", [128, 1024], BF16),
               ("psum", "pv", [128, 128], F32)]
    return sp


LN_COLS = ["c_sum", "c_sqs", "c_mean", "c_msq", "c_m2", "c_var", "c_std",
           "c_rstd", "c_mr", "c_nmr"]


def ln_specs():
    sp = [("sbuf", n, [128, 1], F32) for n in LN_COLS]
    sp += [("sbuf", "c_eps", [128, 1], F32), ("sbuf", "c_neg1", [128, 1], F32),
           ("sbuf", "htile", [128, D], F32),
           ("sbuf", "id_f32", [128, 128], F32)]
    return sp


def emit_init(p, s, n_extra=0):
    p.vop("memset", s["c_eps"][:, :], 1e-5, inc=("init", 1), eng="gpsimd")
    p.vop("memset", s["c_neg1"][:, :], -1.0, inc=("init", 1), eng="gpsimd")
    s["n_init"] = 2 + n_extra
    p.dma("sync", s["id_f32"][:, :], s["idf"], inc=("ld", 16))


def emit_load_xres(p, s):
    p.dma("sync", s["xres"][:, :, :], s["xin"].rearrange("(t p) d -> p t d", p=128),
          inc=("ld", 16))


def emit_ln(p, s, h_fm, tr0, tr1):
    for t in range(NT):
        xt = s["xres"][:, t, :]
        p.wait("scalar", "tr_ev")
        p.act(s["htile"][:, :], xt, AF.Square, inc=("a", 1))
        p.vop("reduce_sum", s["c_sum"][:, :], xt, axis=AX.X, inc=("v", 1))
        p.wait("vector", "a")
        p.vop("reduce_sum", s["c_sqs"][:, :], s["htile"][:, :], axis=AX.X, inc=("v", 1))
        p.wait("scalar", "v")
        p.act(s["c_mean"][:, :], s["c_sum"][:, :], AF.Copy, scale=1.0 / D, inc=("a", 1))
        p.act(s["c_msq"][:, :], s["c_sqs"][:, :], AF.Copy, scale=1.0 / D, inc=("a", 1))
        p.wait("vector", "a")
        p.vop("tensor_tensor", s["c_m2"][:, :], s["c_mean"][:, :], s["c_mean"][:, :],
              op=ALU.mult, inc=("v", 1))
        p.wait("vector", "v")
        p.vop("tensor_sub", s["c_var"][:, :], s["c_msq"][:, :], s["c_m2"][:, :], inc=("v", 1))
        p.wait("scalar", "v")
        p.wait("scalar", "init", s["n_init"])
        p.act(s["c_std"][:, :], s["c_var"][:, :], AF.Sqrt, bias=s["c_eps"][:, 0:1], inc=("a", 1))
        p.wait("vector", "a")
        p.vop("reciprocal", s["c_rstd"][:, :], s["c_std"][:, :], inc=("v", 1))
        p.wait("vector", "v")
        p.vop("tensor_tensor", s["c_mr"][:, :], s["c_mean"][:, :], s["c_rstd"][:, :],
              op=ALU.mult, inc=("v", 1))
        p.wait("vector", "v")
        p.vop("tensor_scalar", s["c_nmr"][:, :], s["c_mr"][:, :], s["c_neg1"][:, 0:1], None,
              op0=ALU.mult, inc=("v", 1))
        p.wait("scalar", "v")
        p.act(s["htile"][:, :], xt, AF.Identity, bias=s["c_nmr"][:, 0:1],
              scale=s["c_rstd"][:, 0:1], inc=("a", 1))
        p.wait("tensor", "a")
        for dc in range(NDC):
            ps = (tr0 if dc % 2 == 0 else tr1)[:, 0:128]
            p.wait("tensor", "tr_ev")
            p.trans(ps, s["htile"][:, dc * 128:(dc + 1) * 128], s["id_f32"][:, :],
                    inc=("t", 1))
            p.wait("vector", "t")
            p.vop("tensor_copy", h_fm[:, dc, t * 128:(t + 1) * 128], ps, inc=("tr_ev", 1))
    p.wait("tensor", "tr_ev")
    p.wait("scalar", "tr_ev")


def emit_store_xres(p, s, guard_sems):
    for g in guard_sems:
        p.wait("gpsimd", g)
    p.dma("gpsimd", s["xout"].rearrange("(t p) d -> p t d", p=128), s["xres"][:, :, :],
          inc=("st", 16))
    p.wait("gpsimd", "st")


def l0a_build(p, nc, s):
    if p is None:
        declare(nc, s, [
            ("dram_in", "xin", [TOK, D], F32),
            ("dram_in", "wp", [NDC, 48, 128, 128], F32),
            ("dram_in", "idf", [128, 128], F32),
            ("dram_out", "qout", [D, TOK], BF16),
            ("dram_out", "kvout", [2 * D, TOK], BF16),
            ("sbuf", "xres", [128, NT, D], F32),
            ("sbuf", "h_fm", [128, NDC, TOK], F32R),
            ("sbuf", "wchunk", [128, 2, 128], F32R),
            ("sbuf", "stage", [128, 2, 512], BF16),
        ] + ln_specs() + psum_specs(with_bf=False))
        return

    emit_init(p, s)
    emit_load_xres(p, s)
    for e in ("vector", "scalar", "tensor"):
        p.wait(e, "ld")
    emit_ln(p, s, s["h_fm"], s["ps0"], s["ps1"])

    for oc in range(48):
        for tt in range(2):
            j = oc * 2 + tt
            ps = s["ps%d" % (j % 2)]
            for dc in range(NDC):
                wslot = s["wchunk"][:, dc % 2, :]
                p.wait("sync", "mmw")
                p.dma("sync", wslot, s["wp"][dc, oc].bitcast(F32R), inc=("wld", 16))
                p.wait("tensor", "wld")
                if dc == 0:
                    p.wait("tensor", "pevac")
                p.mm(ps[:, :], wslot, s["h_fm"][:, dc, tt * 512:(tt + 1) * 512],
                     start=(dc == 0), stop=(dc == NDC - 1), inc=("mmw", 1))
            stg = s["stage"][:, j % 2, :]
            p.wait("scalar", "mmw")
            p.wait("scalar", "stdma")
            p.act(stg, ps[:, :], AF.Copy, scale=(QS0 if oc < 16 else 1.0), inc=("pevac", 1))
            p.wait("gpsimd", "pevac")
            dst = s["qout"] if oc < 16 else s["kvout"]
            row0 = oc * 128 if oc < 16 else (oc - 16) * 128
            p.dma("gpsimd", dst[row0:row0 + 128, tt * 512:(tt + 1) * 512], stg,
                  inc=("stdma", 16))
    p.wait("gpsimd", "stdma")


SEMS_L0A = ["ld", "wld", "init", "v", "a", "t", "tr_ev", "mmw", "pevac", "stdma"]


def attn_build_mk(l1):
    def build(p, nc, s):
        if p is None:
            sp = [
                ("dram_in", "xin", [TOK, D], F32),
                ("dram_in", "wo", [D, D], F32),
                ("dram_in", "qin", [D, TOK], BF16),
                ("dram_in", "m_mine", [128, 128], F32),
                ("dram_in", "m_sib", [128, 128], F32),
                ("dram_in", "idb", [128, 128], BF16),
                ("dram_out", "xout", [TOK, D], F32),
                ("sbuf", "xres", [128, NT, D], F32),
                ("sbuf", "q_sb", [128, TOK], BF16),
                ("sbuf", "k_sb", [128, 2048], BF16),
                ("sbuf", "attn", [128, 2048], BF16),
                ("sbuf", "attnT", [128, 2048], BF16),
                ("sbuf", "aout_fm", [128, NDC, TOK], BF16),
                ("sbuf", "mask0", [128, 128], F32),
                ("sbuf", "mask1", [128, 128], F32),
                ("sbuf", "id_bf", [128, 128], BF16),
                ("sbuf", "wochunk", [128, 2, 512], F32),
                ("sbuf", "wob", [128, 2, 512], BF16),
                ("sbuf", "nmax", [128, 4], F32),
                ("sbuf", "nmaxc", [128, 1], F32),
                ("sbuf", "rsums", [128, 4], F32),
                ("sbuf", "rsum", [128, 1], F32),
                ("sbuf", "rcp", [128, 1], F32),
            ] + psum_specs(with_bf=True)
            if l1:
                sp += [
                    ("dram_in", "qrin", [64, TOK], BF16),
                    ("dram_in", "ckv_mine", [576, TOK], F32),
                    ("dram_in", "ckv_sib", [576, TOK], F32),
                    ("dram_in", "wkup", [4, NH, 128, 128], F32),
                    ("dram_in", "wvup", [512, D], F32),
                    ("sbuf", "qr_sb", [64, TOK], BF16),
                    ("sbuf", "ckv_sb", [128, 4, 2048], F32R),
                    ("sbuf", "krf", [64, 2048], F32),
                    ("sbuf", "kr_sb", [64, 2048], BF16),
                    ("sbuf", "wku", [128, 2, 128], F32R),
                    ("sbuf", "wvu", [128, 4, 512], F32R),
                    ("sbuf", "v4", [128, 16, 512], BF16),
                ]
            else:
                sp += [
                    ("dram_in", "kv_mine", [2 * D, TOK], BF16),
                    ("dram_in", "kv_sib", [2 * D, TOK], BF16),
                    ("sbuf", "v_sb", [128, 16, 128], BF16),
                ]
            declare(nc, s, sp)
            return

        ps_s = [s["ps0"], s["ps1"], s["ps2"], s["ps3"]]
        emit_load_xres(p, s)
        p.dma("sync", s["mask0"][:, :], s["m_mine"], inc=("ld", 16))
        p.dma("sync", s["mask1"][:, :], s["m_sib"], inc=("ld", 16))
        p.dma("sync", s["id_bf"][:, :], s["idb"], inc=("ld", 16))
        if l1:
            p.dma("sync", s["qr_sb"][:, :], s["qrin"], inc=("ld", 16))
            for cc in range(4):
                p.dma("sync", s["ckv_sb"][:, cc, 0:1024],
                      s["ckv_mine"][cc * 128:(cc + 1) * 128, :].bitcast(F32R), inc=("ld", 16))
                p.dma("sync", s["ckv_sb"][:, cc, 1024:2048],
                      s["ckv_sib"][cc * 128:(cc + 1) * 128, :].bitcast(F32R), inc=("ld", 16))
            p.dma("sync", s["krf"][:, 0:1024], s["ckv_mine"][512:576, :], inc=("ld", 16))
            p.dma("sync", s["krf"][:, 1024:2048], s["ckv_sib"][512:576, :], inc=("ld", 16))
            p.wait("vector", "ld")
            p.vop("tensor_copy", s["kr_sb"][:, :], s["krf"][:, :], inc=("v", 1))
        for e in ("tensor", "vector", "scalar"):
            p.wait(e, "ld")
        p.wait("tensor", "v")

        for h in range(NH):
            p.wait("sync", "pvev", h * NT)
            p.dma("sync", s["q_sb"][:, :], s["qin"][h * 128:(h + 1) * 128, :],
                  inc=("kvld", 16))
            if not l1:
                p.dma("sync", s["k_sb"][:, 0:1024], s["kv_mine"][h * 128:(h + 1) * 128, :],
                      inc=("kvld", 16))
                p.dma("sync", s["k_sb"][:, 1024:2048], s["kv_sib"][h * 128:(h + 1) * 128, :],
                      inc=("kvld", 16))
                for cb in range(8):
                    p.dmat("sync", s["v_sb"][:, cb, :],
                           s["kv_mine"][2048 + h * 128:2048 + (h + 1) * 128,
                                        cb * 128:(cb + 1) * 128], inc=("kvld", 16))
                    p.dmat("sync", s["v_sb"][:, 8 + cb, :],
                           s["kv_sib"][2048 + h * 128:2048 + (h + 1) * 128,
                                       cb * 128:(cb + 1) * 128], inc=("kvld", 16))
                p.wait("tensor", "kvld")
            else:
                if h % 4 == 0:
                    g = h // 4
                    p.wait("sync", "pvev", h * NT)
                    p.dma("sync", s["wvu"][:, :, :],
                          s["wvup"].rearrange("(c p) o -> p c o", p=128)
                          [:, :, g * 512:(g + 1) * 512].bitcast(F32R), inc=("vwld", 16))
                    p.wait("tensor", "vwld")
                    p.wait("tensor", "sfree")
                    for tc in range(16):
                        psk = ps_s[tc % 4]
                        p.wait("tensor", "vevac")
                        for cc in range(4):
                            p.mm(psk[:, :], s["ckv_sb"][:, cc, tc * 128:(tc + 1) * 128],
                                 s["wvu"][:, cc, :], start=(cc == 0), stop=(cc == 3),
                                 inc=("mmv", 1) if cc == 3 else None)
                        p.wait("vector", "mmv")
                        p.vop("tensor_copy", s["v4"][:, tc, :], psk[:, :], inc=("vevac", 1))
                p.wait("sync", "pvev", h * NT)
                p.wait("tensor", "sfree")
                p.wait("tensor", "vevac")
                for cc in range(4):
                    wslot = s["wku"][:, cc % 2, :]
                    p.wait("sync", "mmk")
                    p.dma("sync", wslot, s["wkup"][cc, h].bitcast(F32R), inc=("kwld", 16))
                    p.wait("tensor", "kwld")
                    for kt in range(4):
                        if cc == 0:
                            p.wait("tensor", "kevac")
                        p.mm(ps_s[kt][:, :], wslot,
                             s["ckv_sb"][:, cc, kt * 512:(kt + 1) * 512],
                             start=(cc == 0), stop=(cc == 3), inc=("mmk", 1))
                p.wait("vector", "mmk")
                for kt in range(4):
                    p.vop("tensor_copy", s["k_sb"][:, kt * 512:(kt + 1) * 512],
                          ps_s[kt][:, :], inc=("kevac", 1))
                p.wait("tensor", "kevac")
                p.wait("tensor", "kvld")

            for qb in range(NT):
                nb = qb + 1
                L = 2 * nb * 128
                nkt = (L + 511) // 512
                qlh = s["q_sb"][:, qb * 128:(qb + 1) * 128]
                p.wait("tensor", "sfree")
                segs = []
                for run, base in ((0, 0), (1, 1024)):
                    c0 = run * nb * 128
                    done = 0
                    while done < nb * 128:
                        col = c0 + done
                        w = min(512 - (col % 512), nb * 128 - done)
                        segs.append((col, w, base + done))
                        done += w
                tile_written = [False] * 4
                for (col, w, kcol) in segs:
                    kt = col // 512
                    st0 = not tile_written[kt]
                    tile_written[kt] = True
                    if l1:
                        p.mm(ps_s[kt][:, col % 512:col % 512 + w], qlh,
                             s["k_sb"][:, kcol:kcol + w], start=st0, stop=False)
                        p.mm(ps_s[kt][:, col % 512:col % 512 + w],
                             s["qr_sb"][:, qb * 128:(qb + 1) * 128],
                             s["kr_sb"][:, kcol:kcol + w], start=False, stop=True,
                             inc=("ms", 1))
                    else:
                        p.mm(ps_s[kt][:, col % 512:col % 512 + w], qlh,
                             s["k_sb"][:, kcol:kcol + w], start=st0, stop=True,
                             inc=("ms", 1))
                p.wait("vector", "ms")
                for run in range(2):
                    col = (run * nb + nb - 1) * 128
                    kt, off = col // 512, col % 512
                    msk = s["mask0"] if run == 0 else s["mask1"]
                    p.vop("tensor_tensor", ps_s[kt][:, off:off + 128],
                          ps_s[kt][:, off:off + 128], msk[:, :], op=ALU.add, inc=("v", 1))
                p.wait("vector", "v")
                for kt in range(nkt):
                    w = min(512, L - kt * 512)
                    p.vop("reduce_max", s["nmax"][:, kt:kt + 1], ps_s[kt][:, 0:w],
                          axis=AX.X, negate=True, inc=("v", 1))
                p.wait("vector", "v")
                if nkt > 1:
                    p.vop("tensor_reduce", s["nmaxc"][:, :], s["nmax"][:, 0:nkt],
                          axis=AX.X, op=ALU.min, inc=("v", 1))
                else:
                    p.vop("tensor_copy", s["nmaxc"][:, :], s["nmax"][:, 0:1], inc=("v", 1))
                p.wait("scalar", "v")
                p.wait("scalar", "t")
                for kt in range(nkt):
                    w = min(512, L - kt * 512)
                    p.act(s["attn"][:, kt * 512:kt * 512 + w], ps_s[kt][:, 0:w], AF.Exp,
                          bias=s["nmaxc"][:, 0:1], accum_out=s["rsums"][:, kt:kt + 1],
                          inc=("sfree", 1))
                p.wait("vector", "sfree")
                if nkt > 1:
                    p.vop("reduce_sum", s["rsum"][:, :], s["rsums"][:, 0:nkt], axis=AX.X,
                          inc=("v", 1))
                else:
                    p.vop("tensor_copy", s["rsum"][:, :], s["rsums"][:, 0:1], inc=("v", 1))
                p.wait("vector", "v")
                p.vop("reciprocal", s["rcp"][:, :], s["rsum"][:, :], inc=("v", 1))
                p.wait("vector", "v")
                p.vop("tensor_scalar", s["attn"][:, 0:L], s["attn"][:, 0:L],
                      s["rcp"][:, 0:1], None, op0=ALU.mult, inc=("v", 1))
                p.wait("tensor", "v")
                nkb = 2 * nb
                for kb in range(nkb):
                    bank = s["pb0"] if (kb // 8) % 2 == 0 else s["pb1"]
                    slot = kb % 8
                    if slot == 0:
                        p.wait("tensor", "atev", p.cnt["atev"])
                    p.trans(bank[:, slot * 128:(slot + 1) * 128],
                            s["attn"][:, kb * 128:(kb + 1) * 128], s["id_bf"][:, :],
                            inc=("t", 1))
                    if slot == 7 or kb == nkb - 1:
                        p.wait("vector", "t")
                        n = slot + 1
                        p.vop("tensor_copy", s["attnT"][:, (kb - n + 1) * 128:(kb + 1) * 128],
                              bank[:, 0:n * 128], inc=("atev", 1))
                p.wait("tensor", "atev")
                p.wait("tensor", "pvev")
                for kb in range(nkb):
                    run, j = (0, kb) if kb < nb else (1, kb - nb)
                    if l1:
                        vt = s["v4"][:, run * 8 + j, (h % 4) * 128:(h % 4 + 1) * 128]
                    else:
                        vt = s["v_sb"][:, run * 8 + j, :]
                    p.mm(s["pv"][:, :], vt, s["attnT"][:, kb * 128:(kb + 1) * 128],
                         start=(kb == 0), stop=(kb == nkb - 1),
                         inc=("mpv", 1) if kb == nkb - 1 else None)
                p.wait("vector", "mpv")
                p.vop("tensor_copy", s["aout_fm"][:, h, qb * 128:(qb + 1) * 128],
                      s["pv"][:, :], inc=("pvev", 1))

        p.wait("tensor", "pvev")
        p.wait("tensor", "sfree")
        for ot in range(4):
            for qg in range(2):
                for dc in range(NDC):
                    wslot = s["wochunk"][:, dc % 2, :]
                    wbslot = s["wob"][:, dc % 2, :]
                    p.wait("sync", "wcast")
                    p.dma("sync", wslot,
                          s["wo"][dc * 128:(dc + 1) * 128, ot * 512:(ot + 1) * 512],
                          inc=("owld", 16))
                    p.wait("vector", "owld")
                    p.wait("vector", "mmo")
                    p.vop("tensor_copy", wbslot, wslot, inc=("wcast", 1))
                    p.wait("tensor", "wcast")
                    if dc == 0:
                        p.wait("tensor", "oev", p.cnt["oev"])
                    for j in range(4):
                        qb = qg * 4 + j
                        p.mm(ps_s[j][:, :],
                             s["aout_fm"][:, dc, qb * 128:(qb + 1) * 128], wbslot,
                             start=(dc == 0), stop=(dc == NDC - 1), inc=("mmo", 1))
                p.wait("vector", "mmo")
                for j in range(4):
                    qb = qg * 4 + j
                    p.vop("tensor_tensor", s["xres"][:, qb, ot * 512:(ot + 1) * 512],
                          s["xres"][:, qb, ot * 512:(ot + 1) * 512], ps_s[j][:, :],
                          op=ALU.add, inc=("oev", 1))
        emit_store_xres(p, s, ["oev"])
    return build


SEMS_ATTN = ["ld", "kvld", "kwld", "vwld", "owld", "wcast", "v", "t", "ms", "mmk",
             "mmv", "mmo", "mpv", "sfree", "atev", "pvev", "kevac", "vevac", "oev", "st"]


def moe_build(p, nc, s):
    if p is None:
        declare(nc, s, [
            ("dram_in", "xin", [TOK, D], F32),
            ("dram_in", "rw", [128, NDC, 8], F32),
            ("dram_in", "w1", [NE, NDC, 4, 128, 128], F32),
            ("dram_in", "w3", [NE, NDC, 4, 128, 128], F32),
            ("dram_in", "w2", [NE, FF, D], F32),
            ("dram_in", "idf", [128, 128], F32),
            ("dram_out", "xout", [TOK, D], F32),
            ("sbuf", "xres", [128, NT, D], F32),
            ("sbuf", "h_fm", [128, NDC, TOK], F32R),
            ("sbuf", "gates", [128, NT, NE], F32),
            ("sbuf", "sc", [128, NT, NE], F32),
            ("sbuf", "w13", [128, 2, NDC, 128], F32R),
            ("sbuf", "hid", [128, 4, TOK], F32R),
            ("sbuf", "sg", [128, 512], F32),
            ("sbuf", "w2ot", [128, 4, 512], F32R),
            ("sbuf", "c_m1", [128, 1], F32),
            ("sbuf", "c_nm1", [128, 1], F32),
            ("sbuf", "c_m2r", [128, 1], F32),
            ("sbuf", "c_em", [128, 1], F32),
            ("sbuf", "c_den", [128, 1], F32),
            ("sbuf", "c_rden", [128, 1], F32),
            ("sbuf", "c_neg9", [128, 1], F32),
            ("sbuf", "c_one", [128, 1], F32),
            ("sbuf", "rwsb", [128, NDC, 8], F32R),
            ("sbuf", "t_eq", [128, NE], F32),
            ("sbuf", "t_t", [128, NE], F32),
            ("sbuf", "t_e1", [128, NE], F32),
            ("sbuf", "t_ge", [128, NE], F32),
        ] + ln_specs() + psum_specs(with_bf=False))
        return

    emit_init(p, s, n_extra=2)
    p.vop("memset", s["c_neg9"][:, :], NEG, inc=("init", 1), eng="gpsimd")
    p.vop("memset", s["c_one"][:, :], 1.0, inc=("init", 1), eng="gpsimd")
    emit_load_xres(p, s)
    for e in ("vector", "scalar", "tensor"):
        p.wait(e, "ld")
    emit_ln(p, s, s["h_fm"], s["ps0"], s["ps1"])

    p.dma("sync", s["rwsb"][:, :, :], s["rw"].bitcast(F32R), inc=("wld", 16))
    p.wait("tensor", "wld")
    vmark = 0
    for t in range(NT):
        ps = s["ps2"]
        p.wait("tensor", "v", vmark)
        for dc in range(NDC):
            p.mm(ps[:, 0:8], s["h_fm"][:, dc, t * 128:(t + 1) * 128],
                 s["rwsb"][:, dc, :], start=(dc == 0), stop=(dc == NDC - 1),
                 inc=("mmw", 1) if dc == NDC - 1 else None)
        p.wait("vector", "mmw")
        p.vop("tensor_copy", s["sc"][:, t, :], ps[:, 0:8], inc=("v", 1))
        vmark = p.cnt["v"]
        p.wait("vector", "v")
        st = s["sc"][:, t, :]
        p.vop("reduce_max", s["c_m1"][:, :], st, axis=AX.X, inc=("v", 1))
        p.vop("reduce_max", s["c_nm1"][:, :], st, axis=AX.X, negate=True, inc=("v", 1))
        p.wait("vector", "v")
        p.vop("tensor_scalar", s["t_eq"][:, :], st, s["c_m1"][:, 0:1], None,
              op0=ALU.is_ge, inc=("v", 1))
        p.wait("vector", "v")
        p.wait("vector", "init", s["n_init"])
        p.vop("scalar_tensor_tensor", s["t_t"][:, :], s["t_eq"][:, :], s["c_neg9"][:, 0:1],
              st, op0=ALU.mult, op1=ALU.add, inc=("v", 1))
        p.wait("vector", "v")
        p.vop("reduce_max", s["c_m2r"][:, :], s["t_t"][:, :], axis=AX.X, inc=("v", 1))
        p.wait("scalar", "v")
        p.act(s["t_e1"][:, :], st, AF.Exp, bias=s["c_nm1"][:, 0:1], inc=("a", 1))
        p.act(s["c_em"][:, :], s["c_m2r"][:, :], AF.Exp, bias=s["c_nm1"][:, 0:1], inc=("a", 1))
        p.wait("vector", "a")
        p.vop("tensor_tensor", s["c_den"][:, :], s["c_em"][:, :], s["c_one"][:, 0:1],
              op=ALU.add, inc=("v", 1))
        p.wait("vector", "v")
        p.vop("reciprocal", s["c_rden"][:, :], s["c_den"][:, :], inc=("v", 1))
        p.wait("vector", "v")
        p.vop("tensor_scalar", s["t_ge"][:, :], st, s["c_m2r"][:, 0:1], None,
              op0=ALU.is_ge, inc=("v", 1))
        p.wait("vector", "v")
        p.vop("tensor_tensor", s["t_e1"][:, :], s["t_e1"][:, :], s["t_ge"][:, :],
              op=ALU.mult, inc=("v", 1))
        p.wait("vector", "v")
        p.vop("tensor_scalar", s["gates"][:, t, :], s["t_e1"][:, :], s["c_rden"][:, 0:1],
              None, op0=ALU.mult, inc=("v", 1))
    p.wait("tensor", "v")

    for e in range(NE):
        for ffq in range(4):
            for w_i, nm in ((0, "w1"), (1, "w3")):
                p.wait("sync", "mmw")
                p.dma("sync", s["w13"][:, w_i, :, :],
                      s[nm][e].rearrange("c q p f -> p c q f")[:, :, ffq, :]
                      .bitcast(F32R), inc=("wld", 16))
            p.wait("tensor", "wld")
            for tt in range(2):
                p.wait("tensor", "hev")
                for w_i in range(2):
                    ps = s["ps%d" % (2 + w_i)]
                    for dc in range(NDC):
                        p.mm(ps[:, :], s["w13"][:, w_i, dc, :],
                             s["h_fm"][:, dc, tt * 512:(tt + 1) * 512],
                             start=(dc == 0), stop=(dc == NDC - 1), inc=("mmw", 1))
                p.wait("scalar", "mmw")
                p.wait("scalar", "hev")
                p.act(s["sg"][:, :], s["ps2"][:, :], AF.Silu, inc=("a", 1))
                p.wait("vector", "a")
                p.vop("tensor_tensor", s["hid"][:, ffq, tt * 512:(tt + 1) * 512],
                      s["sg"][:, :], s["ps3"][:, :], op=ALU.mult, inc=("hev", 1))
        p.wait("tensor", "hev")
        for ot in range(4):
            p.wait("sync", "mmw")
            p.dma("sync", s["w2ot"][:, :, :],
                  s["w2"][e].rearrange("(c p) o -> p c o", p=128)[:, :, ot * 512:(ot + 1) * 512]
                  .bitcast(F32R), inc=("wld", 16))
            p.wait("tensor", "wld")
            for tq in range(NT):
                ps = s["ps%d" % (tq % 2)]
                p.wait("tensor", "oev")
                for ffc in range(4):
                    p.mm(ps[:, :], s["hid"][:, ffc, tq * 128:(tq + 1) * 128],
                         s["w2ot"][:, ffc, :], start=(ffc == 0), stop=(ffc == 3),
                         inc=("mmw", 1) if ffc == 3 else None)
                p.wait("vector", "mmw")
                p.vop("scalar_tensor_tensor", s["xres"][:, tq, ot * 512:(ot + 1) * 512],
                      ps[:, :], s["gates"][:, tq, e:e + 1],
                      s["xres"][:, tq, ot * 512:(ot + 1) * 512],
                      op0=ALU.mult, op1=ALU.add, inc=("oev", 1))
    emit_store_xres(p, s, ["oev"])


SEMS_MOE = ["ld", "wld", "init", "v", "a", "t", "tr_ev", "mmw", "hev", "oev", "st"]


def l1a_build(p, nc, s):
    if p is None:
        declare(nc, s, [
            ("dram_in", "xin", [TOK, D], F32),
            ("dram_in", "wq", [NDC, NH, 128, 128], F32),
            ("dram_in", "wqr", [NDC, 128, 64], F32),
            ("dram_in", "wqrs", [NDC, 128, 64], F32),
            ("dram_in", "wkv", [NDC, 4, 128, 128], F32),
            ("dram_in", "wkr", [4, 128, 64], F32),
            ("dram_in", "wkrs", [4, 128, 64], F32),
            ("dram_in", "ropetab", [128, TOK], F32),
            ("dram_in", "idf", [128, 128], F32),
            ("dram_out", "qout", [D, TOK], BF16),
            ("dram_out", "qrout", [64, TOK], BF16),
            ("dram_out", "ckvout", [576, TOK], F32),
            ("sbuf", "xres", [128, NT, D], F32),
            ("sbuf", "h_fm", [128, NDC, TOK], F32R),
            ("sbuf", "ckv_sb", [128, 4, TOK], F32R),
            ("sbuf", "wchunk", [128, 2, 128], F32R),
            ("sbuf", "stage", [128, 2, 512], BF16),
            ("sbuf", "ctab", [64, TOK], F32),
            ("sbuf", "sstab", [64, TOK], F32),
            ("sbuf", "rraw", [64, TOK], F32),
            ("sbuf", "rrsw", [64, TOK], F32),
            ("sbuf", "rt1", [64, TOK], F32),
            ("sbuf", "roped", [64, TOK], F32),
            ("sbuf", "ropedb", [64, TOK], BF16),
            ("sbuf", "c_qs1", [128, 1], F32),
        ] + ln_specs() + psum_specs(with_bf=False))
        return

    emit_init(p, s, n_extra=1)
    p.vop("memset", s["c_qs1"][:, :], QS1, inc=("init", 1), eng="gpsimd")
    p.dma("sync", s["ctab"][:, :], s["ropetab"][0:64, :], inc=("ld", 16))
    p.dma("sync", s["sstab"][:, :], s["ropetab"][64:128, :], inc=("ld", 16))
    emit_load_xres(p, s)
    for e in ("vector", "scalar", "tensor"):
        p.wait(e, "ld")
    emit_ln(p, s, s["h_fm"], s["ps0"], s["ps1"])

    for oc in range(NH):
        for tt in range(2):
            j = oc * 2 + tt
            ps = s["ps%d" % (j % 2)]
            for dc in range(NDC):
                wslot = s["wchunk"][:, dc % 2, :]
                p.wait("sync", "mmw")
                p.dma("sync", wslot, s["wq"][dc, oc].bitcast(F32R), inc=("wld", 16))
                p.wait("tensor", "wld")
                if dc == 0:
                    p.wait("tensor", "pevac")
                p.mm(ps[:, :], wslot, s["h_fm"][:, dc, tt * 512:(tt + 1) * 512],
                     start=(dc == 0), stop=(dc == NDC - 1), inc=("mmw", 1))
            stg = s["stage"][:, j % 2, :]
            p.wait("scalar", "mmw")
            p.wait("scalar", "stdma")
            p.act(stg, ps[:, :], AF.Copy, scale=QS1, inc=("pevac", 1))
            p.wait("gpsimd", "pevac")
            p.dma("gpsimd", s["qout"][oc * 128:(oc + 1) * 128, tt * 512:(tt + 1) * 512],
                  stg, inc=("stdma", 16))
    for oc in range(4):
        for tt in range(2):
            j = oc * 2 + tt
            ps = s["ps%d" % (j % 2)]
            for dc in range(NDC):
                wslot = s["wchunk"][:, dc % 2, :]
                p.wait("sync", "mmw")
                p.dma("sync", wslot, s["wkv"][dc, oc].bitcast(F32R), inc=("wld", 16))
                p.wait("tensor", "wld")
                if dc == 0:
                    p.wait("tensor", "pevac")
                p.mm(ps[:, :], wslot, s["h_fm"][:, dc, tt * 512:(tt + 1) * 512],
                     start=(dc == 0), stop=(dc == NDC - 1), inc=("mmw", 1))
            p.wait("vector", "mmw")
            p.vop("tensor_copy", s["ckv_sb"][:, oc, tt * 512:(tt + 1) * 512], ps[:, :],
                  inc=("pevac", 1))
            p.wait("gpsimd", "pevac")
            p.dma("gpsimd", s["ckvout"][oc * 128:(oc + 1) * 128, tt * 512:(tt + 1) * 512],
                  s["ckv_sb"][:, oc, tt * 512:(tt + 1) * 512].bitcast(F32), inc=("stdma", 16))

    for which in ("qr", "kr"):
        ps = s["ps2"]
        for swapped, dst in ((False, "rraw"), (True, "rrsw")):
            wsrc = {("qr", False): "wqr", ("qr", True): "wqrs",
                    ("kr", False): "wkr", ("kr", True): "wkrs"}[(which, swapped)]
            for tt in range(2):
                nchunk = NDC if which == "qr" else 4
                for cc in range(nchunk):
                    wslot = s["wchunk"][:, cc % 2, :]
                    p.wait("sync", "mmw")
                    p.dma("sync", wslot[:, 0:64], s[wsrc][cc].bitcast(F32R), inc=("wld", 16))
                    p.wait("tensor", "wld")
                    if cc == 0:
                        p.wait("tensor", "rev", p.cnt["rev"])
                    rhs = (s["h_fm"][:, cc, tt * 512:(tt + 1) * 512] if which == "qr"
                           else s["ckv_sb"][:, cc, tt * 512:(tt + 1) * 512])
                    p.mm(ps[0:64, 0:512], wslot[:, 0:64], rhs,
                         start=(cc == 0), stop=(cc == nchunk - 1), inc=("mmw", 1))
                p.wait("vector", "mmw")
                p.vop("tensor_copy", s[dst][:, tt * 512:(tt + 1) * 512], ps[0:64, 0:512],
                      inc=("rev", 1))
        p.wait("vector", "rev")
        p.vop("tensor_tensor", s["rt1"][:, :], s["rraw"][:, :], s["ctab"][:, :],
              op=ALU.mult, inc=("v", 1))
        p.vop("tensor_tensor", s["roped"][:, :], s["rrsw"][:, :], s["sstab"][:, :],
              op=ALU.mult, inc=("v", 1))
        p.wait("vector", "v")
        p.vop("tensor_tensor", s["roped"][:, :], s["roped"][:, :], s["rt1"][:, :],
              op=ALU.add, inc=("v", 1))
        p.wait("vector", "v")
        p.wait("vector", "init", s["n_init"])
        if which == "qr":
            p.vop("tensor_scalar", s["ropedb"][:, :], s["roped"][:, :], s["c_qs1"][0:64, 0:1],
                  None, op0=ALU.mult, inc=("v", 1))
            p.wait("gpsimd", "v")
            p.dma("gpsimd", s["qrout"], s["ropedb"][:, :], inc=("stdma", 16))
        else:
            p.wait("gpsimd", "v")
            p.dma("gpsimd", s["ckvout"][512:576, :], s["roped"][:, :], inc=("stdma", 16))
    p.wait("gpsimd", "stdma")


SEMS_L1A = ["ld", "wld", "init", "v", "a", "t", "tr_ev", "mmw", "pevac", "rev", "stdma"]


def ffn_build(p, nc, s):
    if p is None:
        declare(nc, s, [
            ("dram_in", "xin", [TOK, D], F32),
            ("dram_in", "gw", [NDC, NIC, 128, 128], F32),
            ("dram_in", "uw", [NDC, NIC, 128, 128], F32),
            ("dram_in", "dw", [ITER, D], F32),
            ("dram_in", "gs", [128, NIC], F32),
            ("dram_in", "us", [128, NIC], F32),
            ("dram_in", "dsb", [128, D], F32),
            ("dram_in", "idf", [128, 128], F32),
            ("dram_out", "xout", [TOK, D], F32),
            ("sbuf", "xres", [128, NT, D], F32),
            ("sbuf", "h_bf", [128, NDC, TOK], BF16),
            ("sbuf", "hid", [128, NIC, 512], BF16),
            ("sbuf", "gsb", [128, NIC], F32),
            ("sbuf", "usb", [128, NIC], F32),
            ("sbuf", "dsb_sb", [128, D], F32),
            ("sbuf", "wgf", [128, NDC, 128], F32),
            ("sbuf", "wuf", [128, NDC, 128], F32),
            ("sbuf", "wgb", [128, NDC, 128], BF16),
            ("sbuf", "wub", [128, NDC, 128], BF16),
            ("sbuf", "dwf", [128, 512], F32),
            ("sbuf", "dwb", [128, 2, 512], BF16),
            ("sbuf", "sg", [128, 512], F32),
            ("sbuf", "ytmp", [128, 512], F32),
        ] + ln_specs() + psum_specs(with_bf=False))
        return

    emit_init(p, s)
    emit_load_xres(p, s)
    p.dma("sync", s["gsb"][:, :], s["gs"], inc=("ld", 16))
    p.dma("sync", s["usb"][:, :], s["us"], inc=("ld", 16))
    p.dma("sync", s["dsb_sb"][:, :], s["dsb"], inc=("ld", 16))
    for e in ("vector", "scalar", "tensor"):
        p.wait(e, "ld")
    emit_ln(p, s, s["h_bf"], s["ps0"], s["ps1"])

    for th in range(2):
        t0 = th * 512
        for ic in range(NIC):
            p.wait("sync", "cast")
            p.dma("sync", s["wgf"][:, :, :],
                  s["gw"].rearrange("c i p f -> p c i f")[:, :, ic, :], inc=("wld", 16))
            p.dma("sync", s["wuf"][:, :, :],
                  s["uw"].rearrange("c i p f -> p c i f")[:, :, ic, :], inc=("wld", 16))
            p.wait("vector", "wld")
            p.wait("vector", "mmh")
            p.vop("tensor_copy", s["wgb"][:, :, :], s["wgf"][:, :, :], inc=("cast", 1))
            p.vop("tensor_copy", s["wub"][:, :, :], s["wuf"][:, :, :], inc=("cast", 1))
            p.wait("tensor", "cast")
            p.wait("tensor", "hev")
            for w_i in range(2):
                ps = s["ps%d" % (2 + w_i)]
                wb = s["wgb"] if w_i == 0 else s["wub"]
                for dc in range(NDC):
                    p.mm(ps[:, :], wb[:, dc, :], s["h_bf"][:, dc, t0:t0 + 512],
                         start=(dc == 0), stop=(dc == NDC - 1), inc=("mmh", 1))
            p.wait("scalar", "mmh")
            p.wait("scalar", "hev")
            p.act(s["sg"][:, :], s["ps2"][:, :], AF.Silu, scale=s["gsb"][:, ic:ic + 1],
                  inc=("a", 1))
            p.wait("vector", "a")
            p.vop("scalar_tensor_tensor", s["hid"][:, ic, :], s["ps3"][:, :],
                  s["usb"][:, ic:ic + 1], s["sg"][:, :], op0=ALU.mult, op1=ALU.mult,
                  inc=("hev", 1))
        p.wait("tensor", "hev")
        for ot in range(4):
            for ic in range(NIC):
                p.wait("sync", "dcast")
                p.dma("sync", s["dwf"][:, :],
                      s["dw"][ic * 128:(ic + 1) * 128, ot * 512:(ot + 1) * 512],
                      inc=("dwld", 16))
                p.wait("vector", "dwld")
                p.wait("vector", "mmd")
                p.vop("tensor_copy", s["dwb"][:, ic % 2, :], s["dwf"][:, :], inc=("dcast", 1))
                p.wait("tensor", "dcast")
                if ic == 0:
                    p.wait("tensor", "oev", p.cnt["oev"])
                for tq in range(NT // 2):
                    p.mm(s["ps%d" % tq][:, :], s["hid"][:, ic, tq * 128:(tq + 1) * 128],
                         s["dwb"][:, ic % 2, :], start=(ic == 0), stop=(ic == NIC - 1),
                         inc=("mmd", 1))
            p.wait("vector", "mmd")
            for tq in range(NT // 2):
                qb = th * 4 + tq
                p.vop("tensor_tensor", s["ytmp"][:, :], s["ps%d" % tq][:, :],
                      s["dsb_sb"][:, ot * 512:(ot + 1) * 512], op=ALU.mult, inc=("v", 1))
                p.wait("vector", "v")
                p.vop("tensor_tensor", s["xres"][:, qb, ot * 512:(ot + 1) * 512],
                      s["xres"][:, qb, ot * 512:(ot + 1) * 512], s["ytmp"][:, :],
                      op=ALU.add, inc=("oev", 1))
    emit_store_xres(p, s, ["oev"])


SEMS_FFN = ["ld", "wld", "dwld", "init", "v", "a", "t", "tr_ev", "cast", "dcast",
            "mmh", "mmd", "hev", "oev", "st"]


def grid_a(w_t, ocn):
    din, dout = w_t.shape
    dcn = din // 128
    assert dout == ocn * 128
    return np.ascontiguousarray(w_t.reshape(dcn, 128, ocn, 128).transpose(0, 2, 1, 3))


_PROGS = {}


def _prog(key, builder, sems):
    if key not in _PROGS:
        _PROGS[key] = mkprog(builder, sems)
    return _PROGS[key]


def kernel(L0_attn_x, L0_attn_w_project, L0_attn_w_out, L0_ffn_router_w,
           L0_ffn_w1, L0_ffn_w2, L0_ffn_w3, L1_attn_Wkv, L1_attn_Wkup,
           L1_attn_Wvup, L1_attn_Wq, L1_attn_Wqr, L1_attn_Wkr, L1_attn_Wo,
           L1_ffn_gate_w, L1_ffn_up_w, L1_ffn_down_w, L1_ffn_gate_s,
           L1_ffn_up_s, L1_ffn_down_s):
    f = np.asarray
    x = f(L0_attn_x, dtype=np.float32)

    ident_f32 = np.eye(128, dtype=np.float32)
    ident_bf = np.eye(128, dtype=np.float32).astype(BF16N)
    a_i = np.arange(128)[:, None]
    b_i = np.arange(128)[None, :]
    tri_incl = np.where(b_i <= a_i, 0.0, NEG).astype(np.float32)
    tri_strict = np.where(b_i < a_i, 0.0, NEG).astype(np.float32)

    xs = [np.ascontiguousarray(x[c // 2, c % 2::2, :]) for c in range(8)]

    wp_grid = grid_a(np.ascontiguousarray(f(L0_attn_w_project, dtype=np.float32).T), 48)
    nc = _prog("l0a", l0a_build, SEMS_L0A)
    maps = [{"xin": xs[c], "wp": wp_grid, "idf": ident_f32} for c in range(8)]
    res = run_bass_kernel_spmd(nc, maps, core_ids=list(range(8)))
    q0 = [r["qout"] for r in res.results]
    kv0 = [r["kvout"] for r in res.results]

    wo0 = np.ascontiguousarray(f(L0_attn_w_out, dtype=np.float32).T)
    nc = _prog("l0b", attn_build_mk(False), SEMS_ATTN)
    maps = []
    for c in range(8):
        r = c % 2
        sib = c ^ 1
        maps.append({
            "xin": xs[c], "wo": wo0, "qin": q0[c],
            "kv_mine": kv0[c], "kv_sib": kv0[sib],
            "m_mine": tri_incl, "m_sib": (tri_strict if r == 0 else tri_incl),
            "idb": ident_bf,
        })
    res = run_bass_kernel_spmd(nc, maps, core_ids=list(range(8)))
    x1 = [r["xout"] for r in res.results]

    rw = np.ascontiguousarray(
        f(L0_ffn_router_w, dtype=np.float32).T.reshape(NDC, 128, 8).transpose(1, 0, 2))
    w1g = np.stack([grid_a(np.ascontiguousarray(f(L0_ffn_w1, dtype=np.float32)[e]), 4)
                    for e in range(NE)])
    w3g = np.stack([grid_a(np.ascontiguousarray(f(L0_ffn_w3, dtype=np.float32)[e]), 4)
                    for e in range(NE)])
    w2s = np.ascontiguousarray(f(L0_ffn_w2, dtype=np.float32))
    nc = _prog("moe", moe_build, SEMS_MOE)
    maps = [{"xin": x1[c], "rw": rw, "w1": w1g, "w3": w3g, "w2": w2s,
             "idf": ident_f32} for c in range(8)]
    res = run_bass_kernel_spmd(nc, maps, core_ids=list(range(8)))
    x2 = [r["xout"] for r in res.results]

    wqg = grid_a(np.ascontiguousarray(f(L1_attn_Wq, dtype=np.float32)), NH)
    swap = np.r_[32:64, 0:32]
    wqr_ = f(L1_attn_Wqr, dtype=np.float32)
    wkr_ = f(L1_attn_Wkr, dtype=np.float32)
    wqr = np.ascontiguousarray(wqr_.reshape(NDC, 128, 64))
    wqrs = np.ascontiguousarray(wqr_[:, swap].reshape(NDC, 128, 64))
    wkvg = grid_a(np.ascontiguousarray(f(L1_attn_Wkv, dtype=np.float32)), 4)
    wkr = np.ascontiguousarray(wkr_.reshape(4, 128, 64))
    wkrs = np.ascontiguousarray(wkr_[:, swap].reshape(4, 128, 64))
    ropetabs = []
    for c in range(8):
        pos = 2 * np.arange(TOK) + (c % 2)
        dr = 64
        inv = 1.0 / 10000.0 ** (np.arange(0, dr, 2, dtype=np.float64) / dr)
        fr = np.outer(pos.astype(np.float64), inv)
        tab = np.zeros((128, TOK), np.float32)
        tab[0:64, :] = np.concatenate([np.cos(fr).T, np.cos(fr).T], 0)
        tab[64:128, :] = np.concatenate([-np.sin(fr).T, np.sin(fr).T], 0)
        ropetabs.append(tab)
    nc = _prog("l1a", l1a_build, SEMS_L1A)
    maps = [{"xin": x2[c], "wq": wqg, "wqr": wqr, "wqrs": wqrs, "wkv": wkvg,
             "wkr": wkr, "wkrs": wkrs, "ropetab": ropetabs[c], "idf": ident_f32}
            for c in range(8)]
    res = run_bass_kernel_spmd(nc, maps, core_ids=list(range(8)))
    q1 = [r["qout"] for r in res.results]
    qr1 = [r["qrout"] for r in res.results]
    ckv1 = [r["ckvout"] for r in res.results]

    wo1 = np.ascontiguousarray(f(L1_attn_Wo, dtype=np.float32))
    wkupg = grid_a(np.ascontiguousarray(f(L1_attn_Wkup, dtype=np.float32)), NH)
    wvup = np.ascontiguousarray(f(L1_attn_Wvup, dtype=np.float32))
    nc = _prog("l1b", attn_build_mk(True), SEMS_ATTN)
    maps = []
    for c in range(8):
        r = c % 2
        sib = c ^ 1
        maps.append({
            "xin": x2[c], "wo": wo1, "qin": q1[c], "qrin": qr1[c],
            "ckv_mine": ckv1[c], "ckv_sib": ckv1[sib],
            "wkup": wkupg, "wvup": wvup,
            "m_mine": tri_incl, "m_sib": (tri_strict if r == 0 else tri_incl),
            "idb": ident_bf,
        })
    res = run_bass_kernel_spmd(nc, maps, core_ids=list(range(8)))
    x3 = [r["xout"] for r in res.results]

    gwg = grid_a(np.ascontiguousarray(f(L1_ffn_gate_w, dtype=np.float32).T), NIC)
    uwg = grid_a(np.ascontiguousarray(f(L1_ffn_up_w, dtype=np.float32).T), NIC)
    dwt = np.ascontiguousarray(f(L1_ffn_down_w, dtype=np.float32).T)
    gs = np.ascontiguousarray(f(L1_ffn_gate_s, dtype=np.float32).reshape(NIC, 128).T)
    us = np.ascontiguousarray(f(L1_ffn_up_s, dtype=np.float32).reshape(NIC, 128).T)
    dsb = np.ascontiguousarray(
        np.broadcast_to(f(L1_ffn_down_s, dtype=np.float32)[None, :], (128, D)))
    nc = _prog("ffn", ffn_build, SEMS_FFN)
    maps = [{"xin": x3[c], "gw": gwg, "uw": uwg, "dw": dwt, "gs": gs, "us": us,
             "dsb": dsb, "idf": ident_f32} for c in range(8)]
    res = run_bass_kernel_spmd(nc, maps, core_ids=list(range(8)))
    x4 = [r["xout"] for r in res.results]

    out = np.empty((B, S, D), np.float32)
    for c in range(8):
        out[c // 2, c % 2::2, :] = x4[c]
    return out


# version 02
# speedup vs baseline: 1.0643x; 1.0643x over previous
import math
import contextlib
import numpy as np
import ml_dtypes
import concourse.bass as bass
import concourse.mybir as mybir
from concourse.bass_utils import run_bass_kernel_spmd

BF16N = ml_dtypes.bfloat16
F32 = mybir.dt.float32
F32R = mybir.dt.float32r
BF16 = mybir.dt.bfloat16
AF = mybir.ActivationFunctionType
ALU = mybir.AluOpType
AX = mybir.AxisListType

B, S, D = 4, 2048, 2048
NH, HD = 16, 128
TOK = 1024
NT = TOK // 128
NDC = D // 128
NEG = -1e9
QS0 = 1.0 / math.sqrt(HD)
QS1 = 1.0 / math.sqrt(HD + 64)
FF, NE = 512, 8
ITER = 5632
NIC = ITER // 128

ALL_ENGS = ("sync", "tensor", "vector", "scalar", "gpsimd")


class P:
    def __init__(self, nc, sems, which, engines):
        self.nc = nc
        self.sems = sems
        self.which = which
        self.engines = engines
        self.cnt = {n: 0 for n in sems}

    def _do(self, eng, fn, inc):
        if self.which == eng:
            inst = fn(self.engines[eng])
            if inc is not None:
                inst.then_inc(self.sems[inc[0]], inc[1])
        if inc is not None:
            self.cnt[inc[0]] += inc[1]

    def wait(self, eng, sem, val=None):
        if val is None:
            val = self.cnt[sem]
        val = max(0, val)
        if self.which == eng:
            self.engines[eng].wait_ge(self.sems[sem], val)

    def barrier(self):
        for e in ALL_ENGS:
            for sem in self.sems:
                if self.cnt[sem] > 0:
                    self.wait(e, sem)

    def dma(self, q, out, in_, inc):
        self._do(q, lambda e: e.dma_start(out=out, in_=in_), inc)

    def dmat(self, q, out, in_, inc):
        self._do(q, lambda e: e.dma_start_transpose(out, in_), inc)

    def mm(self, out, lhsT, rhs, start, stop, inc=None):
        self._do("tensor", lambda e: e.matmul(out, lhsT, rhs, start=start, stop=stop), inc)

    def trans(self, out, in_, ident, inc=None):
        self._do("tensor", lambda e: e.transpose(out, in_, ident), inc)

    def act(self, out, in_, func, bias=0.0, scale=1.0, accum_out=None, inc=None):
        self._do("scalar", lambda e: e.activation(out, in_, func, bias=bias, scale=scale,
                                                  accum_out=accum_out), inc)

    def vop(self, name, *args, inc=None, eng="vector", **kw):
        self._do(eng, lambda e: getattr(e, name)(*args, **kw), inc)

    def collective(self, kind, op, rg, ins, outs, inc):
        self._do("gpsimd", lambda e: e.collective_compute(
            kind, op, replica_groups=rg, ins=ins, outs=outs), inc)


def mkprog(build_fn, sem_names):
    nc = bass.Bass("TRN2", target_bir_lowering=False, debug=False,
                   dynamic_dma_scratch_size=512)
    state = {}
    build_fn(None, nc, state)
    with contextlib.ExitStack() as st:
        sems = {n: st.enter_context(nc.semaphore(n)) for n in sem_names}
        for name, h in state["tensors"]:
            state[name] = st.enter_context(h)
        block = st.enter_context(nc.Block())
        engines = {}

        def reg(name):
            def deco(engobj):
                engines[name] = engobj
                build_fn(P(nc, sems, name, engines), nc, state)
            return deco

        block.sync(reg("sync"))
        block.tensor(reg("tensor"))
        block.vector(reg("vector"))
        block.scalar(reg("scalar"))
        block.gpsimd(reg("gpsimd"))
    return nc


def declare(nc, s, specs):
    s["tensors"] = []
    for kind, name, shape, dt in specs:
        if kind == "dram_in":
            s[name] = nc.dram_tensor(name, shape, dt, kind="ExternalInput").ap()
        elif kind == "dram_out":
            s[name] = nc.dram_tensor(name, shape, dt, kind="ExternalOutput").ap()
        elif kind == "dram":
            s[name + "_t"] = nc.dram_tensor(name, shape, dt)
        elif kind == "sbuf":
            s["tensors"].append((name, nc.sbuf_tensor(name, shape, dt)))
        elif kind == "psum":
            s["tensors"].append((name, nc.psum_tensor(name, shape, dt)))


def view(arena, byte_off, shape, dt):
    assert byte_off % 4 == 0
    esz = {F32: 4, F32R: 4, BF16: 2}[dt]
    n = 1
    for d_ in shape[1:]:
        n *= d_
    f32len = (n * esz + 3) // 4
    ap = arena[:, byte_off // 4: byte_off // 4 + f32len]
    if dt != F32:
        ap = ap.bitcast(dt)
    if len(shape) == 2:
        return ap
    if len(shape) == 3:
        return ap.rearrange("p (a b) -> p a b", b=shape[2])
    if len(shape) == 4:
        return ap.rearrange("p (a b c) -> p a b c", b=shape[2], c=shape[3])
    raise ValueError(shape)


def emit_ln(p, s, h_fm, router=False):
    tr0, tr1 = s["ps0"], s["ps1"]
    for t in range(NT):
        xt = s["xres"][:, t, :]
        p.wait("scalar", "tr_ev")
        p.act(s["htile"][:, :], xt, AF.Square, inc=("a", 1))
        p.vop("reduce_sum", s["c_sum"][:, :], xt, axis=AX.X, inc=("v", 1))
        p.wait("vector", "a")
        p.vop("reduce_sum", s["c_sqs"][:, :], s["htile"][:, :], axis=AX.X, inc=("v", 1))
        p.wait("scalar", "v")
        p.act(s["c_mean"][:, :], s["c_sum"][:, :], AF.Copy, scale=1.0 / D, inc=("a", 1))
        p.act(s["c_msq"][:, :], s["c_sqs"][:, :], AF.Copy, scale=1.0 / D, inc=("a", 1))
        p.wait("vector", "a")
        p.vop("tensor_tensor", s["c_m2"][:, :], s["c_mean"][:, :], s["c_mean"][:, :],
              op=ALU.mult, inc=("v", 1))
        p.wait("vector", "v")
        p.vop("tensor_sub", s["c_var"][:, :], s["c_msq"][:, :], s["c_m2"][:, :], inc=("v", 1))
        p.wait("scalar", "v")
        p.wait("scalar", "init", s["n_init"])
        p.act(s["c_std"][:, :], s["c_var"][:, :], AF.Sqrt, bias=s["c_eps"][:, 0:1], inc=("a", 1))
        p.wait("vector", "a")
        p.vop("reciprocal", s["c_rstd"][:, :], s["c_std"][:, :], inc=("v", 1))
        p.wait("vector", "v")
        p.vop("tensor_tensor", s["c_mr"][:, :], s["c_mean"][:, :], s["c_rstd"][:, :],
              op=ALU.mult, inc=("v", 1))
        p.wait("vector", "v")
        p.vop("tensor_scalar", s["c_nmr"][:, :], s["c_mr"][:, :], s["c_neg1"][:, 0:1], None,
              op0=ALU.mult, inc=("v", 1))
        p.wait("scalar", "v")
        p.act(s["htile"][:, :], xt, AF.Identity, bias=s["c_nmr"][:, 0:1],
              scale=s["c_rstd"][:, 0:1], inc=("a", 1))
        p.wait("tensor", "a")
        for dc in range(NDC):
            ps = (tr0 if dc % 2 == 0 else tr1)[:, 0:128]
            p.wait("tensor", "tr_ev")
            p.trans(ps, s["htile"][:, dc * 128:(dc + 1) * 128], s["id_f32"][:, :],
                    inc=("t", 1))
            p.wait("vector", "t")
            p.vop("tensor_copy", h_fm[:, dc, t * 128:(t + 1) * 128], ps, inc=("tr_ev", 1))
            if router:
                p.wait("vector", "mmr")
                p.vop("tensor_copy", s["hseg"][:, :], ps, inc=("tr_ev", 1))
                p.wait("tensor", "tr_ev")
                p.mm(s["pv"][:, t * 8:(t + 1) * 8], s["hseg"][:, :], s["rwsb"][:, dc, :],
                     start=(dc == 0), stop=(dc == NDC - 1), inc=("mmr", 1))
    p.wait("tensor", "tr_ev")
    p.wait("scalar", "tr_ev")


def emit_proj_grid(p, s, wgrid, h_fm, oc_evacs):
    j = 0
    for oc, evac in oc_evacs:
        for tt in range(2):
            ps = s["ps0"] if j % 2 == 0 else s["ps1"]
            for dc in range(NDC):
                wslot = s["wchunk"][:, dc % 2, :]
                wbslot = s["wchunkb"][:, dc % 2, :]
                p.wait("sync", "pcast")
                p.dma("sync", wslot, wgrid[dc, oc], inc=("wld", 16))
                p.wait("vector", "wld")
                p.wait("vector", "mmw")
                p.vop("tensor_copy", wbslot, wslot, inc=("pcast", 1))
                p.wait("tensor", "pcast")
                if dc == 0:
                    p.wait("tensor", "pevac")
                p.mm(ps[:, :], wbslot, h_fm[:, dc, tt * 512:(tt + 1) * 512],
                     start=(dc == 0), stop=(dc == NDC - 1), inc=("mmw", 1))
            evac(p, s, ps, oc, tt, j)
            j += 1


def evac_stage_dma(p, s, ps, dst, row0, tt, j, scale):
    stg = s["stage"][:, j % 2, :]
    p.wait("scalar", "mmw")
    p.wait("scalar", "stdma")
    p.act(stg, ps[:, :], AF.Copy, scale=scale, inc=("pevac", 1))
    p.dma("scalar", dst[row0:row0 + 128, tt * 512:(tt + 1) * 512], stg, inc=("stdma", 16))


def emit_l0_qkv(p, s):
    def mk(oc):
        def evac(p, s, ps, oc_, tt, j):
            if oc_ < 16:
                evac_stage_dma(p, s, ps, s["q0d"], oc_ * 128, tt, j, QS0)
            else:
                evac_stage_dma(p, s, ps, s["cin0"], (oc_ - 16) * 128, tt, j, 1.0)
        return evac
    emit_proj_grid(p, s, s["wp"], s["h_fm"], [(oc, mk(oc)) for oc in range(48)])
    p.wait("gpsimd", "stdma")


def emit_attn(p, s, l1):
    ps_s = [s["ps0"], s["ps1"], s["ps2"], s["ps3"]]
    if l1:
        for cc in range(4):
            for r in range(2):
                p.dma("sync", s["ckv_sb"][:, cc, r * 1024:(r + 1) * 1024],
                      s["ckvg"][r * 576 + cc * 128: r * 576 + (cc + 1) * 128, :],
                      inc=("ld2", 16))
        for r in range(2):
            p.dma("sync", s["kr_sb"][:, r * 1024:(r + 1) * 1024],
                  s["ckvg"][r * 576 + 512: r * 576 + 576, :], inc=("ld2", 16))
        p.dma("sync", s["qr_sb"][:, :], s["qr1d"], inc=("ld2", 16))
        p.wait("tensor", "ld2")

    for h in range(NH):
        p.wait("sync", "pvev")
        p.dma("sync", s["q_sb"][:, :], s["qin"][h * 128:(h + 1) * 128, :],
              inc=("kvld", 16))
        if not l1:
            for r in range(2):
                p.dma("sync", s["k_sb"][:, r * 1024:(r + 1) * 1024],
                      s["kvg"][r * 4096 + h * 128: r * 4096 + (h + 1) * 128, :],
                      inc=("kvld", 16))
                for cb in range(8):
                    p.dmat("sync", s["v_sb"][:, r * 8 + cb, :],
                           s["kvg"][r * 4096 + 2048 + h * 128: r * 4096 + 2048 + (h + 1) * 128,
                                    cb * 128:(cb + 1) * 128], inc=("kvld", 16))
            p.wait("tensor", "kvld")
        else:
            if h % 4 == 0:
                g = h // 4
                p.dma("sync", s["wvuf"][:, :, :],
                      s["wvup"].rearrange("(c p) o -> p c o", p=128)
                      [:, :, g * 512:(g + 1) * 512], inc=("vwld", 16))
                p.wait("vector", "vwld")
                p.wait("vector", "mmv")
                p.vop("tensor_copy", s["wvub"][:, :, :], s["wvuf"][:, :, :], inc=("vcast", 1))
                p.wait("tensor", "vcast")
                p.wait("tensor", "sfree")
                for tc in range(16):
                    psk = ps_s[tc % 4]
                    p.wait("tensor", "vevac")
                    for cc in range(4):
                        p.mm(psk[:, :], s["ckv_sb"][:, cc, tc * 128:(tc + 1) * 128],
                             s["wvub"][:, cc, :], start=(cc == 0), stop=(cc == 3),
                             inc=("mmv", 1) if cc == 3 else None)
                    p.wait("vector", "mmv")
                    p.vop("tensor_copy", s["v4"][:, tc, :], psk[:, :], inc=("vevac", 1))
            p.wait("tensor", "sfree")
            p.wait("tensor", "vevac")
            for cc in range(4):
                wslot = s["wkuf"][:, cc % 2, :]
                wbslot = s["wkub"][:, cc % 2, :]
                p.wait("sync", "kcast")
                p.dma("sync", wslot, s["wkup"][cc, h], inc=("kwld", 16))
                p.wait("vector", "kwld")
                p.wait("vector", "mmk")
                p.vop("tensor_copy", wbslot, wslot, inc=("kcast", 1))
                p.wait("tensor", "kcast")
                for kt in range(4):
                    if cc == 0:
                        p.wait("tensor", "kevac")
                    p.mm(ps_s[kt][:, :], wbslot,
                         s["ckv_sb"][:, cc, kt * 512:(kt + 1) * 512],
                         start=(cc == 0), stop=(cc == 3), inc=("mmk", 1))
            p.wait("vector", "mmk")
            for kt in range(4):
                p.vop("tensor_copy", s["k_sb"][:, kt * 512:(kt + 1) * 512],
                      ps_s[kt][:, :], inc=("kevac", 1))
            p.wait("tensor", "kevac")
            p.wait("tensor", "kvld")

        for qb in range(NT):
            nb = qb + 1
            L = 2 * nb * 128
            nkt = (L + 511) // 512
            qlh = s["q_sb"][:, qb * 128:(qb + 1) * 128]
            p.wait("tensor", "sfree")
            segs = []
            for run, base in ((0, 0), (1, 1024)):
                c0 = run * nb * 128
                done = 0
                while done < nb * 128:
                    col = c0 + done
                    w = min(512 - (col % 512), nb * 128 - done)
                    segs.append((col, w, base + done))
                    done += w
            tile_written = [False] * 4
            for (col, w, kcol) in segs:
                kt = col // 512
                st0 = not tile_written[kt]
                tile_written[kt] = True
                if l1:
                    p.mm(ps_s[kt][:, col % 512:col % 512 + w], qlh,
                         s["k_sb"][:, kcol:kcol + w], start=st0, stop=False)
                    p.mm(ps_s[kt][:, col % 512:col % 512 + w],
                         s["qr_sb"][:, qb * 128:(qb + 1) * 128],
                         s["kr_sb"][:, kcol:kcol + w], start=False, stop=True,
                         inc=("ms", 1))
                else:
                    p.mm(ps_s[kt][:, col % 512:col % 512 + w], qlh,
                         s["k_sb"][:, kcol:kcol + w], start=st0, stop=True,
                         inc=("ms", 1))
            p.wait("vector", "ms")
            for run in range(2):
                col = (run * nb + nb - 1) * 128
                kt, off = col // 512, col % 512
                msk = s["mask0"] if run == 0 else s["mask1"]
                p.vop("tensor_tensor", ps_s[kt][:, off:off + 128],
                      ps_s[kt][:, off:off + 128], msk[:, :], op=ALU.add, inc=("v", 1))
            p.wait("vector", "v")
            for kt in range(nkt):
                w = min(512, L - kt * 512)
                p.vop("reduce_max", s["nmax"][:, kt:kt + 1], ps_s[kt][:, 0:w],
                      axis=AX.X, negate=True, inc=("v", 1))
            p.wait("vector", "v")
            if nkt > 1:
                p.vop("tensor_reduce", s["nmaxc"][:, :], s["nmax"][:, 0:nkt],
                      axis=AX.X, op=ALU.min, inc=("v", 1))
            else:
                p.vop("tensor_copy", s["nmaxc"][:, :], s["nmax"][:, 0:1], inc=("v", 1))
            p.wait("scalar", "v")
            p.wait("scalar", "t")
            for kt in range(nkt):
                w = min(512, L - kt * 512)
                p.act(s["attn"][:, kt * 512:kt * 512 + w], ps_s[kt][:, 0:w], AF.Exp,
                      bias=s["nmaxc"][:, 0:1], accum_out=s["rsums"][:, kt:kt + 1],
                      inc=("sfree", 1))
            p.wait("vector", "sfree")
            if nkt > 1:
                p.vop("reduce_sum", s["rsum"][:, :], s["rsums"][:, 0:nkt], axis=AX.X,
                      inc=("v", 1))
            else:
                p.vop("tensor_copy", s["rsum"][:, :], s["rsums"][:, 0:1], inc=("v", 1))
            p.wait("vector", "v")
            p.vop("reciprocal", s["rcp"][:, :], s["rsum"][:, :], inc=("v", 1))
            p.wait("vector", "v")
            p.vop("tensor_scalar", s["attn"][:, 0:L], s["attn"][:, 0:L],
                  s["rcp"][:, 0:1], None, op0=ALU.mult, inc=("v", 1))
            p.wait("tensor", "v")
            nkb = 2 * nb
            for kb in range(nkb):
                bank = s["pb0"] if (kb // 8) % 2 == 0 else s["pb1"]
                slot = kb % 8
                if slot == 0:
                    p.wait("tensor", "atev")
                p.trans(bank[:, slot * 128:(slot + 1) * 128],
                        s["attn"][:, kb * 128:(kb + 1) * 128], s["id_bf"][:, :],
                        inc=("t", 1))
                if slot == 7 or kb == nkb - 1:
                    p.wait("vector", "t")
                    n = slot + 1
                    p.vop("tensor_copy", s["attnT"][:, (kb - n + 1) * 128:(kb + 1) * 128],
                          bank[:, 0:n * 128], inc=("atev", 1))
            p.wait("tensor", "atev")
            p.wait("tensor", "pvev")
            for kb in range(nkb):
                run, jj = (0, kb) if kb < nb else (1, kb - nb)
                if l1:
                    vt = s["v4"][:, run * 8 + jj, (h % 4) * 128:(h % 4 + 1) * 128]
                else:
                    vt = s["v_sb"][:, run * 8 + jj, :]
                p.mm(s["pv"][:, :], vt, s["attnT"][:, kb * 128:(kb + 1) * 128],
                     start=(kb == 0), stop=(kb == nkb - 1),
                     inc=("mpv", 1) if kb == nkb - 1 else None)
            p.wait("vector", "mpv")
            p.vop("tensor_copy", s["aout_fm"][:, h, qb * 128:(qb + 1) * 128],
                  s["pv"][:, :], inc=("pvev", 1))

    p.wait("tensor", "pvev")
    p.wait("tensor", "sfree")
    wo = s["wo1"] if l1 else s["wo0"]
    for ot in range(4):
        for qg in range(2):
            for dc in range(NDC):
                wslot = s["wochunk"][:, dc % 2, :]
                wbslot = s["wob"][:, dc % 2, :]
                p.wait("sync", "wcast")
                p.dma("sync", wslot,
                      wo[dc * 128:(dc + 1) * 128, ot * 512:(ot + 1) * 512],
                      inc=("owld", 16))
                p.wait("vector", "owld")
                p.wait("vector", "mmo")
                p.vop("tensor_copy", wbslot, wslot, inc=("wcast", 1))
                p.wait("tensor", "wcast")
                if dc == 0:
                    p.wait("tensor", "oev")
                for j in range(4):
                    qb = qg * 4 + j
                    p.mm(ps_s[j][:, :],
                         s["aout_fm"][:, dc, qb * 128:(qb + 1) * 128], wbslot,
                         start=(dc == 0), stop=(dc == NDC - 1), inc=("mmo", 1))
            p.wait("vector", "mmo")
            for j in range(4):
                qb = qg * 4 + j
                p.vop("tensor_tensor", s["xres"][:, qb, ot * 512:(ot + 1) * 512],
                      s["xres"][:, qb, ot * 512:(ot + 1) * 512], ps_s[j][:, :],
                      op=ALU.add, inc=("oev", 1))
    p.wait("tensor", "oev")


def emit_moe(p, s):
    p.wait("vector", "mmr")
    p.vop("tensor_copy", s["sc"][:, :, :].rearrange("p a b -> p (a b)"), s["pv"][:, 0:64],
          inc=("v", 1))
    p.wait("vector", "v")
    for t in range(NT):
        st = s["sc"][:, t, :]
        p.vop("reduce_max", s["c_m1"][:, :], st, axis=AX.X, inc=("v", 1))
        p.vop("reduce_max", s["c_nm1"][:, :], st, axis=AX.X, negate=True, inc=("v", 1))
        p.wait("vector", "v")
        p.vop("tensor_scalar", s["t_eq"][:, :], st, s["c_m1"][:, 0:1], None,
              op0=ALU.is_ge, inc=("v", 1))
        p.wait("vector", "v")
        p.vop("scalar_tensor_tensor", s["t_t"][:, :], s["t_eq"][:, :], s["c_neg9"][:, 0:1],
              st, op0=ALU.mult, op1=ALU.add, inc=("v", 1))
        p.wait("vector", "v")
        p.vop("reduce_max", s["c_m2r"][:, :], s["t_t"][:, :], axis=AX.X, inc=("v", 1))
        p.wait("scalar", "v")
        p.act(s["t_e1"][:, :], st, AF.Exp, bias=s["c_nm1"][:, 0:1], inc=("a", 1))
        p.act(s["c_em"][:, :], s["c_m2r"][:, :], AF.Exp, bias=s["c_nm1"][:, 0:1], inc=("a", 1))
        p.wait("vector", "a")
        p.vop("tensor_tensor", s["c_den"][:, :], s["c_em"][:, :], s["c_one"][:, 0:1],
              op=ALU.add, inc=("v", 1))
        p.wait("vector", "v")
        p.vop("reciprocal", s["c_rden"][:, :], s["c_den"][:, :], inc=("v", 1))
        p.wait("vector", "v")
        p.vop("tensor_scalar", s["t_ge"][:, :], st, s["c_m2r"][:, 0:1], None,
              op0=ALU.is_ge, inc=("v", 1))
        p.wait("vector", "v")
        p.vop("tensor_tensor", s["t_e1"][:, :], s["t_e1"][:, :], s["t_ge"][:, :],
              op=ALU.mult, inc=("v", 1))
        p.wait("vector", "v")
        p.vop("tensor_scalar", s["gates"][:, t, :], s["t_e1"][:, :], s["c_rden"][:, 0:1],
              None, op0=ALU.mult, inc=("v", 1))
    p.wait("tensor", "v")

    for e in range(NE):
        for ffq in range(4):
            for w_i, nm in ((0, "w1"), (1, "w3")):
                p.wait("sync", "pcast")
                p.dma("sync", s["w13f"][:, :, :],
                      s[nm][e].rearrange("c q p f -> p c q f")[:, :, ffq, :],
                      inc=("wld", 16))
                p.wait("vector", "wld")
                p.wait("vector", "mmw")
                p.vop("tensor_copy", s["w13b"][:, w_i, :, :], s["w13f"][:, :, :],
                      inc=("pcast", 1))
            p.wait("tensor", "pcast")
            for tt in range(2):
                p.wait("tensor", "hev")
                for w_i in range(2):
                    ps = s["ps%d" % (2 + w_i)]
                    for dc in range(NDC):
                        p.mm(ps[:, :], s["w13b"][:, w_i, dc, :],
                             s["h_fm"][:, dc, tt * 512:(tt + 1) * 512],
                             start=(dc == 0), stop=(dc == NDC - 1), inc=("mmw", 1))
                p.wait("scalar", "mmw")
                p.wait("scalar", "hev")
                p.act(s["sg"][:, :], s["ps2"][:, :], AF.Silu, inc=("a", 1))
                p.wait("vector", "a")
                p.vop("tensor_tensor", s["hid"][:, ffq, tt * 512:(tt + 1) * 512],
                      s["sg"][:, :], s["ps3"][:, :], op=ALU.mult, inc=("hev", 1))
        p.wait("tensor", "hev")
        for ot in range(4):
            p.wait("sync", "pcast")
            p.dma("sync", s["w2otf"][:, :, :],
                  s["w2"][e].rearrange("(c p) o -> p c o", p=128)[:, :, ot * 512:(ot + 1) * 512],
                  inc=("wld", 16))
            p.wait("vector", "wld")
            p.wait("vector", "mmw")
            p.vop("tensor_copy", s["w2otb"][:, :, :], s["w2otf"][:, :, :], inc=("pcast", 1))
            p.wait("tensor", "pcast")
            for tq in range(NT):
                ps = s["ps%d" % (tq % 2)]
                p.wait("tensor", "oev")
                for ffc in range(4):
                    p.mm(ps[:, :], s["hid"][:, ffc, tq * 128:(tq + 1) * 128],
                         s["w2otb"][:, ffc, :], start=(ffc == 0), stop=(ffc == 3),
                         inc=("mmw", 1) if ffc == 3 else None)
                p.wait("vector", "mmw")
                p.vop("scalar_tensor_tensor", s["xres"][:, tq, ot * 512:(ot + 1) * 512],
                      ps[:, :], s["gates"][:, tq, e:e + 1],
                      s["xres"][:, tq, ot * 512:(ot + 1) * 512],
                      op0=ALU.mult, op1=ALU.add, inc=("oev", 1))
    p.wait("tensor", "oev")


def emit_l1_proj(p, s):
    def mkq(oc):
        def evac(p, s, ps, oc_, tt, j):
            evac_stage_dma(p, s, ps, s["q1d"], oc_ * 128, tt, j, QS1)
        return evac
    emit_proj_grid(p, s, s["wq"], s["h_fm"], [(oc, mkq(oc)) for oc in range(NH)])

    def mkc(oc):
        def evac(p, s, ps, oc_, tt, j):
            p.wait("vector", "mmw")
            p.vop("tensor_copy", s["ckv_sb1"][:, oc_, tt * 512:(tt + 1) * 512], ps[:, :],
                  inc=("pevac", 1))
            p.wait("scalar", "pevac")
            p.dma("scalar", s["cin1"][oc_ * 128:(oc_ + 1) * 128, tt * 512:(tt + 1) * 512],
                  s["ckv_sb1"][:, oc_, tt * 512:(tt + 1) * 512], inc=("stdma", 16))
        return evac
    emit_proj_grid(p, s, s["wkv"], s["h_fm"], [(oc, mkc(oc)) for oc in range(4)])

    for which in ("qr", "kr"):
        ps = s["ps2"]
        for swapped, dst in ((False, "rraw"), (True, "rrsw")):
            for tt in range(2):
                nchunk = NDC if which == "qr" else 4
                for cc in range(nchunk):
                    if which == "qr":
                        wsrc = "wqrs" if swapped else "wqr"
                        wslot = s["wchunk"][:, cc % 2, :]
                        wbslot = s["wchunkb"][:, cc % 2, :]
                        p.wait("sync", "pcast")
                        p.dma("sync", wslot[:, 0:64], s[wsrc][cc], inc=("wld", 16))
                        p.wait("vector", "wld")
                        p.wait("vector", "mmw")
                        p.vop("tensor_copy", wbslot[:, 0:64], wslot[:, 0:64],
                              inc=("pcast", 1))
                        p.wait("tensor", "pcast")
                        lhsT = wbslot[:, 0:64]
                        rhs = s["h_fm"][:, cc, tt * 512:(tt + 1) * 512]
                    else:
                        wsrc = "wkrs" if swapped else "wkr"
                        wslot = s["wkuf"][:, cc % 2, 0:64]
                        wbslot = s["wkub"][:, cc % 2, 0:64]
                        p.wait("sync", "kcast")
                        p.dma("sync", wslot, s[wsrc][cc], inc=("kwld", 16))
                        p.wait("vector", "kwld")
                        p.wait("vector", "mmw")
                        p.vop("tensor_copy", wbslot, wslot, inc=("kcast", 1))
                        p.wait("tensor", "kcast")
                        lhsT = wbslot
                        rhs = s["ckv_sb1"][:, cc, tt * 512:(tt + 1) * 512]
                    if cc == 0:
                        p.wait("tensor", "rev")
                    p.mm(ps[0:64, 0:512], lhsT, rhs,
                         start=(cc == 0), stop=(cc == nchunk - 1), inc=("mmw", 1))
                p.wait("vector", "mmw")
                p.vop("tensor_copy", s[dst][:, tt * 512:(tt + 1) * 512], ps[0:64, 0:512],
                      inc=("rev", 1))
        p.wait("vector", "rev")
        p.vop("tensor_tensor", s["rt1"][:, :], s["rraw"][:, :], s["ctab"][:, :],
              op=ALU.mult, inc=("v", 1))
        p.vop("tensor_tensor", s["roped"][:, :], s["rrsw"][:, :], s["sstab"][:, :],
              op=ALU.mult, inc=("v", 1))
        p.wait("vector", "v")
        p.vop("tensor_tensor", s["roped"][:, :], s["roped"][:, :], s["rt1"][:, :],
              op=ALU.add, inc=("v", 1))
        p.wait("vector", "v")
        p.wait("vector", "init", s["n_init"])
        if which == "qr":
            p.vop("tensor_scalar", s["ropedb"][:, :], s["roped"][:, :], s["c_qs1"][0:64, 0:1],
                  None, op0=ALU.mult, inc=("v", 1))
            p.wait("scalar", "v")
            p.dma("scalar", s["qr1d"], s["ropedb"][:, :], inc=("stdma", 16))
        else:
            p.vop("tensor_copy", s["ropedb"][:, :], s["roped"][:, :], inc=("v", 1))
            p.wait("scalar", "v")
            p.dma("scalar", s["cin1"][512:576, :], s["ropedb"][:, :], inc=("stdma", 16))
    p.wait("gpsimd", "stdma")


def emit_ffn(p, s):
    p.dma("sync", s["gsb"][:, :], s["gs"], inc=("ld2", 16))
    p.dma("sync", s["usb"][:, :], s["us"], inc=("ld2", 16))
    p.dma("sync", s["dsb_sb"][:, :], s["dsb"], inc=("ld2", 16))
    for e in ("vector", "scalar", "tensor"):
        p.wait(e, "ld2")
    for th in range(2):
        t0 = th * 512
        for ic in range(NIC):
            for w_i, nm in ((0, "gw"), (1, "uw")):
                p.wait("sync", "cast")
                p.dma("sync", s["wf"][:, :, :],
                      s[nm].rearrange("c i p f -> p c i f")[:, :, ic, :], inc=("wld2", 16))
                p.wait("vector", "wld2")
                p.wait("vector", "mmh")
                p.vop("tensor_copy", s["wgb" if w_i == 0 else "wub"][:, :, :],
                      s["wf"][:, :, :], inc=("cast", 1))
            p.wait("tensor", "cast")
            p.wait("tensor", "hev")
            for w_i in range(2):
                ps = s["ps%d" % (2 + w_i)]
                wb = s["wgb"] if w_i == 0 else s["wub"]
                for dc in range(NDC):
                    p.mm(ps[:, :], wb[:, dc, :], s["h_bf"][:, dc, t0:t0 + 512],
                         start=(dc == 0), stop=(dc == NDC - 1), inc=("mmh", 1))
            p.wait("scalar", "mmh")
            p.wait("scalar", "hev")
            p.act(s["sg"][:, :], s["ps2"][:, :], AF.Silu, scale=s["gsb"][:, ic:ic + 1],
                  inc=("a", 1))
            p.wait("vector", "a")
            p.vop("scalar_tensor_tensor", s["hid1"][:, ic, :], s["ps3"][:, :],
                  s["usb"][:, ic:ic + 1], s["sg"][:, :], op0=ALU.mult, op1=ALU.mult,
                  inc=("hev", 1))
        p.wait("tensor", "hev")
        for ot in range(4):
            for ic in range(NIC):
                p.wait("sync", "dcast")
                p.dma("sync", s["dwf"][:, :],
                      s["dw"][ic * 128:(ic + 1) * 128, ot * 512:(ot + 1) * 512],
                      inc=("dwld", 16))
                p.wait("vector", "dwld")
                p.wait("vector", "mmd")
                p.vop("tensor_copy", s["dwb"][:, ic % 2, :], s["dwf"][:, :], inc=("dcast", 1))
                p.wait("tensor", "dcast")
                if ic == 0:
                    p.wait("tensor", "oev")
                for tq in range(NT // 2):
                    p.mm(s["ps%d" % tq][:, :], s["hid1"][:, ic, tq * 128:(tq + 1) * 128],
                         s["dwb"][:, ic % 2, :], start=(ic == 0), stop=(ic == NIC - 1),
                         inc=("mmd", 1))
            p.wait("vector", "mmd")
            for tq in range(NT // 2):
                qb = th * 4 + tq
                p.vop("tensor_tensor", s["ytmp"][:, :], s["ps%d" % tq][:, :],
                      s["dsb_sb"][:, ot * 512:(ot + 1) * 512], op=ALU.mult, inc=("v", 1))
                p.wait("vector", "v")
                p.vop("tensor_tensor", s["xres"][:, qb, ot * 512:(ot + 1) * 512],
                      s["xres"][:, qb, ot * 512:(ot + 1) * 512], s["ytmp"][:, :],
                      op=ALU.add, inc=("oev", 1))
    p.wait("tensor", "oev")


FUSED_SEMS = ["ld", "ld2", "wld", "wld2", "dwld", "kvld", "kwld", "vwld", "owld",
              "pcast", "mmr",
              "init", "v", "a", "t", "tr_ev", "ms", "mmw", "mmh", "mmd", "mmk",
              "mmv", "mmo", "mpv", "sfree", "atev", "pvev", "kevac", "vevac",
              "hev", "oev", "pevac", "rev", "stdma", "st", "cast", "dcast",
              "kcast", "vcast", "wcast", "cc0", "cc1"]

RG_PAIRS = [[0, 1], [2, 3], [4, 5], [6, 7]]


def fused_build_mk(upto):
    def build(p, nc, s):
        if p is None:
            declare(nc, s, [
                ("dram_in", "xin", [TOK, D], F32),
                ("dram_in", "wp", [NDC, 48, 128, 128], F32),
                ("dram_in", "wo0", [D, D], F32),
                ("dram_in", "rw", [128, NDC, 8], F32),
                ("dram_in", "w1", [NE, NDC, 4, 128, 128], F32),
                ("dram_in", "w3", [NE, NDC, 4, 128, 128], F32),
                ("dram_in", "w2", [NE, FF, D], F32),
                ("dram_in", "wq", [NDC, NH, 128, 128], F32),
                ("dram_in", "wqr", [NDC, 128, 64], F32),
                ("dram_in", "wqrs", [NDC, 128, 64], F32),
                ("dram_in", "wkv", [NDC, 4, 128, 128], F32),
                ("dram_in", "wkr", [4, 128, 64], F32),
                ("dram_in", "wkrs", [4, 128, 64], F32),
                ("dram_in", "wkup", [4, NH, 128, 128], F32),
                ("dram_in", "wvup", [512, D], F32),
                ("dram_in", "wo1", [D, D], F32),
                ("dram_in", "gw", [NDC, NIC, 128, 128], F32),
                ("dram_in", "uw", [NDC, NIC, 128, 128], F32),
                ("dram_in", "dw", [ITER, D], F32),
                ("dram_in", "gs", [128, NIC], F32),
                ("dram_in", "us", [128, NIC], F32),
                ("dram_in", "dsb", [128, D], F32),
                ("dram_in", "ropetab", [128, TOK], F32),
                ("dram_in", "m_r0", [128, 128], F32),
                ("dram_in", "m_r1", [128, 128], F32),
                ("dram_in", "idf", [128, 128], F32),
                ("dram_in", "idb", [128, 128], BF16),
                ("dram_out", "xout", [TOK, D], F32),
                ("dram", "q0d", [D, TOK], BF16),
                ("dram", "q1d", [D, TOK], BF16),
                ("dram", "qr1d", [64, TOK], BF16),
                ("dram", "cin0", [2 * D, TOK], BF16),
                ("dram", "cout0", [4 * D, TOK], BF16),
                ("dram", "cin1", [576, TOK], BF16),
                ("dram", "cout1", [1152, TOK], BF16),
                ("sbuf", "xres", [128, NT, D], F32),
                ("sbuf", "A", [128, 16384], F32),
                ("sbuf", "C", [128, 12288], F32),
                ("sbuf", "W", [128, 7168], F32),
                ("sbuf", "stage", [128, 2, 512], BF16),
                ("sbuf", "wchunk", [128, 2, 128], F32),
                ("sbuf", "wchunkb", [128, 2, 128], BF16),
                ("sbuf", "hseg", [128, 128], F32),
                ("sbuf", "id_f32", [128, 128], F32),
                ("sbuf", "id_bf", [128, 128], BF16),
                ("sbuf", "mask0", [128, 128], F32),
                ("sbuf", "mask1", [128, 128], F32),
                ("sbuf", "rwsb", [128, NDC, 8], F32),
                ("sbuf", "gsb", [128, NIC], F32),
                ("sbuf", "usb", [128, NIC], F32),
                ("sbuf", "gates", [128, NT, NE], F32),
                ("sbuf", "sc", [128, NT, NE], F32),
                ("sbuf", "nmax", [128, 4], F32),
                ("sbuf", "nmaxc", [128, 1], F32),
                ("sbuf", "rsums", [128, 4], F32),
                ("sbuf", "rsum", [128, 1], F32),
                ("sbuf", "rcp", [128, 1], F32),
                ("sbuf", "t_eq", [128, NE], F32),
                ("sbuf", "t_t", [128, NE], F32),
                ("sbuf", "t_e1", [128, NE], F32),
                ("sbuf", "t_ge", [128, NE], F32),
            ] + [("sbuf", n, [128, 1], F32) for n in
                 ["c_sum", "c_sqs", "c_mean", "c_msq", "c_m2", "c_var", "c_std",
                  "c_rstd", "c_mr", "c_nmr", "c_eps", "c_neg1", "c_neg9", "c_one",
                  "c_qs1", "c_m1", "c_nm1", "c_m2r", "c_em", "c_den", "c_rden"]]
              + [("psum", "ps%d" % i, [128, 512], F32) for i in range(4)]
              + [("psum", "pb0", [128, 1024], BF16), ("psum", "pb1", [128, 1024], BF16),
                 ("psum", "pv", [128, 128], F32)])
            return

        A, C, W = s["A"], s["C"], s["W"]
        s["h_fm"] = view(A, 0, [128, NDC, TOK], BF16)
        s["h_bf"] = s["h_fm"]
        s["q_sb"] = view(A, 32768, [128, TOK], BF16)
        s["k_sb"] = view(A, 34816, [128, 2048], BF16)
        s["attn"] = view(A, 38912, [128, 2048], BF16)
        s["attnT"] = view(A, 43008, [128, 2048], BF16)
        s["qr_sb"] = view(A, 47104, [128, TOK], BF16)[0:64, :]
        s["aout_fm"] = view(A, 0, [128, NDC, TOK], BF16)
        s["kr_sb"] = view(A, 49152, [128, 2048], BF16)[0:64, :]
        s["htile"] = view(C, 0, [128, D], F32)
        s["rraw"] = view(C, 24576, [128, TOK], F32)[0:64, :]
        s["rrsw"] = view(C, 28672, [128, TOK], F32)[0:64, :]
        s["rt1"] = view(C, 32768, [128, TOK], F32)[0:64, :]
        s["roped"] = view(C, 36864, [128, TOK], F32)[0:64, :]
        s["ropedb"] = view(C, 40960, [128, TOK], BF16)[0:64, :]
        s["ctab"] = view(C, 8192, [128, TOK], F32)[0:64, :]
        s["sstab"] = view(C, 12288, [128, TOK], F32)[0:64, :]
        s["hid"] = view(C, 24576, [128, 4, TOK], BF16)
        s["ckv_sb"] = view(C, 0, [128, 4, 2048], BF16)
        s["ckv_sb1"] = view(C, 16384, [128, 4, TOK], BF16)
        s["v_sb"] = view(C, 16384, [128, 16, 128], BF16)
        s["v4"] = view(C, 16384, [128, 16, 512], BF16)
        s["hid1"] = view(C, 0, [128, NIC, 512], BF16)
        s["sg"] = view(C, 45056, [128, 512], F32)
        s["ytmp"] = view(C, 47104, [128, 512], F32)
        s["w13f"] = view(W, 0, [128, NDC, 128], F32)
        s["w13b"] = view(W, 8192, [128, 2, NDC, 128], BF16)
        s["w2otf"] = view(W, 16384, [128, 4, 512], F32)
        s["w2otb"] = view(W, 24576, [128, 4, 512], BF16)
        s["wkuf"] = view(W, 0, [128, 2, 128], F32)
        s["wkub"] = view(W, 1024, [128, 2, 128], BF16)
        s["wvuf"] = view(W, 2048, [128, 4, 512], F32)
        s["wvub"] = view(W, 10240, [128, 4, 512], BF16)
        s["wochunk"] = view(W, 16384, [128, 2, 512], F32)
        s["wob"] = view(W, 20480, [128, 2, 512], BF16)
        s["wf"] = view(W, 0, [128, NDC, 128], F32)
        s["wgb"] = view(W, 8192, [128, NDC, 128], BF16)
        s["wub"] = view(W, 12288, [128, NDC, 128], BF16)
        s["dwf"] = view(W, 16384, [128, 512], F32)
        s["dwb"] = view(W, 18432, [128, 2, 512], BF16)
        s["dsb_sb"] = view(W, 20480, [128, D], F32)
        for nm in ("q0d", "q1d", "qr1d", "cin0", "cin1"):
            s[nm] = s[nm + "_t"].ap()

        for nm, val in (("c_eps", 1e-5), ("c_neg1", -1.0), ("c_neg9", NEG),
                        ("c_one", 1.0), ("c_qs1", QS1)):
            p.vop("memset", s[nm][:, :], val, inc=("init", 1), eng="gpsimd")
        s["n_init"] = 5
        p.dma("sync", s["id_f32"][:, :], s["idf"], inc=("ld", 16))
        p.dma("sync", s["id_bf"][:, :], s["idb"], inc=("ld", 16))
        p.dma("sync", s["mask0"][:, :], s["m_r0"], inc=("ld", 16))
        p.dma("sync", s["mask1"][:, :], s["m_r1"], inc=("ld", 16))
        p.dma("sync", s["xres"][:, :, :], s["xin"].rearrange("(t p) d -> p t d", p=128),
              inc=("ld", 16))
        for e in ALL_ENGS:
            p.wait(e, "ld")
            p.wait(e, "init", s["n_init"])

        emit_ln(p, s, s["h_fm"])
        emit_l0_qkv(p, s)
        p.collective("AllGather", ALU.bypass, RG_PAIRS,
                     [s["cin0_t"].ap().opt()], [s["cout0_t"].ap().opt()], inc=("cc0", 1))
        p.wait("gpsimd", "cc0")
        p.barrier()
        s["kvg"] = s["cout0_t"].ap()
        s["qin"] = s["q0d"]
        emit_attn(p, s, l1=False)
        p.barrier()
        if upto >= 2:
            p.dma("sync", s["rwsb"][:, :, :], s["rw"], inc=("ld2", 16))
            p.wait("tensor", "ld2")
            p.wait("vector", "ld2")
            emit_ln(p, s, s["h_fm"], router=True)
            emit_moe(p, s)
            p.barrier()
        if upto >= 3:
            emit_ln(p, s, s["h_fm"])
            p.dma("sync", s["ctab"][:, :], s["ropetab"][0:64, :], inc=("ld2", 16))
            p.dma("sync", s["sstab"][:, :], s["ropetab"][64:128, :], inc=("ld2", 16))
            p.wait("vector", "ld2")
            emit_l1_proj(p, s)
            p.collective("AllGather", ALU.bypass, RG_PAIRS,
                         [s["cin1_t"].ap().opt()], [s["cout1_t"].ap().opt()], inc=("cc1", 1))
            p.wait("gpsimd", "cc1")
            p.barrier()
            s["ckvg"] = s["cout1_t"].ap()
            s["qin"] = s["q1d"]
            emit_attn(p, s, l1=True)
            p.barrier()
        if upto >= 4:
            emit_ln(p, s, s["h_bf"])
            emit_ffn(p, s)
            p.barrier()
        p.wait("scalar", "oev")
        p.dma("scalar", s["xout"].rearrange("(t p) d -> p t d", p=128), s["xres"][:, :, :],
              inc=("st", 16))
        p.wait("scalar", "st")
    return build


def grid_a(w_t, ocn):
    din, dout = w_t.shape
    dcn = din // 128
    assert dout == ocn * 128
    return np.ascontiguousarray(w_t.reshape(dcn, 128, ocn, 128).transpose(0, 2, 1, 3))


_PROGS = {}


def _prog(key, builder, sems):
    if key not in _PROGS:
        _PROGS[key] = mkprog(builder, sems)
    return _PROGS[key]


def stage_inputs(inputs):
    f = np.asarray
    x = f(inputs["L0_attn_x"], dtype=np.float32)
    ident_f32 = np.eye(128, dtype=np.float32)
    ident_bf = np.eye(128, dtype=np.float32).astype(BF16N)
    a_i = np.arange(128)[:, None]
    b_i = np.arange(128)[None, :]
    tri_incl = np.where(b_i <= a_i, 0.0, NEG).astype(np.float32)
    tri_strict = np.where(b_i < a_i, 0.0, NEG).astype(np.float32)

    xs = [np.ascontiguousarray(x[c // 2, c % 2::2, :]) for c in range(8)]
    wp_grid = grid_a(np.ascontiguousarray(f(inputs["L0_attn_w_project"], dtype=np.float32).T), 48)
    wo0 = np.ascontiguousarray(f(inputs["L0_attn_w_out"], dtype=np.float32).T)
    rw = np.ascontiguousarray(
        f(inputs["L0_ffn_router_w"], dtype=np.float32).T.reshape(NDC, 128, 8).transpose(1, 0, 2))
    w1g = np.stack([grid_a(np.ascontiguousarray(f(inputs["L0_ffn_w1"], dtype=np.float32)[e]), 4)
                    for e in range(NE)])
    w3g = np.stack([grid_a(np.ascontiguousarray(f(inputs["L0_ffn_w3"], dtype=np.float32)[e]), 4)
                    for e in range(NE)])
    w2s = np.ascontiguousarray(f(inputs["L0_ffn_w2"], dtype=np.float32))
    wqg = grid_a(np.ascontiguousarray(f(inputs["L1_attn_Wq"], dtype=np.float32)), NH)
    swap = np.r_[32:64, 0:32]
    wqr_ = f(inputs["L1_attn_Wqr"], dtype=np.float32)
    wkr_ = f(inputs["L1_attn_Wkr"], dtype=np.float32)
    wqr = np.ascontiguousarray(wqr_.reshape(NDC, 128, 64))
    wqrs = np.ascontiguousarray(wqr_[:, swap].reshape(NDC, 128, 64))
    wkvg = grid_a(np.ascontiguousarray(f(inputs["L1_attn_Wkv"], dtype=np.float32)), 4)
    wkr = np.ascontiguousarray(wkr_.reshape(4, 128, 64))
    wkrs = np.ascontiguousarray(wkr_[:, swap].reshape(4, 128, 64))
    wkupg = grid_a(np.ascontiguousarray(f(inputs["L1_attn_Wkup"], dtype=np.float32)), NH)
    wvup = np.ascontiguousarray(f(inputs["L1_attn_Wvup"], dtype=np.float32))
    wo1 = np.ascontiguousarray(f(inputs["L1_attn_Wo"], dtype=np.float32))
    gwg = grid_a(np.ascontiguousarray(f(inputs["L1_ffn_gate_w"], dtype=np.float32).T), NIC)
    uwg = grid_a(np.ascontiguousarray(f(inputs["L1_ffn_up_w"], dtype=np.float32).T), NIC)
    dwt = np.ascontiguousarray(f(inputs["L1_ffn_down_w"], dtype=np.float32).T)
    gs = np.ascontiguousarray(f(inputs["L1_ffn_gate_s"], dtype=np.float32).reshape(NIC, 128).T)
    us = np.ascontiguousarray(f(inputs["L1_ffn_up_s"], dtype=np.float32).reshape(NIC, 128).T)
    dsb = np.ascontiguousarray(
        np.broadcast_to(f(inputs["L1_ffn_down_s"], dtype=np.float32)[None, :], (128, D)))

    maps = []
    for c in range(8):
        r = c % 2
        pos = 2 * np.arange(TOK) + r
        inv = 1.0 / 10000.0 ** (np.arange(0, 64, 2, dtype=np.float64) / 64)
        fr = np.outer(pos.astype(np.float64), inv)
        tab = np.zeros((128, TOK), np.float32)
        tab[0:64, :] = np.concatenate([np.cos(fr).T, np.cos(fr).T], 0)
        tab[64:128, :] = np.concatenate([-np.sin(fr).T, np.sin(fr).T], 0)
        maps.append({
            "xin": xs[c], "wp": wp_grid, "wo0": wo0, "rw": rw,
            "w1": w1g, "w3": w3g, "w2": w2s,
            "wq": wqg, "wqr": wqr, "wqrs": wqrs, "wkv": wkvg,
            "wkr": wkr, "wkrs": wkrs, "wkup": wkupg, "wvup": wvup, "wo1": wo1,
            "gw": gwg, "uw": uwg, "dw": dwt, "gs": gs, "us": us, "dsb": dsb,
            "ropetab": tab,
            "m_r0": tri_incl, "m_r1": (tri_strict if r == 0 else tri_incl),
            "idf": ident_f32, "idb": ident_bf,
        })
    return maps


def kernel(**inputs):
    maps = stage_inputs(inputs)
    nc = _prog("fused4", fused_build_mk(4), FUSED_SEMS)
    res = run_bass_kernel_spmd(nc, maps, core_ids=list(range(8)))
    out = np.empty((B, S, D), np.float32)
    for c in range(8):
        out[c // 2, c % 2::2, :] = res.results[c]["xout"]
    return out


# version 03 (final)
# speedup vs baseline: 1.6183x; 1.5206x over previous
import math
import contextlib
import numpy as np
import ml_dtypes
import concourse.bass as bass
import concourse.mybir as mybir
from concourse.bass_utils import run_bass_kernel_spmd

BF16N = ml_dtypes.bfloat16
F32 = mybir.dt.float32
F32R = mybir.dt.float32r
BF16 = mybir.dt.bfloat16
AF = mybir.ActivationFunctionType
ALU = mybir.AluOpType
AX = mybir.AxisListType

B, S, D = 4, 2048, 2048
NH, HD = 16, 128
TOK = 1024
NT = TOK // 128
NDC = D // 128
NEG = -1e9
QS0 = 1.0 / math.sqrt(HD)
QS1 = 1.0 / math.sqrt(HD + 64)
FF, NE = 512, 8
ITER = 5632
NIC = ITER // 128

ALL_ENGS = ("sync", "tensor", "vector", "scalar", "gpsimd")


class P:
    def __init__(self, nc, sems, which, engines):
        self.nc = nc
        self.sems = sems
        self.which = which
        self.engines = engines
        self.cnt = {n: 0 for n in sems}

    def _do(self, eng, fn, inc):
        if self.which == eng:
            inst = fn(self.engines[eng])
            if inc is not None:
                inst.then_inc(self.sems[inc[0]], inc[1])
        if inc is not None:
            self.cnt[inc[0]] += inc[1]

    def wait(self, eng, sem, val=None):
        if val is None:
            val = self.cnt[sem]
        val = max(0, val)
        if self.which == eng:
            self.engines[eng].wait_ge(self.sems[sem], val)

    def barrier(self):
        for e in ALL_ENGS:
            for sem in self.sems:
                if self.cnt[sem] > 0:
                    self.wait(e, sem)

    def dma(self, q, out, in_, inc):
        self._do(q, lambda e: e.dma_start(out=out, in_=in_), inc)

    def dmat(self, q, out, in_, inc):
        self._do(q, lambda e: e.dma_start_transpose(out, in_), inc)

    def mm(self, out, lhsT, rhs, start, stop, inc=None):
        self._do("tensor", lambda e: e.matmul(out, lhsT, rhs, start=start, stop=stop), inc)

    def trans(self, out, in_, ident, inc=None):
        self._do("tensor", lambda e: e.transpose(out, in_, ident), inc)

    def act(self, out, in_, func, bias=0.0, scale=1.0, accum_out=None, inc=None):
        self._do("scalar", lambda e: e.activation(out, in_, func, bias=bias, scale=scale,
                                                  accum_out=accum_out), inc)

    def vop(self, name, *args, inc=None, eng="vector", **kw):
        self._do(eng, lambda e: getattr(e, name)(*args, **kw), inc)

    def collective(self, kind, op, rg, ins, outs, inc):
        self._do("gpsimd", lambda e: e.collective_compute(
            kind, op, replica_groups=rg, ins=ins, outs=outs), inc)


def mkprog(build_fn, sem_names):
    nc = bass.Bass("TRN2", target_bir_lowering=False, debug=False,
                   dynamic_dma_scratch_size=512)
    state = {}
    build_fn(None, nc, state)
    with contextlib.ExitStack() as st:
        sems = {n: st.enter_context(nc.semaphore(n)) for n in sem_names}
        for name, h in state["tensors"]:
            state[name] = st.enter_context(h)
        block = st.enter_context(nc.Block())
        engines = {}

        def reg(name):
            def deco(engobj):
                engines[name] = engobj
                build_fn(P(nc, sems, name, engines), nc, state)
            return deco

        block.sync(reg("sync"))
        block.tensor(reg("tensor"))
        block.vector(reg("vector"))
        block.scalar(reg("scalar"))
        block.gpsimd(reg("gpsimd"))
    return nc


def declare(nc, s, specs):
    s["tensors"] = []
    for kind, name, shape, dt in specs:
        if kind == "dram_in":
            s[name] = nc.dram_tensor(name, shape, dt, kind="ExternalInput").ap()
        elif kind == "dram_out":
            s[name] = nc.dram_tensor(name, shape, dt, kind="ExternalOutput").ap()
        elif kind == "dram":
            s[name + "_t"] = nc.dram_tensor(name, shape, dt)
        elif kind == "sbuf":
            s["tensors"].append((name, nc.sbuf_tensor(name, shape, dt)))
        elif kind == "psum":
            s["tensors"].append((name, nc.psum_tensor(name, shape, dt)))


def view(arena, byte_off, shape, dt):
    assert byte_off % 4 == 0
    esz = {F32: 4, F32R: 4, BF16: 2}[dt]
    n = 1
    for d_ in shape[1:]:
        n *= d_
    f32len = (n * esz + 3) // 4
    ap = arena[:, byte_off // 4: byte_off // 4 + f32len]
    if dt != F32:
        ap = ap.bitcast(dt)
    if len(shape) == 2:
        return ap
    if len(shape) == 3:
        return ap.rearrange("p (a b) -> p a b", b=shape[2])
    if len(shape) == 4:
        return ap.rearrange("p (a b c) -> p a b c", b=shape[2], c=shape[3])
    raise ValueError(shape)


def emit_ln(p, s, h_fm, router=False):
    tr0, tr1 = s["ps0"], s["ps1"]
    for t in range(NT):
        xt = s["xres"][:, t, :]
        p.wait("scalar", "tr_ev")
        p.act(s["htile"][:, :], xt, AF.Square, inc=("a", 1))
        p.vop("reduce_sum", s["c_sum"][:, :], xt, axis=AX.X, inc=("v", 1))
        p.wait("vector", "a")
        p.vop("reduce_sum", s["c_sqs"][:, :], s["htile"][:, :], axis=AX.X, inc=("v", 1))
        p.wait("scalar", "v")
        p.act(s["c_mean"][:, :], s["c_sum"][:, :], AF.Copy, scale=1.0 / D, inc=("a", 1))
        p.act(s["c_msq"][:, :], s["c_sqs"][:, :], AF.Copy, scale=1.0 / D, inc=("a", 1))
        p.wait("vector", "a")
        p.vop("tensor_tensor", s["c_m2"][:, :], s["c_mean"][:, :], s["c_mean"][:, :],
              op=ALU.mult, inc=("v", 1))
        p.wait("vector", "v")
        p.vop("tensor_sub", s["c_var"][:, :], s["c_msq"][:, :], s["c_m2"][:, :], inc=("v", 1))
        p.wait("scalar", "v")
        p.wait("scalar", "init", s["n_init"])
        p.act(s["c_std"][:, :], s["c_var"][:, :], AF.Sqrt, bias=s["c_eps"][:, 0:1], inc=("a", 1))
        p.wait("vector", "a")
        p.vop("reciprocal", s["c_rstd"][:, :], s["c_std"][:, :], inc=("v", 1))
        p.wait("vector", "v")
        p.vop("tensor_tensor", s["c_mr"][:, :], s["c_mean"][:, :], s["c_rstd"][:, :],
              op=ALU.mult, inc=("v", 1))
        p.wait("vector", "v")
        p.vop("tensor_scalar", s["c_nmr"][:, :], s["c_mr"][:, :], s["c_neg1"][:, 0:1], None,
              op0=ALU.mult, inc=("v", 1))
        p.wait("scalar", "v")
        p.act(s["htile"][:, :], xt, AF.Identity, bias=s["c_nmr"][:, 0:1],
              scale=s["c_rstd"][:, 0:1], inc=("a", 1))
        p.wait("tensor", "a")
        for dc in range(NDC):
            ps = (tr0 if dc % 2 == 0 else tr1)[:, 0:128]
            p.wait("tensor", "tr_ev")
            p.trans(ps, s["htile"][:, dc * 128:(dc + 1) * 128], s["id_f32"][:, :],
                    inc=("t", 1))
            p.wait("vector", "t")
            p.vop("tensor_copy", h_fm[:, dc, t * 128:(t + 1) * 128], ps, inc=("tr_ev", 1))
            if router:
                p.wait("vector", "mmr")
                p.vop("tensor_copy", s["hseg"][:, :], ps, inc=("tr_ev", 1))
                p.wait("tensor", "tr_ev")
                p.mm(s["pv"][:, t * 8:(t + 1) * 8], s["hseg"][:, :], s["rwsb"][:, dc, :],
                     start=(dc == 0), stop=(dc == NDC - 1), inc=("mmr", 1))
    p.wait("tensor", "tr_ev")
    p.wait("scalar", "tr_ev")


def emit_proj_grid(p, s, wgrid, h_fm, oc_evacs):
    mm_per_slab = 2 * NDC
    b_pcast = p.cnt["pcast"]
    b_wld = p.cnt["wld"]
    b_mmw = p.cnt["mmw"]
    b_pevac = p.cnt["pevac"]
    for i, (oc, evac) in enumerate(oc_evacs):
        sl = i % 2
        p.wait("sync", "pcast", b_pcast + i - 1)
        p.dma("sync", s["wsf"][:, sl, :, :],
              wgrid[:, oc].rearrange("c p f -> p c f"), inc=("wld", 16))
        p.wait("vector", "wld", b_wld + (i + 1) * 16)
        p.wait("vector", "mmw", b_mmw + (i - 1) * mm_per_slab)
        p.vop("tensor_copy", s["wsb"][:, sl, :, :], s["wsf"][:, sl, :, :],
              inc=("pcast", 1))
        p.wait("tensor", "pcast", b_pcast + i + 1)
        for tt in range(2):
            j = i * 2 + tt
            ps = s["ps%d" % (j % 4)]
            p.wait("tensor", "pevac", b_pevac + j - 3)
            for dc in range(NDC):
                p.mm(ps[:, :], s["wsb"][:, sl, dc, :],
                     h_fm[:, dc, tt * 512:(tt + 1) * 512],
                     start=(dc == 0), stop=(dc == NDC - 1), inc=("mmw", 1))
            evac(p, s, ps, oc, tt, j)


def evac_stage_dma(p, s, ps, dst, row0, tt, j, scale):
    stg = s["stage"][:, j % 2, :]
    sem = "stdma" if j % 2 == 0 else "stdmb"
    p.wait("scalar", "mmw")
    p.wait("scalar", sem)
    p.act(stg, ps[:, :], AF.Copy, scale=scale, inc=("pevac", 1))
    p.dma("scalar", dst[row0:row0 + 128, tt * 512:(tt + 1) * 512], stg, inc=(sem, 16))


def emit_l0_qkv(p, s):
    def mk(oc):
        def evac(p, s, ps, oc_, tt, j):
            if oc_ < 16:
                evac_stage_dma(p, s, ps, s["q0d"], oc_ * 128, tt, j, QS0)
            else:
                evac_stage_dma(p, s, ps, s["cin0"], (oc_ - 16) * 128, tt, j, 1.0)
        return evac
    emit_proj_grid(p, s, s["wp"], s["h_fm"], [(oc, mk(oc)) for oc in range(48)])
    p.wait("gpsimd", "stdma")
    p.wait("gpsimd", "stdmb")


def emit_attn(p, s, l1):
    ps_s = [s["ps0"], s["ps1"], s["ps2"], s["ps3"]]
    if l1:
        for cc in range(4):
            for r in range(2):
                p.dma("sync", s["ckv_sb"][:, cc, r * 1024:(r + 1) * 1024],
                      s["ckvg"][r * 576 + cc * 128: r * 576 + (cc + 1) * 128, :],
                      inc=("ld2", 16))
        for r in range(2):
            p.dma("sync", s["kr_sb"][:, r * 1024:(r + 1) * 1024],
                  s["ckvg"][r * 576 + 512: r * 576 + 576, :], inc=("ld2", 16))
        p.dma("sync", s["qr_sb"][:, :], s["qr1d"], inc=("ld2", 16))
        p.wait("tensor", "ld2")

    for h in range(NH):
        p.wait("sync", "pvev")
        p.dma("sync", s["q_sb"][:, :], s["qin"][h * 128:(h + 1) * 128, :],
              inc=("kvld", 16))
        if not l1:
            for r in range(2):
                p.dma("sync", s["k_sb"][:, r * 1024:(r + 1) * 1024],
                      s["kvg"][r * 4096 + h * 128: r * 4096 + (h + 1) * 128, :],
                      inc=("kvld", 16))
                for cb in range(8):
                    p.dmat("sync", s["v_sb"][:, r * 8 + cb, :],
                           s["kvg"][r * 4096 + 2048 + h * 128: r * 4096 + 2048 + (h + 1) * 128,
                                    cb * 128:(cb + 1) * 128], inc=("kvld", 16))
            p.wait("tensor", "kvld")
        else:
            if h % 4 == 0:
                g = h // 4
                p.dma("sync", s["wvuf"][:, :, :],
                      s["wvup"].rearrange("(c p) o -> p c o", p=128)
                      [:, :, g * 512:(g + 1) * 512], inc=("vwld", 16))
                p.wait("vector", "vwld")
                p.wait("vector", "mmv")
                p.vop("tensor_copy", s["wvub"][:, :, :], s["wvuf"][:, :, :], inc=("vcast", 1))
                p.wait("tensor", "vcast")
                p.wait("tensor", "sfree")
                for tc in range(16):
                    psk = ps_s[tc % 4]
                    p.wait("tensor", "vevac")
                    for cc in range(4):
                        p.mm(psk[:, :], s["ckv_sb"][:, cc, tc * 128:(tc + 1) * 128],
                             s["wvub"][:, cc, :], start=(cc == 0), stop=(cc == 3),
                             inc=("mmv", 1) if cc == 3 else None)
                    p.wait("vector", "mmv")
                    p.vop("tensor_copy", s["v4"][:, tc, :], psk[:, :], inc=("vevac", 1))
            p.wait("tensor", "sfree")
            p.wait("tensor", "vevac")
            for cc in range(4):
                wslot = s["wkuf"][:, cc % 2, :]
                wbslot = s["wkub"][:, cc % 2, :]
                p.wait("sync", "kcast")
                p.dma("sync", wslot, s["wkup"][cc, h], inc=("kwld", 16))
                p.wait("vector", "kwld")
                p.wait("vector", "mmk")
                p.vop("tensor_copy", wbslot, wslot, inc=("kcast", 1))
                p.wait("tensor", "kcast")
                for kt in range(4):
                    if cc == 0:
                        p.wait("tensor", "kevac")
                    p.mm(ps_s[kt][:, :], wbslot,
                         s["ckv_sb"][:, cc, kt * 512:(kt + 1) * 512],
                         start=(cc == 0), stop=(cc == 3), inc=("mmk", 1))
            p.wait("vector", "mmk")
            for kt in range(4):
                p.vop("tensor_copy", s["k_sb"][:, kt * 512:(kt + 1) * 512],
                      ps_s[kt][:, :], inc=("kevac", 1))
            p.wait("tensor", "kevac")
            p.wait("tensor", "kvld")

        for qb in range(NT):
            nb = qb + 1
            L = 2 * nb * 128
            nkt = (L + 511) // 512
            qlh = s["q_sb"][:, qb * 128:(qb + 1) * 128]
            p.wait("tensor", "sfree")
            segs = []
            for run, base in ((0, 0), (1, 1024)):
                c0 = run * nb * 128
                done = 0
                while done < nb * 128:
                    col = c0 + done
                    w = min(512 - (col % 512), nb * 128 - done)
                    segs.append((col, w, base + done))
                    done += w
            tile_written = [False] * 4
            for (col, w, kcol) in segs:
                kt = col // 512
                st0 = not tile_written[kt]
                tile_written[kt] = True
                if l1:
                    p.mm(ps_s[kt][:, col % 512:col % 512 + w], qlh,
                         s["k_sb"][:, kcol:kcol + w], start=st0, stop=False)
                    p.mm(ps_s[kt][:, col % 512:col % 512 + w],
                         s["qr_sb"][:, qb * 128:(qb + 1) * 128],
                         s["kr_sb"][:, kcol:kcol + w], start=False, stop=True,
                         inc=("ms", 1))
                else:
                    p.mm(ps_s[kt][:, col % 512:col % 512 + w], qlh,
                         s["k_sb"][:, kcol:kcol + w], start=st0, stop=True,
                         inc=("ms", 1))
            p.wait("vector", "ms")
            for run in range(2):
                col = (run * nb + nb - 1) * 128
                kt, off = col // 512, col % 512
                msk = s["mask0"] if run == 0 else s["mask1"]
                p.vop("tensor_tensor", ps_s[kt][:, off:off + 128],
                      ps_s[kt][:, off:off + 128], msk[:, :], op=ALU.add, inc=("v", 1))
            p.wait("vector", "v")
            for kt in range(nkt):
                w = min(512, L - kt * 512)
                p.vop("reduce_max", s["nmax"][:, kt:kt + 1], ps_s[kt][:, 0:w],
                      axis=AX.X, negate=True, inc=("v", 1))
            p.wait("vector", "v")
            if nkt > 1:
                p.vop("tensor_reduce", s["nmaxc"][:, :], s["nmax"][:, 0:nkt],
                      axis=AX.X, op=ALU.min, inc=("v", 1))
            else:
                p.vop("tensor_copy", s["nmaxc"][:, :], s["nmax"][:, 0:1], inc=("v", 1))
            p.wait("scalar", "v")
            p.wait("scalar", "t")
            for kt in range(nkt):
                w = min(512, L - kt * 512)
                p.act(s["attn"][:, kt * 512:kt * 512 + w], ps_s[kt][:, 0:w], AF.Exp,
                      bias=s["nmaxc"][:, 0:1], accum_out=s["rsums"][:, kt:kt + 1],
                      inc=("sfree", 1))
            p.wait("vector", "sfree")
            if nkt > 1:
                p.vop("reduce_sum", s["rsum"][:, :], s["rsums"][:, 0:nkt], axis=AX.X,
                      inc=("v", 1))
            else:
                p.vop("tensor_copy", s["rsum"][:, :], s["rsums"][:, 0:1], inc=("v", 1))
            p.wait("vector", "v")
            p.vop("reciprocal", s["rcp"][:, :], s["rsum"][:, :], inc=("v", 1))
            p.wait("vector", "v")
            p.vop("tensor_scalar", s["attn"][:, 0:L], s["attn"][:, 0:L],
                  s["rcp"][:, 0:1], None, op0=ALU.mult, inc=("v", 1))
            p.wait("tensor", "v")
            nkb = 2 * nb
            for kb in range(nkb):
                bank = s["pb0"] if (kb // 8) % 2 == 0 else s["pb1"]
                slot = kb % 8
                if slot == 0:
                    p.wait("tensor", "atev")
                p.trans(bank[:, slot * 128:(slot + 1) * 128],
                        s["attn"][:, kb * 128:(kb + 1) * 128], s["id_bf"][:, :],
                        inc=("t", 1))
                if slot == 7 or kb == nkb - 1:
                    p.wait("vector", "t")
                    n = slot + 1
                    p.vop("tensor_copy", s["attnT"][:, (kb - n + 1) * 128:(kb + 1) * 128],
                          bank[:, 0:n * 128], inc=("atev", 1))
            p.wait("tensor", "atev")
            p.wait("tensor", "pvev")
            for kb in range(nkb):
                run, jj = (0, kb) if kb < nb else (1, kb - nb)
                if l1:
                    vt = s["v4"][:, run * 8 + jj, (h % 4) * 128:(h % 4 + 1) * 128]
                else:
                    vt = s["v_sb"][:, run * 8 + jj, :]
                p.mm(s["pv"][:, :], vt, s["attnT"][:, kb * 128:(kb + 1) * 128],
                     start=(kb == 0), stop=(kb == nkb - 1),
                     inc=("mpv", 1) if kb == nkb - 1 else None)
            p.wait("vector", "mpv")
            p.vop("tensor_copy", s["aout_fm"][:, h, qb * 128:(qb + 1) * 128],
                  s["pv"][:, :], inc=("pvev", 1))

    p.wait("tensor", "pvev")
    p.wait("tensor", "sfree")
    wo = s["wo1"] if l1 else s["wo0"]
    wo_mark = {}
    wc_free = {0: 0, 1: 0}
    n_wo = 0
    for ot in range(4):
        for qg in range(2):
            for dc in range(NDC):
                wslot = s["wochunk"][:, dc % 2, :]
                wbslot = s["wob"][:, dc % 2, :]
                p.wait("sync", "wcast")
                p.dma("sync", wslot,
                      wo[dc * 128:(dc + 1) * 128, ot * 512:(ot + 1) * 512],
                      inc=("owld", 16))
                p.wait("vector", "owld")
                p.wait("vector", "mmo")
                p.vop("tensor_copy", wbslot, wslot, inc=("wcast", 1))
                wc_free[n_wo % 2] = p.cnt["wcast"]
                p.wait("tensor", "wcast")
                if dc == 0:
                    p.wait("tensor", "oev")
                for j in range(4):
                    qb = qg * 4 + j
                    p.mm(ps_s[j][:, :],
                         s["aout_fm"][:, dc, qb * 128:(qb + 1) * 128], wbslot,
                         start=(dc == 0), stop=(dc == NDC - 1), inc=("mmo", 1))
                wo_mark[n_wo] = p.cnt["mmo"]
                n_wo += 1
            p.wait("vector", "mmo")
            for j in range(4):
                qb = qg * 4 + j
                p.vop("tensor_tensor", s["xres"][:, qb, ot * 512:(ot + 1) * 512],
                      s["xres"][:, qb, ot * 512:(ot + 1) * 512], ps_s[j][:, :],
                      op=ALU.add, inc=("oev", 1))
    p.wait("tensor", "oev")


def emit_moe(p, s):
    p.wait("vector", "mmr")
    p.vop("tensor_copy", s["sc"][:, :, :].rearrange("p a b -> p (a b)"), s["pv"][:, 0:64],
          inc=("v", 1))
    p.wait("vector", "v")
    for t in range(NT):
        st = s["sc"][:, t, :]
        p.wait("vector", "a")
        p.vop("reduce_max", s["c_m1"][:, :], st, axis=AX.X, inc=("v", 1))
        p.vop("reduce_max", s["c_nm1"][:, :], st, axis=AX.X, negate=True, inc=("v", 1))
        p.wait("vector", "v")
        p.vop("tensor_scalar", s["t_eq"][:, :], st, s["c_m1"][:, 0:1], None,
              op0=ALU.is_ge, inc=("v", 1))
        p.wait("vector", "v")
        p.vop("scalar_tensor_tensor", s["t_t"][:, :], s["t_eq"][:, :], s["c_neg9"][:, 0:1],
              st, op0=ALU.mult, op1=ALU.add, inc=("v", 1))
        p.wait("vector", "v")
        p.vop("reduce_max", s["c_m2r"][:, :], s["t_t"][:, :], axis=AX.X, inc=("v", 1))
        p.wait("scalar", "v")
        p.act(s["t_e1"][:, :], st, AF.Exp, bias=s["c_nm1"][:, 0:1], inc=("a", 1))
        p.act(s["c_em"][:, :], s["c_m2r"][:, :], AF.Exp, bias=s["c_nm1"][:, 0:1], inc=("a", 1))
        p.wait("vector", "a")
        p.vop("tensor_tensor", s["c_den"][:, :], s["c_em"][:, :], s["c_one"][:, 0:1],
              op=ALU.add, inc=("v", 1))
        p.wait("vector", "v")
        p.vop("reciprocal", s["c_rden"][:, :], s["c_den"][:, :], inc=("v", 1))
        p.wait("vector", "v")
        p.vop("tensor_scalar", s["t_ge"][:, :], st, s["c_m2r"][:, 0:1], None,
              op0=ALU.is_ge, inc=("v", 1))
        p.wait("vector", "v")
        p.vop("tensor_tensor", s["t_e1"][:, :], s["t_e1"][:, :], s["t_ge"][:, :],
              op=ALU.mult, inc=("v", 1))
        p.wait("vector", "v")
        p.vop("tensor_scalar", s["gates"][:, t, :], s["t_e1"][:, :], s["c_rden"][:, 0:1],
              None, op0=ALU.mult, inc=("v", 1))
    p.wait("tensor", "v")

    b_hev = p.cnt["hev"]
    b_oev = p.cnt["oev"]
    w13f_free = 0
    w2otf_free = 0
    w13b_mark = {}
    w2_mark = {}
    wld_n = p.cnt["wld"]
    n_oev = 0
    for e in range(NE):
        for ffq in range(4):
            g = e * 4 + ffq
            for w_i, nm in ((0, "w1"), (1, "w3")):
                li = g * 2 + w_i
                p.wait("sync", "pcast", w13f_free)
                p.dma("sync", s["w13f"][:, :, :],
                      s[nm][e].rearrange("c q p f -> p c q f")[:, :, ffq, :],
                      inc=("wld", 16))
                p.wait("vector", "wld")
                p.wait("vector", "mmw", w13b_mark.get(g - 2, 0))
                p.vop("tensor_copy", s["w13b"][:, li % 4, :, :], s["w13f"][:, :, :],
                      inc=("pcast", 1))
                w13f_free = p.cnt["pcast"]
            p.wait("tensor", "pcast")
            for tt in range(2):
                p.wait("tensor", "hev", b_hev + 2 * g + tt)
                for w_i in range(2):
                    ps = s["ps%d" % (2 + w_i)]
                    for dc in range(NDC):
                        p.mm(ps[:, :], s["w13b"][:, (g * 2 + w_i) % 4, dc, :],
                             s["h_fm"][:, dc, tt * 512:(tt + 1) * 512],
                             start=(dc == 0), stop=(dc == NDC - 1), inc=("mmw", 1))
                p.wait("scalar", "mmw")
                p.wait("scalar", "hev", b_hev + 2 * g + tt)
                p.act(s["sg"][:, :], s["ps2"][:, :], AF.Silu, inc=("a", 1))
                p.wait("vector", "a")
                p.vop("tensor_tensor", s["hid"][:, ffq, tt * 512:(tt + 1) * 512],
                      s["sg"][:, :], s["ps3"][:, :], op=ALU.mult, inc=("hev", 1))
            w13b_mark[g] = p.cnt["mmw"]
        p.wait("tensor", "hev")
        for ot in range(4):
            li = e * 4 + ot
            p.wait("sync", "pcast", w2otf_free)
            p.dma("sync", s["w2otf"][:, :, :],
                  s["w2"][e].rearrange("(c p) o -> p c o", p=128)[:, :, ot * 512:(ot + 1) * 512],
                  inc=("wld", 16))
            p.wait("vector", "wld")
            p.wait("vector", "mmw", w2_mark.get(li - 2, 0))
            p.vop("tensor_copy", s["w2otb"][:, li % 2, :, :], s["w2otf"][:, :, :],
                  inc=("pcast", 1))
            w2otf_free = p.cnt["pcast"]
            p.wait("tensor", "pcast")
            for tq in range(NT):
                ps = s["ps%d" % (tq % 2)]
                n_oev += 1
                p.wait("tensor", "oev", b_oev + n_oev - 2)
                for ffc in range(4):
                    p.mm(ps[:, :], s["hid"][:, ffc, tq * 128:(tq + 1) * 128],
                         s["w2otb"][:, li % 2, ffc, :], start=(ffc == 0), stop=(ffc == 3),
                         inc=("mmw", 1) if ffc == 3 else None)
                p.wait("vector", "mmw")
                p.vop("scalar_tensor_tensor", s["xres"][:, tq, ot * 512:(ot + 1) * 512],
                      ps[:, :], s["gates"][:, tq, e:e + 1],
                      s["xres"][:, tq, ot * 512:(ot + 1) * 512],
                      op0=ALU.mult, op1=ALU.add, inc=("oev", 1))
            w2_mark[li] = p.cnt["mmw"]
    p.wait("tensor", "oev")


def emit_l1_proj(p, s):
    def mkq(oc):
        def evac(p, s, ps, oc_, tt, j):
            evac_stage_dma(p, s, ps, s["q1d"], oc_ * 128, tt, j, QS1)
        return evac
    emit_proj_grid(p, s, s["wq"], s["h_fm"], [(oc, mkq(oc)) for oc in range(NH)])

    def mkc(oc):
        def evac(p, s, ps, oc_, tt, j):
            p.wait("vector", "mmw")
            p.vop("tensor_copy", s["ckv_sb1"][:, oc_, tt * 512:(tt + 1) * 512], ps[:, :],
                  inc=("pevac", 1))
            p.wait("scalar", "pevac")
            p.dma("scalar", s["cin1"][oc_ * 128:(oc_ + 1) * 128, tt * 512:(tt + 1) * 512],
                  s["ckv_sb1"][:, oc_, tt * 512:(tt + 1) * 512], inc=("stdma", 16))
        return evac
    emit_proj_grid(p, s, s["wkv"], s["h_fm"], [(oc, mkc(oc)) for oc in range(4)])

    for which in ("qr", "kr"):
        ps = s["ps2"]
        for swapped, dst in ((False, "rraw"), (True, "rrsw")):
            for tt in range(2):
                nchunk = NDC if which == "qr" else 4
                for cc in range(nchunk):
                    if which == "qr":
                        wsrc = "wqrs" if swapped else "wqr"
                        wslot = s["wchunk"][:, cc % 2, :]
                        wbslot = s["wchunkb"][:, cc % 2, :]
                        p.wait("sync", "pcast")
                        p.dma("sync", wslot[:, 0:64], s[wsrc][cc], inc=("wld", 16))
                        p.wait("vector", "wld")
                        p.wait("vector", "mmw")
                        p.vop("tensor_copy", wbslot[:, 0:64], wslot[:, 0:64],
                              inc=("pcast", 1))
                        p.wait("tensor", "pcast")
                        lhsT = wbslot[:, 0:64]
                        rhs = s["h_fm"][:, cc, tt * 512:(tt + 1) * 512]
                    else:
                        wsrc = "wkrs" if swapped else "wkr"
                        wslot = s["wkuf"][:, cc % 2, 0:64]
                        wbslot = s["wkub"][:, cc % 2, 0:64]
                        p.wait("sync", "kcast")
                        p.dma("sync", wslot, s[wsrc][cc], inc=("kwld", 16))
                        p.wait("vector", "kwld")
                        p.wait("vector", "mmw")
                        p.vop("tensor_copy", wbslot, wslot, inc=("kcast", 1))
                        p.wait("tensor", "kcast")
                        lhsT = wbslot
                        rhs = s["ckv_sb1"][:, cc, tt * 512:(tt + 1) * 512]
                    if cc == 0:
                        p.wait("tensor", "rev")
                    p.mm(ps[0:64, 0:512], lhsT, rhs,
                         start=(cc == 0), stop=(cc == nchunk - 1), inc=("mmw", 1))
                p.wait("vector", "mmw")
                p.vop("tensor_copy", s[dst][:, tt * 512:(tt + 1) * 512], ps[0:64, 0:512],
                      inc=("rev", 1))
        p.wait("vector", "rev")
        p.vop("tensor_tensor", s["rt1"][:, :], s["rraw"][:, :], s["ctab"][:, :],
              op=ALU.mult, inc=("v", 1))
        p.vop("tensor_tensor", s["roped"][:, :], s["rrsw"][:, :], s["sstab"][:, :],
              op=ALU.mult, inc=("v", 1))
        p.wait("vector", "v")
        p.vop("tensor_tensor", s["roped"][:, :], s["roped"][:, :], s["rt1"][:, :],
              op=ALU.add, inc=("v", 1))
        p.wait("vector", "v")
        p.wait("vector", "init", s["n_init"])
        if which == "qr":
            p.vop("tensor_scalar", s["ropedb"][:, :], s["roped"][:, :], s["c_qs1"][0:64, 0:1],
                  None, op0=ALU.mult, inc=("v", 1))
            p.wait("scalar", "v")
            p.dma("scalar", s["qr1d"], s["ropedb"][:, :], inc=("stdma", 16))
        else:
            p.vop("tensor_copy", s["ropedb"][:, :], s["roped"][:, :], inc=("v", 1))
            p.wait("scalar", "v")
            p.dma("scalar", s["cin1"][512:576, :], s["ropedb"][:, :], inc=("stdma", 16))
    p.wait("gpsimd", "stdma")
    p.wait("gpsimd", "stdmb")


def emit_ffn(p, s):
    p.dma("sync", s["gsb"][:, :], s["gs"], inc=("ld2", 16))
    p.dma("sync", s["usb"][:, :], s["us"], inc=("ld2", 16))
    p.dma("sync", s["dsb_sb"][:, :], s["dsb"], inc=("ld2", 16))
    for e in ("vector", "scalar", "tensor"):
        p.wait(e, "ld2")
    b_hev = p.cnt["hev"]
    b_oev = p.cnt["oev"]
    gu_mark = {}
    dw_mark = {}
    wf_free = {0: 0, 1: 0}
    dwf_free = {0: 0, 1: 0}
    n_ic = 0
    n_dw = 0
    for th in range(2):
        t0 = th * 512
        for ic in range(NIC):
            for w_i, nm in ((0, "gw"), (1, "uw")):
                p.wait("sync", "cast")
                p.dma("sync", s["wf"][:, w_i, :, :],
                      s[nm].rearrange("c i p f -> p c i f")[:, :, ic, :], inc=("wld2", 16))
                p.wait("vector", "wld2")
                p.wait("vector", "mmh")
                p.vop("tensor_copy", s["wgb" if w_i == 0 else "wub"][:, :, :],
                      s["wf"][:, w_i, :, :], inc=("cast", 1))
                wf_free[w_i] = p.cnt["cast"]
            p.wait("tensor", "cast")
            n_ic += 1
            p.wait("tensor", "hev", b_hev + n_ic - 1)
            for w_i in range(2):
                ps = s["ps%d" % (2 + w_i)]
                wb = s["wgb"] if w_i == 0 else s["wub"]
                for dc in range(NDC):
                    p.mm(ps[:, :], wb[:, dc, :], s["h_bf"][:, dc, t0:t0 + 512],
                         start=(dc == 0), stop=(dc == NDC - 1), inc=("mmh", 1))
            gu_mark[n_ic - 1] = p.cnt["mmh"]
            p.wait("scalar", "mmh")
            p.wait("scalar", "hev", b_hev + n_ic - 1)
            p.act(s["sg"][:, :], s["ps2"][:, :], AF.Silu, scale=s["gsb"][:, ic:ic + 1],
                  inc=("a", 1))
            p.wait("vector", "a")
            p.vop("scalar_tensor_tensor", s["hid1"][:, ic, :], s["ps3"][:, :],
                  s["usb"][:, ic:ic + 1], s["sg"][:, :], op0=ALU.mult, op1=ALU.mult,
                  inc=("hev", 1))
        p.wait("tensor", "hev")
        for ot in range(4):
            for ic in range(NIC):
                li = ot * NIC + ic
                p.wait("sync", "dcast")
                p.dma("sync", s["dwf"][:, li % 2, :],
                      s["dw"][ic * 128:(ic + 1) * 128, ot * 512:(ot + 1) * 512],
                      inc=("dwld", 16))
                p.wait("vector", "dwld")
                p.wait("vector", "mmd")
                p.vop("tensor_copy", s["dwb"][:, li % 2, :], s["dwf"][:, li % 2, :],
                      inc=("dcast", 1))
                dwf_free[n_dw % 2] = p.cnt["dcast"]
                p.wait("tensor", "dcast")
                if ic == 0:
                    p.wait("tensor", "oev")
                for tq in range(NT // 2):
                    p.mm(s["ps%d" % tq][:, :], s["hid1"][:, ic, tq * 128:(tq + 1) * 128],
                         s["dwb"][:, li % 2, :], start=(ic == 0), stop=(ic == NIC - 1),
                         inc=("mmd", 1))
                dw_mark[n_dw] = p.cnt["mmd"]
                n_dw += 1
            p.wait("vector", "mmd")
            for tq in range(NT // 2):
                qb = th * 4 + tq
                p.vop("tensor_tensor", s["ytmp"][:, :], s["ps%d" % tq][:, :],
                      s["dsb_sb"][:, ot * 512:(ot + 1) * 512], op=ALU.mult, inc=("v", 1))
                p.wait("vector", "v")
                p.vop("tensor_tensor", s["xres"][:, qb, ot * 512:(ot + 1) * 512],
                      s["xres"][:, qb, ot * 512:(ot + 1) * 512], s["ytmp"][:, :],
                      op=ALU.add, inc=("oev", 1))
    p.wait("tensor", "oev")


FUSED_SEMS = ["ld", "ld2", "wld", "wld2", "dwld", "kvld", "kwld", "vwld", "owld",
              "pcast", "mmr",
              "init", "v", "a", "t", "tr_ev", "ms", "mmw", "mmh", "mmd", "mmk",
              "mmv", "mmo", "mpv", "sfree", "atev", "pvev", "kevac", "vevac",
              "hev", "oev", "pevac", "rev", "stdma", "stdmb", "st", "cast", "dcast",
              "kcast", "vcast", "wcast", "cc0", "cc1"]

RG_PAIRS = [[0, 1], [2, 3], [4, 5], [6, 7]]


def fused_build_mk(upto):
    def build(p, nc, s):
        if p is None:
            declare(nc, s, [
                ("dram_in", "xin", [TOK, D], F32),
                ("dram_in", "wp", [NDC, 48, 128, 128], F32),
                ("dram_in", "wo0", [D, D], F32),
                ("dram_in", "rw", [128, NDC, 8], F32),
                ("dram_in", "w1", [NE, NDC, 4, 128, 128], F32),
                ("dram_in", "w3", [NE, NDC, 4, 128, 128], F32),
                ("dram_in", "w2", [NE, FF, D], F32),
                ("dram_in", "wq", [NDC, NH, 128, 128], F32),
                ("dram_in", "wqr", [NDC, 128, 64], F32),
                ("dram_in", "wqrs", [NDC, 128, 64], F32),
                ("dram_in", "wkv", [NDC, 4, 128, 128], F32),
                ("dram_in", "wkr", [4, 128, 64], F32),
                ("dram_in", "wkrs", [4, 128, 64], F32),
                ("dram_in", "wkup", [4, NH, 128, 128], F32),
                ("dram_in", "wvup", [512, D], F32),
                ("dram_in", "wo1", [D, D], F32),
                ("dram_in", "gw", [NDC, NIC, 128, 128], F32),
                ("dram_in", "uw", [NDC, NIC, 128, 128], F32),
                ("dram_in", "dw", [ITER, D], F32),
                ("dram_in", "gs", [128, NIC], F32),
                ("dram_in", "us", [128, NIC], F32),
                ("dram_in", "dsb", [128, D], F32),
                ("dram_in", "ropetab", [128, TOK], F32),
                ("dram_in", "m_r0", [128, 128], F32),
                ("dram_in", "m_r1", [128, 128], F32),
                ("dram_in", "idf", [128, 128], F32),
                ("dram_in", "idb", [128, 128], BF16),
                ("dram_out", "xout", [TOK, D], F32),
                ("dram", "q0d", [D, TOK], BF16),
                ("dram", "q1d", [D, TOK], BF16),
                ("dram", "qr1d", [64, TOK], BF16),
                ("dram", "cin0", [2 * D, TOK], BF16),
                ("dram", "cout0", [4 * D, TOK], BF16),
                ("dram", "cin1", [576, TOK], BF16),
                ("dram", "cout1", [1152, TOK], BF16),
                ("sbuf", "xres", [128, NT, D], F32),
                ("sbuf", "A", [128, 13312], F32),
                ("sbuf", "C", [128, 12288], F32),
                ("sbuf", "W", [128, 10240], F32),
                ("sbuf", "stage", [128, 2, 512], BF16),
                ("sbuf", "wchunk", [128, 2, 128], F32),
                ("sbuf", "wchunkb", [128, 2, 128], BF16),
                ("sbuf", "hseg", [128, 128], F32),
                ("sbuf", "id_f32", [128, 128], F32),
                ("sbuf", "id_bf", [128, 128], BF16),
                ("sbuf", "mask0", [128, 128], F32),
                ("sbuf", "mask1", [128, 128], F32),
                ("sbuf", "rwsb", [128, NDC, 8], F32),
                ("sbuf", "gsb", [128, NIC], F32),
                ("sbuf", "usb", [128, NIC], F32),
                ("sbuf", "gates", [128, NT, NE], F32),
                ("sbuf", "sc", [128, NT, NE], F32),
                ("sbuf", "nmax", [128, 4], F32),
                ("sbuf", "nmaxc", [128, 1], F32),
                ("sbuf", "rsums", [128, 4], F32),
                ("sbuf", "rsum", [128, 1], F32),
                ("sbuf", "rcp", [128, 1], F32),
                ("sbuf", "t_eq", [128, NE], F32),
                ("sbuf", "t_t", [128, NE], F32),
                ("sbuf", "t_e1", [128, NE], F32),
                ("sbuf", "t_ge", [128, NE], F32),
            ] + [("sbuf", n, [128, 1], F32) for n in
                 ["c_sum", "c_sqs", "c_mean", "c_msq", "c_m2", "c_var", "c_std",
                  "c_rstd", "c_mr", "c_nmr", "c_eps", "c_neg1", "c_neg9", "c_one",
                  "c_qs1", "c_m1", "c_nm1", "c_m2r", "c_em", "c_den", "c_rden"]]
              + [("psum", "ps%d" % i, [128, 512], F32) for i in range(4)]
              + [("psum", "pb0", [128, 1024], BF16), ("psum", "pb1", [128, 1024], BF16),
                 ("psum", "pv", [128, 128], F32)])
            return

        A, C, W = s["A"], s["C"], s["W"]
        s["h_fm"] = view(A, 0, [128, NDC, TOK], BF16)
        s["h_bf"] = s["h_fm"]
        s["q_sb"] = view(A, 32768, [128, TOK], BF16)
        s["k_sb"] = view(A, 34816, [128, 2048], BF16)
        s["attn"] = view(A, 38912, [128, 2048], BF16)
        s["attnT"] = view(A, 43008, [128, 2048], BF16)
        s["qr_sb"] = view(A, 47104, [128, TOK], BF16)[0:64, :]
        s["aout_fm"] = view(A, 0, [128, NDC, TOK], BF16)
        s["kr_sb"] = view(A, 49152, [128, 2048], BF16)[0:64, :]
        s["htile"] = view(C, 0, [128, D], F32)
        s["rraw"] = view(C, 24576, [128, TOK], F32)[0:64, :]
        s["rrsw"] = view(C, 28672, [128, TOK], F32)[0:64, :]
        s["rt1"] = view(C, 32768, [128, TOK], F32)[0:64, :]
        s["roped"] = view(C, 36864, [128, TOK], F32)[0:64, :]
        s["ropedb"] = view(C, 40960, [128, TOK], BF16)[0:64, :]
        s["ctab"] = view(C, 8192, [128, TOK], F32)[0:64, :]
        s["sstab"] = view(C, 12288, [128, TOK], F32)[0:64, :]
        s["hid"] = view(C, 24576, [128, 4, TOK], BF16)
        s["ckv_sb"] = view(C, 0, [128, 4, 2048], BF16)
        s["ckv_sb1"] = view(C, 16384, [128, 4, TOK], BF16)
        s["v_sb"] = view(C, 16384, [128, 16, 128], BF16)
        s["v4"] = view(C, 16384, [128, 16, 512], BF16)
        s["hid1"] = view(C, 0, [128, NIC, 512], BF16)
        s["sg"] = view(C, 45056, [128, 512], F32)
        s["ytmp"] = view(C, 47104, [128, 512], F32)
        s["wsf"] = view(W, 0, [128, 2, NDC, 128], F32)
        s["wsb"] = view(W, 16384, [128, 2, NDC, 128], BF16)
        s["w13f"] = view(W, 0, [128, NDC, 128], F32)
        s["w13b"] = view(W, 8192, [128, 4, NDC, 128], BF16)
        s["w2otf"] = view(W, 24576, [128, 4, 512], F32)
        s["w2otb"] = view(W, 32768, [128, 2, 4, 512], BF16)
        s["wkuf"] = view(W, 0, [128, 2, 128], F32)
        s["wkub"] = view(W, 1024, [128, 2, 128], BF16)
        s["wvuf"] = view(W, 2048, [128, 4, 512], F32)
        s["wvub"] = view(W, 10240, [128, 4, 512], BF16)
        s["wochunk"] = view(W, 16384, [128, 2, 512], F32)
        s["wob"] = view(W, 20480, [128, 2, 512], BF16)
        s["wf"] = view(W, 0, [128, 2, NDC, 128], F32)
        s["wgb"] = view(W, 16384, [128, NDC, 128], BF16)
        s["wub"] = view(W, 20480, [128, NDC, 128], BF16)
        s["dwf"] = view(W, 24576, [128, 2, 512], F32)
        s["dwb"] = view(W, 28672, [128, 2, 512], BF16)
        s["dsb_sb"] = view(W, 32768, [128, D], F32)
        for nm in ("q0d", "q1d", "qr1d", "cin0", "cin1"):
            s[nm] = s[nm + "_t"].ap()

        for nm, val in (("c_eps", 1e-5), ("c_neg1", -1.0), ("c_neg9", NEG),
                        ("c_one", 1.0), ("c_qs1", QS1)):
            p.vop("memset", s[nm][:, :], val, inc=("init", 1), eng="gpsimd")
        s["n_init"] = 5
        p.dma("sync", s["id_f32"][:, :], s["idf"], inc=("ld", 16))
        p.dma("sync", s["id_bf"][:, :], s["idb"], inc=("ld", 16))
        p.dma("sync", s["mask0"][:, :], s["m_r0"], inc=("ld", 16))
        p.dma("sync", s["mask1"][:, :], s["m_r1"], inc=("ld", 16))
        p.dma("sync", s["xres"][:, :, :], s["xin"].rearrange("(t p) d -> p t d", p=128),
              inc=("ld", 16))
        for e in ALL_ENGS:
            p.wait(e, "ld")
            p.wait(e, "init", s["n_init"])

        emit_ln(p, s, s["h_fm"])
        emit_l0_qkv(p, s)
        p.collective("AllGather", ALU.bypass, RG_PAIRS,
                     [s["cin0_t"].ap().opt()], [s["cout0_t"].ap().opt()], inc=("cc0", 1))
        p.wait("gpsimd", "cc0")
        p.barrier()
        s["kvg"] = s["cout0_t"].ap()
        s["qin"] = s["q0d"]
        emit_attn(p, s, l1=False)
        p.barrier()
        if upto >= 2:
            p.dma("sync", s["rwsb"][:, :, :], s["rw"], inc=("ld2", 16))
            p.wait("tensor", "ld2")
            p.wait("vector", "ld2")
            emit_ln(p, s, s["h_fm"], router=True)
            emit_moe(p, s)
            p.barrier()
        if upto >= 3:
            emit_ln(p, s, s["h_fm"])
            p.dma("sync", s["ctab"][:, :], s["ropetab"][0:64, :], inc=("ld2", 16))
            p.dma("sync", s["sstab"][:, :], s["ropetab"][64:128, :], inc=("ld2", 16))
            p.wait("vector", "ld2")
            emit_l1_proj(p, s)
            p.collective("AllGather", ALU.bypass, RG_PAIRS,
                         [s["cin1_t"].ap().opt()], [s["cout1_t"].ap().opt()], inc=("cc1", 1))
            p.wait("gpsimd", "cc1")
            p.barrier()
            s["ckvg"] = s["cout1_t"].ap()
            s["qin"] = s["q1d"]
            emit_attn(p, s, l1=True)
            p.barrier()
        if upto >= 4:
            emit_ln(p, s, s["h_bf"])
            emit_ffn(p, s)
            p.barrier()
        p.wait("scalar", "oev")
        p.dma("scalar", s["xout"].rearrange("(t p) d -> p t d", p=128), s["xres"][:, :, :],
              inc=("st", 16))
        p.wait("scalar", "st")
    return build


def grid_a(w_t, ocn):
    din, dout = w_t.shape
    dcn = din // 128
    assert dout == ocn * 128
    return np.ascontiguousarray(w_t.reshape(dcn, 128, ocn, 128).transpose(0, 2, 1, 3))


_PROGS = {}


def _prog(key, builder, sems):
    if key not in _PROGS:
        _PROGS[key] = mkprog(builder, sems)
    return _PROGS[key]


def stage_inputs(inputs):
    f = np.asarray
    x = f(inputs["L0_attn_x"], dtype=np.float32)
    ident_f32 = np.eye(128, dtype=np.float32)
    ident_bf = np.eye(128, dtype=np.float32).astype(BF16N)
    a_i = np.arange(128)[:, None]
    b_i = np.arange(128)[None, :]
    tri_incl = np.where(b_i <= a_i, 0.0, NEG).astype(np.float32)
    tri_strict = np.where(b_i < a_i, 0.0, NEG).astype(np.float32)

    xs = [np.ascontiguousarray(x[c // 2, c % 2::2, :]) for c in range(8)]
    wp_grid = grid_a(np.ascontiguousarray(f(inputs["L0_attn_w_project"], dtype=np.float32).T), 48)
    wo0 = np.ascontiguousarray(f(inputs["L0_attn_w_out"], dtype=np.float32).T)
    rw = np.ascontiguousarray(
        f(inputs["L0_ffn_router_w"], dtype=np.float32).T.reshape(NDC, 128, 8).transpose(1, 0, 2))
    w1g = np.stack([grid_a(np.ascontiguousarray(f(inputs["L0_ffn_w1"], dtype=np.float32)[e]), 4)
                    for e in range(NE)])
    w3g = np.stack([grid_a(np.ascontiguousarray(f(inputs["L0_ffn_w3"], dtype=np.float32)[e]), 4)
                    for e in range(NE)])
    w2s = np.ascontiguousarray(f(inputs["L0_ffn_w2"], dtype=np.float32))
    wqg = grid_a(np.ascontiguousarray(f(inputs["L1_attn_Wq"], dtype=np.float32)), NH)
    swap = np.r_[32:64, 0:32]
    wqr_ = f(inputs["L1_attn_Wqr"], dtype=np.float32)
    wkr_ = f(inputs["L1_attn_Wkr"], dtype=np.float32)
    wqr = np.ascontiguousarray(wqr_.reshape(NDC, 128, 64))
    wqrs = np.ascontiguousarray(wqr_[:, swap].reshape(NDC, 128, 64))
    wkvg = grid_a(np.ascontiguousarray(f(inputs["L1_attn_Wkv"], dtype=np.float32)), 4)
    wkr = np.ascontiguousarray(wkr_.reshape(4, 128, 64))
    wkrs = np.ascontiguousarray(wkr_[:, swap].reshape(4, 128, 64))
    wkupg = grid_a(np.ascontiguousarray(f(inputs["L1_attn_Wkup"], dtype=np.float32)), NH)
    wvup = np.ascontiguousarray(f(inputs["L1_attn_Wvup"], dtype=np.float32))
    wo1 = np.ascontiguousarray(f(inputs["L1_attn_Wo"], dtype=np.float32))
    gwg = grid_a(np.ascontiguousarray(f(inputs["L1_ffn_gate_w"], dtype=np.float32).T), NIC)
    uwg = grid_a(np.ascontiguousarray(f(inputs["L1_ffn_up_w"], dtype=np.float32).T), NIC)
    dwt = np.ascontiguousarray(f(inputs["L1_ffn_down_w"], dtype=np.float32).T)
    gs = np.ascontiguousarray(f(inputs["L1_ffn_gate_s"], dtype=np.float32).reshape(NIC, 128).T)
    us = np.ascontiguousarray(f(inputs["L1_ffn_up_s"], dtype=np.float32).reshape(NIC, 128).T)
    dsb = np.ascontiguousarray(
        np.broadcast_to(f(inputs["L1_ffn_down_s"], dtype=np.float32)[None, :], (128, D)))

    maps = []
    for c in range(8):
        r = c % 2
        pos = 2 * np.arange(TOK) + r
        inv = 1.0 / 10000.0 ** (np.arange(0, 64, 2, dtype=np.float64) / 64)
        fr = np.outer(pos.astype(np.float64), inv)
        tab = np.zeros((128, TOK), np.float32)
        tab[0:64, :] = np.concatenate([np.cos(fr).T, np.cos(fr).T], 0)
        tab[64:128, :] = np.concatenate([-np.sin(fr).T, np.sin(fr).T], 0)
        maps.append({
            "xin": xs[c], "wp": wp_grid, "wo0": wo0, "rw": rw,
            "w1": w1g, "w3": w3g, "w2": w2s,
            "wq": wqg, "wqr": wqr, "wqrs": wqrs, "wkv": wkvg,
            "wkr": wkr, "wkrs": wkrs, "wkup": wkupg, "wvup": wvup, "wo1": wo1,
            "gw": gwg, "uw": uwg, "dw": dwt, "gs": gs, "us": us, "dsb": dsb,
            "ropetab": tab,
            "m_r0": tri_incl, "m_r1": (tri_strict if r == 0 else tri_incl),
            "idf": ident_f32, "idb": ident_bf,
        })
    return maps


def kernel(**inputs):
    maps = stage_inputs(inputs)
    nc = _prog("fused4", fused_build_mk(4), FUSED_SEMS)
    res = run_bass_kernel_spmd(nc, maps, core_ids=list(range(8)))
    out = np.empty((B, S, D), np.float32)
    for c in range(8):
        out[c // 2, c % 2::2, :] = res.results[c]["xout"]
    return out


# version 05
# speedup vs baseline: 1.6777x; 1.0367x over previous
import math
import contextlib
import numpy as np
import ml_dtypes
import concourse.bass as bass
import concourse.mybir as mybir
from concourse.bass_utils import run_bass_kernel_spmd

BF16N = ml_dtypes.bfloat16
F32 = mybir.dt.float32
F32R = mybir.dt.float32r
BF16 = mybir.dt.bfloat16
AF = mybir.ActivationFunctionType
ALU = mybir.AluOpType
AX = mybir.AxisListType

B, S, D = 4, 2048, 2048
NH, HD = 16, 128
TOK = 1024
NT = TOK // 128
NDC = D // 128
NEG = -1e9
QS0 = 1.0 / math.sqrt(HD)
QS1 = 1.0 / math.sqrt(HD + 64)
FF, NE = 512, 8
ITER = 5632
NIC = ITER // 128

ALL_ENGS = ("sync", "tensor", "vector", "scalar", "gpsimd")


class P:
    def __init__(self, nc, sems, which, engines):
        self.nc = nc
        self.sems = sems
        self.which = which
        self.engines = engines
        self.cnt = {n: 0 for n in sems}

    def _do(self, eng, fn, inc):
        if self.which == eng:
            inst = fn(self.engines[eng])
            if inc is not None:
                inst.then_inc(self.sems[inc[0]], inc[1])
        if inc is not None:
            self.cnt[inc[0]] += inc[1]

    def wait(self, eng, sem, val=None):
        if val is None:
            val = self.cnt[sem]
        val = max(0, val)
        if self.which == eng:
            self.engines[eng].wait_ge(self.sems[sem], val)

    def barrier(self):
        for e in ALL_ENGS:
            for sem in self.sems:
                if self.cnt[sem] > 0:
                    self.wait(e, sem)

    def dma(self, q, out, in_, inc):
        self._do(q, lambda e: e.dma_start(out=out, in_=in_), inc)

    def dmat(self, q, out, in_, inc):
        self._do(q, lambda e: e.dma_start_transpose(out, in_), inc)

    def mm(self, out, lhsT, rhs, start, stop, inc=None):
        self._do("tensor", lambda e: e.matmul(out, lhsT, rhs, start=start, stop=stop), inc)

    def trans(self, out, in_, ident, inc=None):
        self._do("tensor", lambda e: e.transpose(out, in_, ident), inc)

    def act(self, out, in_, func, bias=0.0, scale=1.0, accum_out=None, inc=None):
        self._do("scalar", lambda e: e.activation(out, in_, func, bias=bias, scale=scale,
                                                  accum_out=accum_out), inc)

    def vop(self, name, *args, inc=None, eng="vector", **kw):
        self._do(eng, lambda e: getattr(e, name)(*args, **kw), inc)

    def collective(self, kind, op, rg, ins, outs, inc):
        self._do("gpsimd", lambda e: e.collective_compute(
            kind, op, replica_groups=rg, ins=ins, outs=outs), inc)


def mkprog(build_fn, sem_names):
    nc = bass.Bass("TRN2", target_bir_lowering=False, debug=False,
                   dynamic_dma_scratch_size=512)
    state = {}
    build_fn(None, nc, state)
    with contextlib.ExitStack() as st:
        sems = {n: st.enter_context(nc.semaphore(n)) for n in sem_names}
        for name, h in state["tensors"]:
            state[name] = st.enter_context(h)
        block = st.enter_context(nc.Block())
        engines = {}

        def reg(name):
            def deco(engobj):
                engines[name] = engobj
                build_fn(P(nc, sems, name, engines), nc, state)
            return deco

        block.sync(reg("sync"))
        block.tensor(reg("tensor"))
        block.vector(reg("vector"))
        block.scalar(reg("scalar"))
        block.gpsimd(reg("gpsimd"))
    return nc


def declare(nc, s, specs):
    s["tensors"] = []
    for kind, name, shape, dt in specs:
        if kind == "dram_in":
            s[name] = nc.dram_tensor(name, shape, dt, kind="ExternalInput").ap()
        elif kind == "dram_out":
            s[name] = nc.dram_tensor(name, shape, dt, kind="ExternalOutput").ap()
        elif kind == "dram":
            s[name + "_t"] = nc.dram_tensor(name, shape, dt)
        elif kind == "sbuf":
            s["tensors"].append((name, nc.sbuf_tensor(name, shape, dt)))
        elif kind == "psum":
            s["tensors"].append((name, nc.psum_tensor(name, shape, dt)))


def view(arena, byte_off, shape, dt):
    assert byte_off % 4 == 0
    esz = {F32: 4, F32R: 4, BF16: 2}[dt]
    n = 1
    for d_ in shape[1:]:
        n *= d_
    f32len = (n * esz + 3) // 4
    ap = arena[:, byte_off // 4: byte_off // 4 + f32len]
    if dt != F32:
        ap = ap.bitcast(dt)
    if len(shape) == 2:
        return ap
    if len(shape) == 3:
        return ap.rearrange("p (a b) -> p a b", b=shape[2])
    if len(shape) == 4:
        return ap.rearrange("p (a b c) -> p a b c", b=shape[2], c=shape[3])
    raise ValueError(shape)


def emit_ln(p, s, h_fm, router=False):
    tr0, tr1 = s["ps0"], s["ps1"]
    for t in range(NT):
        xt = s["xres"][:, t, :]
        p.wait("scalar", "tr_ev")
        p.act(s["htile"][:, :], xt, AF.Square, inc=("a", 1))
        p.vop("reduce_sum", s["c_sum"][:, :], xt, axis=AX.X, inc=("v", 1))
        p.wait("vector", "a")
        p.vop("reduce_sum", s["c_sqs"][:, :], s["htile"][:, :], axis=AX.X, inc=("v", 1))
        p.wait("scalar", "v")
        p.act(s["c_mean"][:, :], s["c_sum"][:, :], AF.Copy, scale=1.0 / D, inc=("a", 1))
        p.act(s["c_msq"][:, :], s["c_sqs"][:, :], AF.Copy, scale=1.0 / D, inc=("a", 1))
        p.wait("vector", "a")
        p.vop("tensor_tensor", s["c_m2"][:, :], s["c_mean"][:, :], s["c_mean"][:, :],
              op=ALU.mult, inc=("v", 1))
        p.wait("vector", "v")
        p.vop("tensor_sub", s["c_var"][:, :], s["c_msq"][:, :], s["c_m2"][:, :], inc=("v", 1))
        p.wait("scalar", "v")
        p.wait("scalar", "init", s["n_init"])
        p.act(s["c_std"][:, :], s["c_var"][:, :], AF.Sqrt, bias=s["c_eps"][:, 0:1], inc=("a", 1))
        p.wait("vector", "a")
        p.vop("reciprocal", s["c_rstd"][:, :], s["c_std"][:, :], inc=("v", 1))
        p.wait("vector", "v")
        p.vop("tensor_tensor", s["c_mr"][:, :], s["c_mean"][:, :], s["c_rstd"][:, :],
              op=ALU.mult, inc=("v", 1))
        p.wait("vector", "v")
        p.vop("tensor_scalar", s["c_nmr"][:, :], s["c_mr"][:, :], s["c_neg1"][:, 0:1], None,
              op0=ALU.mult, inc=("v", 1))
        p.wait("scalar", "v")
        p.act(s["htile"][:, :], xt, AF.Identity, bias=s["c_nmr"][:, 0:1],
              scale=s["c_rstd"][:, 0:1], inc=("a", 1))
        p.wait("tensor", "a")
        for dc in range(NDC):
            ps = (tr0 if dc % 2 == 0 else tr1)[:, 0:128]
            p.wait("tensor", "tr_ev")
            p.trans(ps, s["htile"][:, dc * 128:(dc + 1) * 128], s["id_f32"][:, :],
                    inc=("t", 1))
            p.wait("vector", "t")
            p.vop("tensor_copy", h_fm[:, dc, t * 128:(t + 1) * 128], ps, inc=("tr_ev", 1))
            if router:
                p.wait("vector", "mmr")
                p.vop("tensor_copy", s["hseg"][:, :], ps, inc=("tr_ev", 1))
                p.wait("tensor", "tr_ev")
                p.mm(s["pv"][:, t * 8:(t + 1) * 8], s["hseg"][:, :], s["rwsb"][:, dc, :],
                     start=(dc == 0), stop=(dc == NDC - 1), inc=("mmr", 1))
    p.wait("tensor", "tr_ev")
    p.wait("scalar", "tr_ev")


def emit_proj_grid(p, s, wgrid, h_fm, oc_evacs):
    mm_per_slab = 2 * NDC
    b_pcast = p.cnt["pcast"]
    b_wld = p.cnt["wld"]
    b_mmw = p.cnt["mmw"]
    b_pevac = p.cnt["pevac"]
    for i, (oc, evac) in enumerate(oc_evacs):
        sl = i % 2
        p.wait("sync", "pcast", b_pcast + i - 1)
        p.dma("sync", s["wsf"][:, sl, :, :],
              wgrid[:, oc].rearrange("c p f -> p c f"), inc=("wld", 16))
        p.wait("vector", "wld", b_wld + (i + 1) * 16)
        p.wait("vector", "mmw", b_mmw + (i - 1) * mm_per_slab)
        p.vop("tensor_copy", s["wsb"][:, sl, :, :], s["wsf"][:, sl, :, :],
              inc=("pcast", 1))
        p.wait("tensor", "pcast", b_pcast + i + 1)
        for tt in range(2):
            j = i * 2 + tt
            ps = s["ps%d" % (j % 4)]
            p.wait("tensor", "pevac", b_pevac + j - 3)
            for dc in range(NDC):
                p.mm(ps[:, :], s["wsb"][:, sl, dc, :],
                     h_fm[:, dc, tt * 512:(tt + 1) * 512],
                     start=(dc == 0), stop=(dc == NDC - 1), inc=("mmw", 1))
            evac(p, s, ps, oc, tt, j)


def evac_stage_dma(p, s, ps, dst, row0, tt, j, scale):
    stg = s["stage"][:, j % 2, :]
    sem = "stdma" if j % 2 == 0 else "stdmb"
    p.wait("scalar", "mmw")
    p.wait("scalar", sem)
    p.act(stg, ps[:, :], AF.Copy, scale=scale, inc=("pevac", 1))
    p.dma("scalar", dst[row0:row0 + 128, tt * 512:(tt + 1) * 512], stg, inc=(sem, 16))


def emit_l0_qkv(p, s):
    def mk(oc):
        def evac(p, s, ps, oc_, tt, j):
            if oc_ < 16:
                evac_stage_dma(p, s, ps, s["q0d"], oc_ * 128, tt, j, QS0)
            else:
                evac_stage_dma(p, s, ps, s["cin0"], (oc_ - 16) * 128, tt, j, 1.0)
        return evac
    emit_proj_grid(p, s, s["wp"], s["h_fm"], [(oc, mk(oc)) for oc in range(48)])
    p.wait("gpsimd", "stdma")
    p.wait("gpsimd", "stdmb")


def emit_attn(p, s, l1):
    ps_s = [s["ps0"], s["ps1"], s["ps2"], s["ps3"]]
    if l1:
        for cc in range(4):
            for r in range(2):
                p.dma("sync", s["ckv_sb"][:, cc, r * 1024:(r + 1) * 1024],
                      s["ckvg"][r * 576 + cc * 128: r * 576 + (cc + 1) * 128, :],
                      inc=("ld2", 16))
        for r in range(2):
            p.dma("sync", s["kr_sb"][:, r * 1024:(r + 1) * 1024],
                  s["ckvg"][r * 576 + 512: r * 576 + 576, :], inc=("ld2", 16))
        p.dma("sync", s["qr_sb"][:, :], s["qr1d"], inc=("ld2", 16))
        p.wait("tensor", "ld2")

    b_pvev = p.cnt["pvev"]
    for h in range(NH):
        if not l1:
            qbuf = s["q_sb"] if h % 2 == 0 else s["q_sb2"]
            kbuf = s["k_sb"] if h % 2 == 0 else s["k_sb2"]
            vbuf = s["v_sb"] if h % 2 == 0 else s["v_sb2"]
            p.wait("sync", "pvev", b_pvev + max(0, h - 1) * NT)
            p.dma("sync", qbuf[:, :], s["qin"][h * 128:(h + 1) * 128, :],
                  inc=("kvld", 16))
            for r in range(2):
                p.dma("sync", kbuf[:, r * 1024:(r + 1) * 1024],
                      s["kvg"][r * 4096 + h * 128: r * 4096 + (h + 1) * 128, :],
                      inc=("kvld", 16))
                for cb in range(8):
                    p.dmat("sync", vbuf[:, r * 8 + cb, :],
                           s["kvg"][r * 4096 + 2048 + h * 128: r * 4096 + 2048 + (h + 1) * 128,
                                    cb * 128:(cb + 1) * 128], inc=("kvld", 16))
            p.wait("tensor", "kvld")
        else:
            qbuf, kbuf = s["q_sb"], s["k_sb"]
            p.wait("sync", "pvev")
            p.dma("sync", qbuf[:, :], s["qin"][h * 128:(h + 1) * 128, :],
                  inc=("kvld", 16))
            if h % 4 == 0:
                g = h // 4
                p.dma("sync", s["wvuf"][:, :, :],
                      s["wvup"].rearrange("(c p) o -> p c o", p=128)
                      [:, :, g * 512:(g + 1) * 512], inc=("vwld", 16))
                p.wait("vector", "vwld")
                p.wait("vector", "mmv")
                p.vop("tensor_copy", s["wvub"][:, :, :], s["wvuf"][:, :, :], inc=("vcast", 1))
                p.wait("tensor", "vcast")
                p.wait("tensor", "sfree")
                for tc in range(16):
                    psk = ps_s[tc % 4]
                    p.wait("tensor", "vevac")
                    for cc in range(4):
                        p.mm(psk[:, :], s["ckv_sb"][:, cc, tc * 128:(tc + 1) * 128],
                             s["wvub"][:, cc, :], start=(cc == 0), stop=(cc == 3),
                             inc=("mmv", 1) if cc == 3 else None)
                    p.wait("vector", "mmv")
                    p.vop("tensor_copy", s["v4"][:, tc, :], psk[:, :], inc=("vevac", 1))
            p.wait("tensor", "sfree")
            p.wait("tensor", "vevac")
            for cc in range(4):
                wslot = s["wkuf"][:, cc % 2, :]
                wbslot = s["wkub"][:, cc % 2, :]
                p.wait("sync", "kcast")
                p.dma("sync", wslot, s["wkup"][cc, h], inc=("kwld", 16))
                p.wait("vector", "kwld")
                p.wait("vector", "mmk")
                p.vop("tensor_copy", wbslot, wslot, inc=("kcast", 1))
                p.wait("tensor", "kcast")
                for kt in range(4):
                    if cc == 0:
                        p.wait("tensor", "kevac")
                    p.mm(ps_s[kt][:, :], wbslot,
                         s["ckv_sb"][:, cc, kt * 512:(kt + 1) * 512],
                         start=(cc == 0), stop=(cc == 3), inc=("mmk", 1))
            p.wait("vector", "mmk")
            for kt in range(4):
                p.vop("tensor_copy", s["k_sb"][:, kt * 512:(kt + 1) * 512],
                      ps_s[kt][:, :], inc=("kevac", 1))
            p.wait("tensor", "kevac")
            p.wait("tensor", "kvld")

        for qb in range(NT):
            nb = qb + 1
            L = 2 * nb * 128
            nkt = (L + 511) // 512
            qlh = qbuf[:, qb * 128:(qb + 1) * 128]
            p.wait("tensor", "sfree")
            segs = []
            for run, base in ((0, 0), (1, 1024)):
                c0 = run * nb * 128
                done = 0
                while done < nb * 128:
                    col = c0 + done
                    w = min(512 - (col % 512), nb * 128 - done)
                    segs.append((col, w, base + done))
                    done += w
            tile_written = [False] * 4
            for (col, w, kcol) in segs:
                kt = col // 512
                st0 = not tile_written[kt]
                tile_written[kt] = True
                if l1:
                    p.mm(ps_s[kt][:, col % 512:col % 512 + w], qlh,
                         kbuf[:, kcol:kcol + w], start=st0, stop=False)
                    p.mm(ps_s[kt][:, col % 512:col % 512 + w],
                         s["qr_sb"][:, qb * 128:(qb + 1) * 128],
                         s["kr_sb"][:, kcol:kcol + w], start=False, stop=True,
                         inc=("ms", 1))
                else:
                    p.mm(ps_s[kt][:, col % 512:col % 512 + w], qlh,
                         kbuf[:, kcol:kcol + w], start=st0, stop=True,
                         inc=("ms", 1))
            p.wait("vector", "ms")
            for run in range(2):
                col = (run * nb + nb - 1) * 128
                kt, off = col // 512, col % 512
                msk = s["mask0"] if run == 0 else s["mask1"]
                p.vop("tensor_tensor", ps_s[kt][:, off:off + 128],
                      ps_s[kt][:, off:off + 128], msk[:, :], op=ALU.add, inc=("v", 1))
            p.wait("vector", "v")
            for kt in range(nkt):
                w = min(512, L - kt * 512)
                p.vop("reduce_max", s["nmax"][:, kt:kt + 1], ps_s[kt][:, 0:w],
                      axis=AX.X, negate=True, inc=("v", 1))
            p.wait("vector", "v")
            if nkt > 1:
                p.vop("tensor_reduce", s["nmaxc"][:, :], s["nmax"][:, 0:nkt],
                      axis=AX.X, op=ALU.min, inc=("v", 1))
            else:
                p.vop("tensor_copy", s["nmaxc"][:, :], s["nmax"][:, 0:1], inc=("v", 1))
            p.wait("scalar", "v")
            p.wait("scalar", "t")
            for kt in range(nkt):
                w = min(512, L - kt * 512)
                p.act(s["attn"][:, kt * 512:kt * 512 + w], ps_s[kt][:, 0:w], AF.Exp,
                      bias=s["nmaxc"][:, 0:1], accum_out=s["rsums"][:, kt:kt + 1],
                      inc=("sfree", 1))
            p.wait("vector", "sfree")
            if nkt > 1:
                p.vop("reduce_sum", s["rsum"][:, :], s["rsums"][:, 0:nkt], axis=AX.X,
                      inc=("v", 1))
            else:
                p.vop("tensor_copy", s["rsum"][:, :], s["rsums"][:, 0:1], inc=("v", 1))
            p.wait("vector", "v")
            p.vop("reciprocal", s["rcp"][:, :], s["rsum"][:, :], inc=("v", 1))
            p.wait("vector", "v")
            p.vop("tensor_scalar", s["attn"][:, 0:L], s["attn"][:, 0:L],
                  s["rcp"][:, 0:1], None, op0=ALU.mult, inc=("v", 1))
            p.wait("tensor", "v")
            nkb = 2 * nb
            for kb in range(nkb):
                bank = s["pb0"] if (kb // 8) % 2 == 0 else s["pb1"]
                slot = kb % 8
                if slot == 0:
                    p.wait("tensor", "atev")
                p.trans(bank[:, slot * 128:(slot + 1) * 128],
                        s["attn"][:, kb * 128:(kb + 1) * 128], s["id_bf"][:, :],
                        inc=("t", 1))
                if slot == 7 or kb == nkb - 1:
                    p.wait("vector", "t")
                    n = slot + 1
                    p.vop("tensor_copy", s["attnT"][:, (kb - n + 1) * 128:(kb + 1) * 128],
                          bank[:, 0:n * 128], inc=("atev", 1))
            p.wait("tensor", "atev")
            p.wait("tensor", "pvev")
            for kb in range(nkb):
                run, jj = (0, kb) if kb < nb else (1, kb - nb)
                if l1:
                    vt = s["v4"][:, run * 8 + jj, (h % 4) * 128:(h % 4 + 1) * 128]
                else:
                    vt = vbuf[:, run * 8 + jj, :]
                p.mm(s["pv"][:, :], vt, s["attnT"][:, kb * 128:(kb + 1) * 128],
                     start=(kb == 0), stop=(kb == nkb - 1),
                     inc=("mpv", 1) if kb == nkb - 1 else None)
            p.wait("vector", "mpv")
            p.vop("tensor_copy", s["aout_fm"][:, h, qb * 128:(qb + 1) * 128],
                  s["pv"][:, :], inc=("pvev", 1))

    p.wait("tensor", "pvev")
    p.wait("tensor", "sfree")
    wo = s["wo1"] if l1 else s["wo0"]
    wo_mark = {}
    wc_free = {0: 0, 1: 0}
    n_wo = 0
    for ot in range(4):
        for qg in range(2):
            for dc in range(NDC):
                wslot = s["wochunk"][:, dc % 2, :]
                wbslot = s["wob"][:, dc % 2, :]
                p.wait("sync", "wcast")
                p.dma("sync", wslot,
                      wo[dc * 128:(dc + 1) * 128, ot * 512:(ot + 1) * 512],
                      inc=("owld", 16))
                p.wait("vector", "owld")
                p.wait("vector", "mmo")
                p.vop("tensor_copy", wbslot, wslot, inc=("wcast", 1))
                wc_free[n_wo % 2] = p.cnt["wcast"]
                p.wait("tensor", "wcast")
                if dc == 0:
                    p.wait("tensor", "oev")
                for j in range(4):
                    qb = qg * 4 + j
                    p.mm(ps_s[j][:, :],
                         s["aout_fm"][:, dc, qb * 128:(qb + 1) * 128], wbslot,
                         start=(dc == 0), stop=(dc == NDC - 1), inc=("mmo", 1))
                wo_mark[n_wo] = p.cnt["mmo"]
                n_wo += 1
            p.wait("vector", "mmo")
            for j in range(4):
                qb = qg * 4 + j
                p.vop("tensor_tensor", s["xres"][:, qb, ot * 512:(ot + 1) * 512],
                      s["xres"][:, qb, ot * 512:(ot + 1) * 512], ps_s[j][:, :],
                      op=ALU.add, inc=("oev", 1))
    p.wait("tensor", "oev")


def emit_moe(p, s):
    p.wait("vector", "mmr")
    p.vop("tensor_copy", s["sc"][:, :, :].rearrange("p a b -> p (a b)"), s["pv"][:, 0:64],
          inc=("v", 1))
    p.wait("vector", "v")
    for t in range(NT):
        st = s["sc"][:, t, :]
        p.wait("vector", "a")
        p.vop("reduce_max", s["c_m1"][:, :], st, axis=AX.X, inc=("v", 1))
        p.vop("reduce_max", s["c_nm1"][:, :], st, axis=AX.X, negate=True, inc=("v", 1))
        p.wait("vector", "v")
        p.vop("tensor_scalar", s["t_eq"][:, :], st, s["c_m1"][:, 0:1], None,
              op0=ALU.is_ge, inc=("v", 1))
        p.wait("vector", "v")
        p.vop("scalar_tensor_tensor", s["t_t"][:, :], s["t_eq"][:, :], s["c_neg9"][:, 0:1],
              st, op0=ALU.mult, op1=ALU.add, inc=("v", 1))
        p.wait("vector", "v")
        p.vop("reduce_max", s["c_m2r"][:, :], s["t_t"][:, :], axis=AX.X, inc=("v", 1))
        p.wait("scalar", "v")
        p.act(s["t_e1"][:, :], st, AF.Exp, bias=s["c_nm1"][:, 0:1], inc=("a", 1))
        p.act(s["c_em"][:, :], s["c_m2r"][:, :], AF.Exp, bias=s["c_nm1"][:, 0:1], inc=("a", 1))
        p.wait("vector", "a")
        p.vop("tensor_tensor", s["c_den"][:, :], s["c_em"][:, :], s["c_one"][:, 0:1],
              op=ALU.add, inc=("v", 1))
        p.wait("vector", "v")
        p.vop("reciprocal", s["c_rden"][:, :], s["c_den"][:, :], inc=("v", 1))
        p.wait("vector", "v")
        p.vop("tensor_scalar", s["t_ge"][:, :], st, s["c_m2r"][:, 0:1], None,
              op0=ALU.is_ge, inc=("v", 1))
        p.wait("vector", "v")
        p.vop("tensor_tensor", s["t_e1"][:, :], s["t_e1"][:, :], s["t_ge"][:, :],
              op=ALU.mult, inc=("v", 1))
        p.wait("vector", "v")
        p.vop("tensor_scalar", s["gates"][:, t, :], s["t_e1"][:, :], s["c_rden"][:, 0:1],
              None, op0=ALU.mult, inc=("v", 1))
    p.wait("tensor", "v")

    b_hev = p.cnt["hev"]
    b_oev = p.cnt["oev"]
    w13f_free = 0
    w2otf_free = 0
    w13b_mark = {}
    w2_mark = {}
    wld_n = p.cnt["wld"]
    n_oev = 0
    for e in range(NE):
        for ffq in range(4):
            g = e * 4 + ffq
            for w_i, nm in ((0, "w1"), (1, "w3")):
                li = g * 2 + w_i
                p.wait("sync", "pcast", w13f_free)
                p.dma("sync", s["w13f"][:, :, :],
                      s[nm][e].rearrange("c q p f -> p c q f")[:, :, ffq, :],
                      inc=("wld", 16))
                p.wait("vector", "wld")
                p.wait("vector", "mmw", w13b_mark.get(g - 2, 0))
                p.vop("tensor_copy", s["w13b"][:, li % 4, :, :], s["w13f"][:, :, :],
                      inc=("pcast", 1))
                w13f_free = p.cnt["pcast"]
            p.wait("tensor", "pcast")
            for tt in range(2):
                p.wait("tensor", "hev", b_hev + 2 * g + tt)
                for w_i in range(2):
                    ps = s["ps%d" % (2 + w_i)]
                    for dc in range(NDC):
                        p.mm(ps[:, :], s["w13b"][:, (g * 2 + w_i) % 4, dc, :],
                             s["h_fm"][:, dc, tt * 512:(tt + 1) * 512],
                             start=(dc == 0), stop=(dc == NDC - 1), inc=("mmw", 1))
                p.wait("scalar", "mmw")
                p.wait("scalar", "hev", b_hev + 2 * g + tt)
                p.act(s["sg"][:, :], s["ps2"][:, :], AF.Silu, inc=("a", 1))
                p.wait("vector", "a")
                p.vop("tensor_tensor", s["hid"][:, ffq, tt * 512:(tt + 1) * 512],
                      s["sg"][:, :], s["ps3"][:, :], op=ALU.mult, inc=("hev", 1))
            w13b_mark[g] = p.cnt["mmw"]
        p.wait("tensor", "hev")
        for ot in range(4):
            li = e * 4 + ot
            p.wait("sync", "pcast", w2otf_free)
            p.dma("sync", s["w2otf"][:, :, :],
                  s["w2"][e].rearrange("(c p) o -> p c o", p=128)[:, :, ot * 512:(ot + 1) * 512],
                  inc=("wld", 16))
            p.wait("vector", "wld")
            p.wait("vector", "mmw", w2_mark.get(li - 2, 0))
            p.vop("tensor_copy", s["w2otb"][:, li % 2, :, :], s["w2otf"][:, :, :],
                  inc=("pcast", 1))
            w2otf_free = p.cnt["pcast"]
            p.wait("tensor", "pcast")
            for tq in range(NT):
                ps = s["ps%d" % (tq % 2)]
                n_oev += 1
                p.wait("tensor", "oev", b_oev + n_oev - 2)
                for ffc in range(4):
                    p.mm(ps[:, :], s["hid"][:, ffc, tq * 128:(tq + 1) * 128],
                         s["w2otb"][:, li % 2, ffc, :], start=(ffc == 0), stop=(ffc == 3),
                         inc=("mmw", 1) if ffc == 3 else None)
                p.wait("vector", "mmw")
                p.vop("scalar_tensor_tensor", s["xres"][:, tq, ot * 512:(ot + 1) * 512],
                      ps[:, :], s["gates"][:, tq, e:e + 1],
                      s["xres"][:, tq, ot * 512:(ot + 1) * 512],
                      op0=ALU.mult, op1=ALU.add, inc=("oev", 1))
            w2_mark[li] = p.cnt["mmw"]
    p.wait("tensor", "oev")


def emit_l1_proj(p, s):
    def mkq(oc):
        def evac(p, s, ps, oc_, tt, j):
            evac_stage_dma(p, s, ps, s["q1d"], oc_ * 128, tt, j, QS1)
        return evac
    emit_proj_grid(p, s, s["wq"], s["h_fm"], [(oc, mkq(oc)) for oc in range(NH)])

    def mkc(oc):
        def evac(p, s, ps, oc_, tt, j):
            p.wait("vector", "mmw")
            p.vop("tensor_copy", s["ckv_sb1"][:, oc_, tt * 512:(tt + 1) * 512], ps[:, :],
                  inc=("pevac", 1))
            p.wait("scalar", "pevac")
            p.dma("scalar", s["cin1"][oc_ * 128:(oc_ + 1) * 128, tt * 512:(tt + 1) * 512],
                  s["ckv_sb1"][:, oc_, tt * 512:(tt + 1) * 512], inc=("stdma", 16))
        return evac
    emit_proj_grid(p, s, s["wkv"], s["h_fm"], [(oc, mkc(oc)) for oc in range(4)])

    for which in ("qr", "kr"):
        ps = s["ps2"]
        for swapped, dst in ((False, "rraw"), (True, "rrsw")):
            for tt in range(2):
                nchunk = NDC if which == "qr" else 4
                for cc in range(nchunk):
                    if which == "qr":
                        wsrc = "wqrs" if swapped else "wqr"
                        wslot = s["wchunk"][:, cc % 2, :]
                        wbslot = s["wchunkb"][:, cc % 2, :]
                        p.wait("sync", "pcast")
                        p.dma("sync", wslot[:, 0:64], s[wsrc][cc], inc=("wld", 16))
                        p.wait("vector", "wld")
                        p.wait("vector", "mmw")
                        p.vop("tensor_copy", wbslot[:, 0:64], wslot[:, 0:64],
                              inc=("pcast", 1))
                        p.wait("tensor", "pcast")
                        lhsT = wbslot[:, 0:64]
                        rhs = s["h_fm"][:, cc, tt * 512:(tt + 1) * 512]
                    else:
                        wsrc = "wkrs" if swapped else "wkr"
                        wslot = s["wkuf"][:, cc % 2, 0:64]
                        wbslot = s["wkub"][:, cc % 2, 0:64]
                        p.wait("sync", "kcast")
                        p.dma("sync", wslot, s[wsrc][cc], inc=("kwld", 16))
                        p.wait("vector", "kwld")
                        p.wait("vector", "mmw")
                        p.vop("tensor_copy", wbslot, wslot, inc=("kcast", 1))
                        p.wait("tensor", "kcast")
                        lhsT = wbslot
                        rhs = s["ckv_sb1"][:, cc, tt * 512:(tt + 1) * 512]
                    if cc == 0:
                        p.wait("tensor", "rev")
                    p.mm(ps[0:64, 0:512], lhsT, rhs,
                         start=(cc == 0), stop=(cc == nchunk - 1), inc=("mmw", 1))
                p.wait("vector", "mmw")
                p.vop("tensor_copy", s[dst][:, tt * 512:(tt + 1) * 512], ps[0:64, 0:512],
                      inc=("rev", 1))
        p.wait("vector", "rev")
        p.vop("tensor_tensor", s["rt1"][:, :], s["rraw"][:, :], s["ctab"][:, :],
              op=ALU.mult, inc=("v", 1))
        p.vop("tensor_tensor", s["roped"][:, :], s["rrsw"][:, :], s["sstab"][:, :],
              op=ALU.mult, inc=("v", 1))
        p.wait("vector", "v")
        p.vop("tensor_tensor", s["roped"][:, :], s["roped"][:, :], s["rt1"][:, :],
              op=ALU.add, inc=("v", 1))
        p.wait("vector", "v")
        p.wait("vector", "init", s["n_init"])
        if which == "qr":
            p.vop("tensor_scalar", s["ropedb"][:, :], s["roped"][:, :], s["c_qs1"][0:64, 0:1],
                  None, op0=ALU.mult, inc=("v", 1))
            p.wait("scalar", "v")
            p.dma("scalar", s["qr1d"], s["ropedb"][:, :], inc=("stdma", 16))
        else:
            p.vop("tensor_copy", s["ropedb"][:, :], s["roped"][:, :], inc=("v", 1))
            p.wait("scalar", "v")
            p.dma("scalar", s["cin1"][512:576, :], s["ropedb"][:, :], inc=("stdma", 16))
    p.wait("gpsimd", "stdma")
    p.wait("gpsimd", "stdmb")


def emit_ffn(p, s):
    p.dma("sync", s["gsb"][:, :], s["gs"], inc=("ld2", 16))
    p.dma("sync", s["usb"][:, :], s["us"], inc=("ld2", 16))
    p.dma("sync", s["dsb_sb"][:, :], s["dsb"], inc=("ld2", 16))
    for e in ("vector", "scalar", "tensor"):
        p.wait(e, "ld2")
    b_hev = p.cnt["hev"]
    b_oev = p.cnt["oev"]
    gu_mark = {}
    dw_mark = {}
    wf_free = {0: 0, 1: 0}
    dwf_free = {0: 0, 1: 0}
    n_ic = 0
    n_dw = 0
    for th in range(2):
        t0 = th * 512
        for ic in range(NIC):
            for w_i, nm in ((0, "gw"), (1, "uw")):
                p.wait("sync", "cast")
                p.dma("sync", s["wf"][:, w_i, :, :],
                      s[nm].rearrange("c i p f -> p c i f")[:, :, ic, :], inc=("wld2", 16))
                p.wait("vector", "wld2")
                p.wait("vector", "mmh")
                p.vop("tensor_copy", s["wgb" if w_i == 0 else "wub"][:, :, :],
                      s["wf"][:, w_i, :, :], inc=("cast", 1))
                wf_free[w_i] = p.cnt["cast"]
            p.wait("tensor", "cast")
            n_ic += 1
            p.wait("tensor", "hev", b_hev + n_ic - 1)
            for w_i in range(2):
                ps = s["ps%d" % (2 + w_i)]
                wb = s["wgb"] if w_i == 0 else s["wub"]
                for dc in range(NDC):
                    p.mm(ps[:, :], wb[:, dc, :], s["h_bf"][:, dc, t0:t0 + 512],
                         start=(dc == 0), stop=(dc == NDC - 1), inc=("mmh", 1))
            gu_mark[n_ic - 1] = p.cnt["mmh"]
            p.wait("scalar", "mmh")
            p.wait("scalar", "hev", b_hev + n_ic - 1)
            p.act(s["sg"][:, :], s["ps2"][:, :], AF.Silu, scale=s["gsb"][:, ic:ic + 1],
                  inc=("a", 1))
            p.wait("vector", "a")
            p.vop("scalar_tensor_tensor", s["hid1"][:, ic, :], s["ps3"][:, :],
                  s["usb"][:, ic:ic + 1], s["sg"][:, :], op0=ALU.mult, op1=ALU.mult,
                  inc=("hev", 1))
        p.wait("tensor", "hev")
        for ot in range(4):
            for ic in range(NIC):
                li = ot * NIC + ic
                p.wait("sync", "dcast")
                p.dma("sync", s["dwf"][:, li % 2, :],
                      s["dw"][ic * 128:(ic + 1) * 128, ot * 512:(ot + 1) * 512],
                      inc=("dwld", 16))
                p.wait("vector", "dwld")
                p.wait("vector", "mmd")
                p.vop("tensor_copy", s["dwb"][:, li % 2, :], s["dwf"][:, li % 2, :],
                      inc=("dcast", 1))
                dwf_free[n_dw % 2] = p.cnt["dcast"]
                p.wait("tensor", "dcast")
                if ic == 0:
                    p.wait("tensor", "oev")
                for tq in range(NT // 2):
                    p.mm(s["ps%d" % tq][:, :], s["hid1"][:, ic, tq * 128:(tq + 1) * 128],
                         s["dwb"][:, li % 2, :], start=(ic == 0), stop=(ic == NIC - 1),
                         inc=("mmd", 1))
                dw_mark[n_dw] = p.cnt["mmd"]
                n_dw += 1
            p.wait("vector", "mmd")
            for tq in range(NT // 2):
                qb = th * 4 + tq
                p.vop("tensor_tensor", s["ytmp"][:, :], s["ps%d" % tq][:, :],
                      s["dsb_sb"][:, ot * 512:(ot + 1) * 512], op=ALU.mult, inc=("v", 1))
                p.wait("vector", "v")
                p.vop("tensor_tensor", s["xres"][:, qb, ot * 512:(ot + 1) * 512],
                      s["xres"][:, qb, ot * 512:(ot + 1) * 512], s["ytmp"][:, :],
                      op=ALU.add, inc=("oev", 1))
    p.wait("tensor", "oev")


FUSED_SEMS = ["ld", "ld2", "wld", "wld2", "dwld", "kvld", "kwld", "vwld", "owld",
              "pcast", "mmr",
              "init", "v", "a", "t", "tr_ev", "ms", "mmw", "mmh", "mmd", "mmk",
              "mmv", "mmo", "mpv", "sfree", "atev", "pvev", "kevac", "vevac",
              "hev", "oev", "pevac", "rev", "stdma", "stdmb", "st", "cast", "dcast",
              "kcast", "vcast", "wcast", "cc0", "cc1"]

RG_PAIRS = [[0, 1], [2, 3], [4, 5], [6, 7]]


def fused_build_mk(upto):
    def build(p, nc, s):
        if p is None:
            declare(nc, s, [
                ("dram_in", "xin", [TOK, D], F32),
                ("dram_in", "wp", [NDC, 48, 128, 128], F32),
                ("dram_in", "wo0", [D, D], F32),
                ("dram_in", "rw", [128, NDC, 8], F32),
                ("dram_in", "w1", [NE, NDC, 4, 128, 128], F32),
                ("dram_in", "w3", [NE, NDC, 4, 128, 128], F32),
                ("dram_in", "w2", [NE, FF, D], F32),
                ("dram_in", "wq", [NDC, NH, 128, 128], F32),
                ("dram_in", "wqr", [NDC, 128, 64], F32),
                ("dram_in", "wqrs", [NDC, 128, 64], F32),
                ("dram_in", "wkv", [NDC, 4, 128, 128], F32),
                ("dram_in", "wkr", [4, 128, 64], F32),
                ("dram_in", "wkrs", [4, 128, 64], F32),
                ("dram_in", "wkup", [4, NH, 128, 128], F32),
                ("dram_in", "wvup", [512, D], F32),
                ("dram_in", "wo1", [D, D], F32),
                ("dram_in", "gw", [NDC, NIC, 128, 128], F32),
                ("dram_in", "uw", [NDC, NIC, 128, 128], F32),
                ("dram_in", "dw", [ITER, D], F32),
                ("dram_in", "gs", [128, NIC], F32),
                ("dram_in", "us", [128, NIC], F32),
                ("dram_in", "dsb", [128, D], F32),
                ("dram_in", "ropetab", [128, TOK], F32),
                ("dram_in", "m_r0", [128, 128], F32),
                ("dram_in", "m_r1", [128, 128], F32),
                ("dram_in", "idf", [128, 128], F32),
                ("dram_in", "idb", [128, 128], BF16),
                ("dram_out", "xout", [TOK, D], F32),
                ("dram", "q0d", [D, TOK], BF16),
                ("dram", "q1d", [D, TOK], BF16),
                ("dram", "qr1d", [64, TOK], BF16),
                ("dram", "cin0", [2 * D, TOK], BF16),
                ("dram", "cout0", [4 * D, TOK], BF16),
                ("dram", "cin1", [576, TOK], BF16),
                ("dram", "cout1", [1152, TOK], BF16),
                ("sbuf", "xres", [128, NT, D], F32),
                ("sbuf", "A", [128, 13312], F32),
                ("sbuf", "C", [128, 12288], F32),
                ("sbuf", "W", [128, 10240], F32),
                ("sbuf", "stage", [128, 2, 512], BF16),
                ("sbuf", "wchunk", [128, 2, 128], F32),
                ("sbuf", "wchunkb", [128, 2, 128], BF16),
                ("sbuf", "hseg", [128, 128], F32),
                ("sbuf", "id_f32", [128, 128], F32),
                ("sbuf", "id_bf", [128, 128], BF16),
                ("sbuf", "mask0", [128, 128], F32),
                ("sbuf", "mask1", [128, 128], F32),
                ("sbuf", "rwsb", [128, NDC, 8], F32),
                ("sbuf", "gsb", [128, NIC], F32),
                ("sbuf", "usb", [128, NIC], F32),
                ("sbuf", "gates", [128, NT, NE], F32),
                ("sbuf", "sc", [128, NT, NE], F32),
                ("sbuf", "nmax", [128, 4], F32),
                ("sbuf", "nmaxc", [128, 1], F32),
                ("sbuf", "rsums", [128, 4], F32),
                ("sbuf", "rsum", [128, 1], F32),
                ("sbuf", "rcp", [128, 1], F32),
                ("sbuf", "t_eq", [128, NE], F32),
                ("sbuf", "t_t", [128, NE], F32),
                ("sbuf", "t_e1", [128, NE], F32),
                ("sbuf", "t_ge", [128, NE], F32),
            ] + [("sbuf", n, [128, 1], F32) for n in
                 ["c_sum", "c_sqs", "c_mean", "c_msq", "c_m2", "c_var", "c_std",
                  "c_rstd", "c_mr", "c_nmr", "c_eps", "c_neg1", "c_neg9", "c_one",
                  "c_qs1", "c_m1", "c_nm1", "c_m2r", "c_em", "c_den", "c_rden"]]
              + [("psum", "ps%d" % i, [128, 512], F32) for i in range(4)]
              + [("psum", "pb0", [128, 1024], BF16), ("psum", "pb1", [128, 1024], BF16),
                 ("psum", "pv", [128, 128], F32)])
            return

        A, C, W = s["A"], s["C"], s["W"]
        s["h_fm"] = view(A, 0, [128, NDC, TOK], BF16)
        s["h_bf"] = s["h_fm"]
        s["q_sb"] = view(A, 32768, [128, TOK], BF16)
        s["k_sb"] = view(A, 34816, [128, 2048], BF16)
        s["attn"] = view(A, 38912, [128, 2048], BF16)
        s["attnT"] = view(A, 43008, [128, 2048], BF16)
        s["qr_sb"] = view(A, 47104, [128, TOK], BF16)[0:64, :]
        s["aout_fm"] = view(A, 0, [128, NDC, TOK], BF16)
        s["kr_sb"] = view(A, 49152, [128, 2048], BF16)[0:64, :]
        s["q_sb2"] = view(A, 47104, [128, TOK], BF16)
        s["k_sb2"] = view(A, 49152, [128, 2048], BF16)
        s["v_sb2"] = view(C, 20480, [128, 16, 128], BF16)
        s["htile"] = view(C, 0, [128, D], F32)
        s["rraw"] = view(C, 24576, [128, TOK], F32)[0:64, :]
        s["rrsw"] = view(C, 28672, [128, TOK], F32)[0:64, :]
        s["rt1"] = view(C, 32768, [128, TOK], F32)[0:64, :]
        s["roped"] = view(C, 36864, [128, TOK], F32)[0:64, :]
        s["ropedb"] = view(C, 40960, [128, TOK], BF16)[0:64, :]
        s["ctab"] = view(C, 8192, [128, TOK], F32)[0:64, :]
        s["sstab"] = view(C, 12288, [128, TOK], F32)[0:64, :]
        s["hid"] = view(C, 24576, [128, 4, TOK], BF16)
        s["ckv_sb"] = view(C, 0, [128, 4, 2048], BF16)
        s["ckv_sb1"] = view(C, 16384, [128, 4, TOK], BF16)
        s["v_sb"] = view(C, 16384, [128, 16, 128], BF16)
        s["v4"] = view(C, 16384, [128, 16, 512], BF16)
        s["hid1"] = view(C, 0, [128, NIC, 512], BF16)
        s["sg"] = view(C, 45056, [128, 512], F32)
        s["ytmp"] = view(C, 47104, [128, 512], F32)
        s["wsf"] = view(W, 0, [128, 2, NDC, 128], F32)
        s["wsb"] = view(W, 16384, [128, 2, NDC, 128], BF16)
        s["w13f"] = view(W, 0, [128, NDC, 128], F32)
        s["w13b"] = view(W, 8192, [128, 4, NDC, 128], BF16)
        s["w2otf"] = view(W, 24576, [128, 4, 512], F32)
        s["w2otb"] = view(W, 32768, [128, 2, 4, 512], BF16)
        s["wkuf"] = view(W, 0, [128, 2, 128], F32)
        s["wkub"] = view(W, 1024, [128, 2, 128], BF16)
        s["wvuf"] = view(W, 2048, [128, 4, 512], F32)
        s["wvub"] = view(W, 10240, [128, 4, 512], BF16)
        s["wochunk"] = view(W, 16384, [128, 2, 512], F32)
        s["wob"] = view(W, 20480, [128, 2, 512], BF16)
        s["wf"] = view(W, 0, [128, 2, NDC, 128], F32)
        s["wgb"] = view(W, 16384, [128, NDC, 128], BF16)
        s["wub"] = view(W, 20480, [128, NDC, 128], BF16)
        s["dwf"] = view(W, 24576, [128, 2, 512], F32)
        s["dwb"] = view(W, 28672, [128, 2, 512], BF16)
        s["dsb_sb"] = view(W, 32768, [128, D], F32)
        for nm in ("q0d", "q1d", "qr1d", "cin0", "cin1"):
            s[nm] = s[nm + "_t"].ap()

        for nm, val in (("c_eps", 1e-5), ("c_neg1", -1.0), ("c_neg9", NEG),
                        ("c_one", 1.0), ("c_qs1", QS1)):
            p.vop("memset", s[nm][:, :], val, inc=("init", 1), eng="gpsimd")
        s["n_init"] = 5
        p.dma("sync", s["id_f32"][:, :], s["idf"], inc=("ld", 16))
        p.dma("sync", s["id_bf"][:, :], s["idb"], inc=("ld", 16))
        p.dma("sync", s["mask0"][:, :], s["m_r0"], inc=("ld", 16))
        p.dma("sync", s["mask1"][:, :], s["m_r1"], inc=("ld", 16))
        p.dma("sync", s["xres"][:, :, :], s["xin"].rearrange("(t p) d -> p t d", p=128),
              inc=("ld", 16))
        for e in ALL_ENGS:
            p.wait(e, "ld")
            p.wait(e, "init", s["n_init"])

        emit_ln(p, s, s["h_fm"])
        emit_l0_qkv(p, s)
        p.collective("AllGather", ALU.bypass, RG_PAIRS,
                     [s["cin0_t"].ap().opt()], [s["cout0_t"].ap().opt()], inc=("cc0", 1))
        p.wait("gpsimd", "cc0")
        p.barrier()
        s["kvg"] = s["cout0_t"].ap()
        s["qin"] = s["q0d"]
        emit_attn(p, s, l1=False)
        p.barrier()
        if upto >= 2:
            p.dma("sync", s["rwsb"][:, :, :], s["rw"], inc=("ld2", 16))
            p.wait("tensor", "ld2")
            p.wait("vector", "ld2")
            emit_ln(p, s, s["h_fm"], router=True)
            emit_moe(p, s)
            p.barrier()
        if upto >= 3:
            emit_ln(p, s, s["h_fm"])
            p.dma("sync", s["ctab"][:, :], s["ropetab"][0:64, :], inc=("ld2", 16))
            p.dma("sync", s["sstab"][:, :], s["ropetab"][64:128, :], inc=("ld2", 16))
            p.wait("vector", "ld2")
            emit_l1_proj(p, s)
            p.collective("AllGather", ALU.bypass, RG_PAIRS,
                         [s["cin1_t"].ap().opt()], [s["cout1_t"].ap().opt()], inc=("cc1", 1))
            p.wait("gpsimd", "cc1")
            p.barrier()
            s["ckvg"] = s["cout1_t"].ap()
            s["qin"] = s["q1d"]
            emit_attn(p, s, l1=True)
            p.barrier()
        if upto >= 4:
            emit_ln(p, s, s["h_bf"])
            emit_ffn(p, s)
            p.barrier()
        p.wait("scalar", "oev")
        p.dma("scalar", s["xout"].rearrange("(t p) d -> p t d", p=128), s["xres"][:, :, :],
              inc=("st", 16))
        p.wait("scalar", "st")
    return build


def grid_a(w_t, ocn):
    din, dout = w_t.shape
    dcn = din // 128
    assert dout == ocn * 128
    return np.ascontiguousarray(w_t.reshape(dcn, 128, ocn, 128).transpose(0, 2, 1, 3))


_PROGS = {}


def _prog(key, builder, sems):
    if key not in _PROGS:
        _PROGS[key] = mkprog(builder, sems)
    return _PROGS[key]


def stage_inputs(inputs):
    f = np.asarray
    x = f(inputs["L0_attn_x"], dtype=np.float32)
    ident_f32 = np.eye(128, dtype=np.float32)
    ident_bf = np.eye(128, dtype=np.float32).astype(BF16N)
    a_i = np.arange(128)[:, None]
    b_i = np.arange(128)[None, :]
    tri_incl = np.where(b_i <= a_i, 0.0, NEG).astype(np.float32)
    tri_strict = np.where(b_i < a_i, 0.0, NEG).astype(np.float32)

    xs = [np.ascontiguousarray(x[c // 2, c % 2::2, :]) for c in range(8)]
    wp_grid = grid_a(np.ascontiguousarray(f(inputs["L0_attn_w_project"], dtype=np.float32).T), 48)
    wo0 = np.ascontiguousarray(f(inputs["L0_attn_w_out"], dtype=np.float32).T)
    rw = np.ascontiguousarray(
        f(inputs["L0_ffn_router_w"], dtype=np.float32).T.reshape(NDC, 128, 8).transpose(1, 0, 2))
    w1g = np.stack([grid_a(np.ascontiguousarray(f(inputs["L0_ffn_w1"], dtype=np.float32)[e]), 4)
                    for e in range(NE)])
    w3g = np.stack([grid_a(np.ascontiguousarray(f(inputs["L0_ffn_w3"], dtype=np.float32)[e]), 4)
                    for e in range(NE)])
    w2s = np.ascontiguousarray(f(inputs["L0_ffn_w2"], dtype=np.float32))
    wqg = grid_a(np.ascontiguousarray(f(inputs["L1_attn_Wq"], dtype=np.float32)), NH)
    swap = np.r_[32:64, 0:32]
    wqr_ = f(inputs["L1_attn_Wqr"], dtype=np.float32)
    wkr_ = f(inputs["L1_attn_Wkr"], dtype=np.float32)
    wqr = np.ascontiguousarray(wqr_.reshape(NDC, 128, 64))
    wqrs = np.ascontiguousarray(wqr_[:, swap].reshape(NDC, 128, 64))
    wkvg = grid_a(np.ascontiguousarray(f(inputs["L1_attn_Wkv"], dtype=np.float32)), 4)
    wkr = np.ascontiguousarray(wkr_.reshape(4, 128, 64))
    wkrs = np.ascontiguousarray(wkr_[:, swap].reshape(4, 128, 64))
    wkupg = grid_a(np.ascontiguousarray(f(inputs["L1_attn_Wkup"], dtype=np.float32)), NH)
    wvup = np.ascontiguousarray(f(inputs["L1_attn_Wvup"], dtype=np.float32))
    wo1 = np.ascontiguousarray(f(inputs["L1_attn_Wo"], dtype=np.float32))
    gwg = grid_a(np.ascontiguousarray(f(inputs["L1_ffn_gate_w"], dtype=np.float32).T), NIC)
    uwg = grid_a(np.ascontiguousarray(f(inputs["L1_ffn_up_w"], dtype=np.float32).T), NIC)
    dwt = np.ascontiguousarray(f(inputs["L1_ffn_down_w"], dtype=np.float32).T)
    gs = np.ascontiguousarray(f(inputs["L1_ffn_gate_s"], dtype=np.float32).reshape(NIC, 128).T)
    us = np.ascontiguousarray(f(inputs["L1_ffn_up_s"], dtype=np.float32).reshape(NIC, 128).T)
    dsb = np.ascontiguousarray(
        np.broadcast_to(f(inputs["L1_ffn_down_s"], dtype=np.float32)[None, :], (128, D)))

    maps = []
    for c in range(8):
        r = c % 2
        pos = 2 * np.arange(TOK) + r
        inv = 1.0 / 10000.0 ** (np.arange(0, 64, 2, dtype=np.float64) / 64)
        fr = np.outer(pos.astype(np.float64), inv)
        tab = np.zeros((128, TOK), np.float32)
        tab[0:64, :] = np.concatenate([np.cos(fr).T, np.cos(fr).T], 0)
        tab[64:128, :] = np.concatenate([-np.sin(fr).T, np.sin(fr).T], 0)
        maps.append({
            "xin": xs[c], "wp": wp_grid, "wo0": wo0, "rw": rw,
            "w1": w1g, "w3": w3g, "w2": w2s,
            "wq": wqg, "wqr": wqr, "wqrs": wqrs, "wkv": wkvg,
            "wkr": wkr, "wkrs": wkrs, "wkup": wkupg, "wvup": wvup, "wo1": wo1,
            "gw": gwg, "uw": uwg, "dw": dwt, "gs": gs, "us": us, "dsb": dsb,
            "ropetab": tab,
            "m_r0": tri_incl, "m_r1": (tri_strict if r == 0 else tri_incl),
            "idf": ident_f32, "idb": ident_bf,
        })
    return maps


def kernel(**inputs):
    maps = stage_inputs(inputs)
    nc = _prog("fused4", fused_build_mk(4), FUSED_SEMS)
    res = run_bass_kernel_spmd(nc, maps, core_ids=list(range(8)))
    out = np.empty((B, S, D), np.float32)
    for c in range(8):
        out[c // 2, c % 2::2, :] = res.results[c]["xout"]
    return out
